# version 16
# baseline (speedup 1.0000x reference)
import numpy as np
import ml_dtypes
from contextlib import ExitStack

import concourse.bass as bass
import concourse.mybir as mybir
from concourse.bass_utils import run_bass_kernel_spmd
import concourse.dve_ops as dve_ops
from concourse.dve_op import DveOp
from concourse.dve_spec import Spec, Src0, Src1, C0, C1, Zero, maxx, lower

F32 = mybir.dt.float32
BF16 = mybir.dt.bfloat16
FP8 = mybir.dt.float8e5
AF = mybir.ActivationFunctionType
ALU = mybir.AluOpType
AX = mybir.AxisListType

B, T, C, NH, HS = 4, 4096, 1024, 16, 64
NPAIR = 8
SPLIT = 2432
SWAPM = [(i + 1) if i % 2 == 0 else (i - 1) for i in range(32)]
NEG = -8.0e10
LN_EPS = 1e-5
LNQ = float(np.log(0.125))


def _threefry_keep_mask():
    n = B * T * C
    rot = ((13, 15, 26, 6), (17, 29, 16, 24))
    k0, k1 = np.uint32(0), np.uint32(42)
    ks = (k0, k1, k0 ^ k1 ^ np.uint32(0x1BD11BDA))
    x0 = np.zeros(n, np.uint32)
    x1 = np.arange(n, dtype=np.uint32)
    x0 += ks[0]; x1 += ks[1]
    for i in range(5):
        r = rot[i % 2]
        for j in range(4):
            x0 += x1
            x1 = (x1 << np.uint32(r[j])) | (x1 >> np.uint32(32 - r[j]))
            x1 ^= x0
        x0 += ks[(i + 1) % 3]; x1 += ks[(i + 2) % 3]
        x1 += np.uint32(i + 1)
    bits = x0 ^ x1
    u = ((bits >> np.uint32(9)) | np.uint32(0x3F800000)).view(np.float32) - np.float32(1.0)
    return (u < np.float32(0.8)).reshape(B, T, C)


def _rope_tables():
    half = HS // 2
    inv_freq = (np.float32(1.0) / np.float32(10000.0) **
                (np.arange(half, dtype=np.float32) * np.float32(2.0) / np.float32(HS))).astype(np.float32)
    ang = (np.arange(T, dtype=np.float32)[:, None] * inv_freq[None, :]).astype(np.float32)
    cos = np.cos(ang).astype(np.float32)
    sin = np.sin(ang).astype(np.float32)
    Cfm = np.empty((HS, T), np.float32)
    Sfm = np.empty((HS, T), np.float32)
    Cfm[0::2, :] = cos.T
    Cfm[1::2, :] = cos.T
    Sfm[0::2, :] = sin.T
    Sfm[1::2, :] = -sin.T
    return np.vstack([Cfm, Cfm]), np.vstack([Sfm, Sfm])


def _boundary_masks():
    out = np.zeros((4, 128, 512), np.float32)
    tri = np.where(np.arange(128)[:, None] <= np.arange(128)[None, :], 0.0, NEG).astype(np.float32)
    for j in range(4):
        for jq in range(4):
            blk = out[j][:, 128 * jq:128 * (jq + 1)]
            if jq < j:
                blk[:] = NEG
            elif jq == j:
                blk[:] = tri
    return out



def _fq_exact(x):
    x = np.asarray(x)
    dt = x.dtype
    ax = np.abs(x)
    with np.errstate(divide="ignore"):
        e = np.floor(np.log2(np.maximum(ax, np.array(1e-38, dt))))
    e = np.maximum(e, np.array(-14.0, dt))
    scale = np.exp2(e - 2)
    q = np.round(x / scale) * scale
    return np.where(ax == 0, np.zeros_like(x), q).astype(dt)


def _bf16_round_exact(x):
    x = np.asarray(x)
    dt = x.dtype
    ax = np.abs(x)
    with np.errstate(divide="ignore"):
        e = np.floor(np.log2(np.maximum(ax, np.array(1e-38, dt))))
    e = np.maximum(e, np.array(-126.0, dt))
    scale = np.exp2(e - 7)
    q = np.round(x / scale) * scale
    return np.where(ax == 0, x, q).astype(dt)


_OPS = {}


def _register_ops():
    if _OPS:
        return _OPS

    def mk(name, spec, ref):
        sha = {"v3": lower(spec, ver="v3", perf_en=False, name=name).sha("v3")}
        return DveOp(name, spec, subdim=False, uops_sha=sha, reference=ref)

    ops = [
        mk("ANT_FQ8", Spec(body=Src0 + Zero),
           lambda in0, in1, s0, s1, imm2: _fq_exact(in0)),
        mk("ANT_FQ8MOD", Spec(body=Src0 * C0 + C1),
           lambda in0, in1, s0, s1, imm2: _fq_exact(in0 * s0 + s1)),
        mk("ANT_RM", Spec(body=Src0 + C0),
           lambda in0, in1, s0, s1, imm2: _bf16_round_exact(in0 + s0)),
        mk("ANT_FQ8MOD2", Spec(body=(Src0 - Src1) * C0 + C1),
           lambda in0, in1, s0, s1, imm2: _fq_exact((in0 - in1) * s0 + s1)),
    ]
    for i, op in enumerate(ops):
        old = dve_ops.OPS[i]
        if old.name != op.name:
            dve_ops.OPS[i] = op
            dve_ops._SUB_OPCODE_FOR_NAME[op.name] = dve_ops._SUB_OPCODE_FOR_NAME.pop(old.name)
            dve_ops.CUSTOM_DVE_REFS.pop(old.name, None)
            dve_ops.CUSTOM_DVE_SPECS.pop(old.name, None)
            dve_ops.CUSTOM_DVE_REFS[op.name] = op.ref
            dve_ops.CUSTOM_DVE_SPECS[op.name] = op.spec
        _OPS[op.name] = op
    return _OPS


def _blocks(lo, hi, step=512):
    out, t = [], lo
    while t < hi:
        w = min(step, hi - t)
        out.append((t, w))
        t += w
    return out


class Ctr:
    def __init__(self, sem):
        self.sem = sem
        self.v = 0

    def inc(self, inst, n=1):
        inst.then_inc(self.sem, n)
        self.v += n
        return self.v



def build_program(dbg=False):
    OPS = _register_ops()
    nc = bass.Bass("TRN2", target_bir_lowering=False)
    top = ExitStack()
    sb = lambda n, shape, d: top.enter_context(nc.sbuf_tensor(n, shape, d))
    V, S, TN, G, SY = nc.vector, nc.scalar, nc.tensor, nc.gpsimd, nc.sync
    ALL_ENG = [SY, V, S, TN, G]
    TOmax = SPLIT

    ap = lambda h: h.ap()
    d_xT = ap(nc.dram_tensor("xT", [C, T], F32, kind="ExternalInput"))
    d_cos = ap(nc.dram_tensor("cosT", [128, T], F32, kind="ExternalInput"))
    d_sin = ap(nc.dram_tensor("sinT", [128, T], F32, kind="ExternalInput"))
    d_qm = ap(nc.dram_tensor("qmask4", [4 * 128, 512], BF16, kind="ExternalInput"))
    d_perm = ap(nc.dram_tensor("permswap", [128, 128], BF16, kind="ExternalInput"))
    d_eye = ap(nc.dram_tensor("eye128", [128, 128], BF16, kind="ExternalInput"))
    d_tri = ap(nc.dram_tensor("tri128", [128, 128], BF16, kind="ExternalInput"))
    d_maskz = ap(nc.dram_tensor("maskz", [C, TOmax], BF16, kind="ExternalInput"))
    d_Wq = ap(nc.dram_tensor("Wq", [C, C], F32, kind="ExternalInput"))
    d_Wk = ap(nc.dram_tensor("Wk", [C, C], F32, kind="ExternalInput"))
    d_Wv = ap(nc.dram_tensor("Wv", [C, C], F32, kind="ExternalInput"))
    d_Wp = ap(nc.dram_tensor("Wp", [C, C], F32, kind="ExternalInput"))
    d_W1T = ap(nc.dram_tensor("W1T", [C, 4 * C], F32, kind="ExternalInput"))
    d_W2T = ap(nc.dram_tensor("W2T", [4 * C, C], F32, kind="ExternalInput"))
    d_wmod = ap(nc.dram_tensor("wmod", [C, 6 * C], F32, kind="ExternalInput"))
    d_ccol = ap(nc.dram_tensor("ccol", [128, 8], F32, kind="ExternalInput"))
    d_bqr = ap(nc.dram_tensor("bq_row", [1, C], F32, kind="ExternalInput"))
    d_bkr = ap(nc.dram_tensor("bk_row", [1, C], F32, kind="ExternalInput"))
    d_bvr = ap(nc.dram_tensor("bv_row", [1, C], F32, kind="ExternalInput"))
    d_bpr = ap(nc.dram_tensor("bp_row", [1, C], F32, kind="ExternalInput"))
    d_b1c = ap(nc.dram_tensor("b1col", [128, 32], F32, kind="ExternalInput"))
    d_b2c = ap(nc.dram_tensor("b2col", [128, 8], F32, kind="ExternalInput"))
    d_out = ap(nc.dram_tensor("outT", [C, TOmax], F32, kind="ExternalOutput"))
    sk = "ExternalOutput" if dbg else "Internal"
    d_kd = ap(nc.dram_tensor("kd_s", [128, NPAIR * T], BF16, kind=sk))
    d_vd = ap(nc.dram_tensor("vd_s", [128, NPAIR * T], BF16, kind=sk))
    d_qd = ap(nc.dram_tensor("qd_s", [128, NPAIR * TOmax], BF16, kind=sk))
    d_w1q8 = ap(nc.dram_tensor("w1q8_s", [C, 4 * C], FP8, kind="Internal"))
    d_w2q8 = ap(nc.dram_tensor("w2q8_s", [4 * C, C], FP8, kind="Internal"))
    if dbg:
        d_dmod = ap(nc.dram_tensor("dmod", [128, 40], F32, kind="ExternalOutput"))
        d_dsa = ap(nc.dram_tensor("dsa", [128, 8], BF16, kind="ExternalOutput"))
        d_dxA = ap(nc.dram_tensor("dxA", [128, 8 * TOmax], BF16, kind="ExternalOutput"))
        d_dxB = ap(nc.dram_tensor("dxB", [128, 8 * TOmax], BF16, kind="ExternalOutput"))
        d_dob = ap(nc.dram_tensor("dob", [128, 8 * TOmax], BF16, kind="ExternalOutput"))
        d_dx2 = ap(nc.dram_tensor("dx2", [128, 8 * TOmax], BF16, kind="ExternalOutput"))

    sem = lambda n: top.enter_context(nc.semaphore(n))
    LD = Ctr(sem("LD"))
    GP = Ctr(sem("GP"))
    VD = Ctr(sem("VD"))
    SD = Ctr(sem("SD"))
    VE = Ctr(sem("VE"))
    SC = Ctr(sem("SC"))
    TE = Ctr(sem("TE"))
    ST = Ctr(sem("ST"))
    SEMS = [LD, GP, VD, SD, VE, SC, TE, ST]

    qmask = sb("qmaskS", [128, 4 * 512], BF16)
    rowf = sb("rowf", [1, 4 * 512], F32)
    rowb = sb("rowb", [1, 4 * 512], BF16)
    stgA = sb("stgA", [128, 2048], F32)
    stgB = sb("stgB", [128, 2048], F32)
    stg1 = sb("stg1", [128, 512], F32)
    stg2 = sb("stg2", [128, 512], F32)
    stg3 = sb("stg3", [128, 512], F32)
    bstg1 = sb("bstg1", [128, 512], BF16)
    bstg2 = sb("bstg2", [128, 512], BF16)
    xsq1 = sb("xsq1", [128, 512], BF16)
    xsq2 = sb("xsq2", [128, 512], BF16)
    bprow = sb("bprow", [1, C], BF16)
    caP1 = sb("caP1", [128, 8], F32)
    cbP1 = sb("cbP1", [128, 8], F32)
    sb_c = sb("sb_c", [128, 8], F32)
    ga_c = sb("ga_c", [128, 8], F32)
    gb_c = sb("gb_c", [128, 8], F32)
    sa_cb = sb("sa_cb", [128, 8], BF16)
    silu_cb = sb("silu_cb", [128, 8], BF16)
    fqb1c = sb("fqb1c", [128, 32], F32)
    fqb2c = sb("fqb2c", [128, 8], F32)
    f8tmp = sb("f8tmp", [128, 32], FP8)
    ones1c = sb("ones1c", [128, 1], BF16)
    oner_lhs = sb("oner_lhs", [1, 128], BF16)
    onesr = sb("onesr", [1, 512], BF16)
    ones128 = sb("ones128", [128, 128], BF16)
    silu_lo = sb("silu_lo", [128, 8], BF16)
    permsw = sb("permsw", [128, 128], BF16)
    eyeb = sb("eyeb", [128, 128], BF16)
    trib = sb("trib", [128, 128], BF16)
    negrow = sb("negrow", [1, 128], BF16)
    r_col = sb("r_col", [128, 32], F32)
    negBall = sb("negBall", [128, 8], F32)
    kmax2 = sb("kmax2", [1, 32], F32)
    big0 = top.enter_context(nc.psum_tensor("big0", [128, 1024], F32))
    big1 = top.enter_context(nc.psum_tensor("big1", [128, 1024], F32))
    p4 = top.enter_context(nc.psum_tensor("p4", [128, 512], F32))
    p5 = top.enter_context(nc.psum_tensor("p5", [128, 512], F32))
    p6 = top.enter_context(nc.psum_tensor("p6", [128, 512], F32))
    p7 = top.enter_context(nc.psum_tensor("p7", [128, 512], F32))
    pb = [big0[:, 0:512], big0[:, 512:1024], big1[:, 0:512], big1[:, 512:1024],
          p4[:], p5[:], p6[:], p7[:]]

    def syfence():
        SY.wait_ge(LD.sem, LD.v)
        return LD.v

    def gfence():
        G.wait_ge(GP.sem, GP.v)
        return GP.v

    def phase_barrier():
        for eng in ALL_ENG:
            for ctr in (VE, SC, TE, GP, SD, LD, VD):
                if ctr.v > 0:
                    eng.wait_ge(ctr.sem, ctr.v)

    def vcustom(op, **kw):
        v = VE.inc(V._custom_dve(op, **kw))
        V.wait_ge(VE.sem, v)
        return v

    def dbg_dump(pairs):
        if not dbg:
            return
        for dst, src in pairs:
            LD.inc(SY.dma_start(out=dst, in_=src), 16)
        dv = syfence()
        for eng in ALL_ENG:
            eng.wait_ge(LD.sem, dv)

    mrow = rowf[0:1, 0:512]
    srow = rowf[0:1, 512:1024]
    lrow = rowf[0:1, 1024:1536]
    rrow = rowf[0:1, 1536:2048]
    rbf = rowb[0:1, 0:512]
    mr2b = rowb[0:1, 512:1024]
    scr_b = rowb[0:1, 1024:1536]
    scr_c = rowb[0:1, 1536:2048]

    V.memset(ones1c[:], 1.0)
    V.memset(oner_lhs[:], 1.0)
    V.memset(onesr[:], 1.0)
    V.memset(ones128[:], 1.0)
    cpool = sb("cpool", [128, 4], F32)
    V.memset(cpool[:, 0:1], LN_EPS)
    V.memset(cpool[:, 1:2], 1.0)
    ve_consts = VE.inc(V.memset(cpool[:, 2:3], LNQ))
    nc.const_aps.register(F32, LN_EPS, cpool[:, 0:1])
    nc.const_aps.register(F32, 1.0, cpool[:, 1:2])
    nc.const_aps.register(F32, LNQ, cpool[:, 2:3])
    S.wait_ge(VE.sem, ve_consts)
    TN.wait_ge(VE.sem, ve_consts)

    for j in range(4):
        LD.inc(SY.dma_start(out=qmask[:, j * 512:(j + 1) * 512], in_=d_qm[j * 128:(j + 1) * 128, :]), 16)
    LD.inc(SY.dma_start(out=permsw[:], in_=d_perm), 16)
    LD.inc(SY.dma_start(out=eyeb[:], in_=d_eye), 16)
    LD.inc(SY.dma_start(out=trib[:], in_=d_tri), 16)
    V.memset(negrow[:], NEG)
    LD.inc(SY.dma_start(out=stg1[:, 0:8], in_=d_ccol), 16)
    LD.inc(SY.dma_start(out=stg2[:, 0:32], in_=d_b1c), 16)
    LD.inc(SY.dma_start(out=stg2[:, 32:40], in_=d_b2c), 16)
    LD.inc(SY.dma_start(out=stgA[0:1, 1024:2048], in_=d_bpr), 16)
    ld0 = syfence()

    V.wait_ge(LD.sem, ld0)
    vcustom(OPS["ANT_FQ8"], out=f8tmp[:, 0:32], in0=stg2[:, 0:32])
    vcp = VE.inc(V.tensor_copy(out=fqb1c[:], in_=f8tmp[:, 0:32]))
    V.wait_ge(VE.sem, vcp)
    vcustom(OPS["ANT_FQ8"], out=f8tmp[:, 0:8], in0=stg2[:, 32:40])
    V.tensor_copy(out=fqb2c[:], in_=f8tmp[:, 0:8])
    ve_bp = VE.inc(V.tensor_copy(out=bprow[0:1, 0:C], in_=stgA[0:1, 1024:2048]))

    S.wait_ge(LD.sem, ld0)
    sc_si = SC.inc(S.activation(out=stg1[:, 8:16], in_=stg1[:, 0:8], func=AF.Silu))
    V.wait_ge(SC.sem, sc_si)
    ve_si0 = VE.inc(V.tensor_copy(out=silu_cb[:], in_=stg1[:, 8:16]))
    V.wait_ge(VE.sem, ve_si0)
    ve_si = VE.inc(V.tensor_sub(out=silu_lo[:], in0=stg1[:, 8:16], in1=silu_cb[:]))

    mod_te = {}
    mod_sc = {}
    for mb in range(12):
        bank = pb[mb % 2]
        for cc in range(8):
            k = mb * 8 + cc
            stgw = stgA if k % 2 == 0 else stgB
            bw = bstg1 if k % 2 == 0 else bstg2
            if k % 2 == 0:
                if k >= 2:
                    SY.wait_ge(TE.sem, mod_te[k - 2])
                LD.inc(SY.dma_start(out=stgw[:, 0:512],
                                    in_=d_wmod[cc * 128:(cc + 1) * 128, mb * 512:(mb + 1) * 512]), 16)
                dv = syfence()
                V.wait_ge(LD.sem, dv)
            else:
                if k >= 2:
                    S.wait_ge(TE.sem, mod_te[k - 2])
                SD.inc(S.dma_start(out=stgw[:, 0:512],
                                   in_=d_wmod[cc * 128:(cc + 1) * 128, mb * 512:(mb + 1) * 512]), 16)
                S.wait_ge(SD.sem, SD.v)
                V.wait_ge(SD.sem, SD.v)
            if k >= 2:
                V.wait_ge(TE.sem, mod_te[k - 2])
            gmb = mb in (4, 5, 10, 11)
            cv = VE.inc(V.tensor_copy(out=bw[:, 0:512], in_=stgw[:, 0:512]))
            if gmb:
                bwlo = xsq1 if k % 2 == 0 else xsq2
                V.wait_ge(VE.sem, cv)
                cv = VE.inc(V.tensor_sub(out=bwlo[:, 0:512], in0=stgw[:, 0:512], in1=bw[:, 0:512]))
            TN.wait_ge(VE.sem, cv)
            TN.wait_ge(VE.sem, ve_si)
            if cc == 0 and mb >= 2:
                TN.wait_ge(SC.sem, mod_sc[mb - 2])
            TN.ldweights(silu_cb[:, cc:cc + 1])
            if gmb:
                TN.matmul(bank[0:1, 0:512], silu_cb[:, cc:cc + 1], bw[:, 0:512],
                          start=(cc == 0), stop=False)
                TN.matmul(bank[0:1, 0:512], silu_cb[:, cc:cc + 1], bwlo[:, 0:512],
                          start=False, stop=False)
                TN.ldweights(silu_lo[:, cc:cc + 1])
                mod_te[k] = TE.inc(TN.matmul(bank[0:1, 0:512], silu_lo[:, cc:cc + 1], bw[:, 0:512],
                                             start=False, stop=(cc == 7)))
            else:
                mod_te[k] = TE.inc(TN.matmul(bank[0:1, 0:512], silu_cb[:, cc:cc + 1], bw[:, 0:512],
                                             start=(cc == 0), stop=(cc == 7)))
        S.wait_ge(TE.sem, mod_te[mb * 8 + 7])
        mod_sc[mb] = SC.inc(S.copy(out=rowf[0:1, (mb % 2) * 512:(mb % 2) * 512 + 512],
                                   in_=bank[0:1, 0:512]))
        if mb % 2 == 1:
            vecidx = mb // 2
            modr = rowf[0:1, 0:1024]
            V.wait_ge(SC.sem, mod_sc[mb])
            if vecidx in (2, 5):
                scs = SC.inc(S.activation(out=stg3[0:1, 0:512], in_=modr[0:1, 0:512], func=AF.Sign))
                scs2 = SC.inc(S.activation(out=rowf[0:1, 1536:2048], in_=modr[0:1, 512:1024], func=AF.Sign))
                v_ = VE.inc(V.tensor_scalar_mul(out=stg1[0:1, 0:512], in0=modr[0:1, 0:512], scalar1=-1.0))
                v_ = VE.inc(V.tensor_scalar_mul(out=stg2[0:1, 0:512], in0=modr[0:1, 512:1024], scalar1=-1.0))
                V.wait_ge(VE.sem, v_)
                v_ = VE.inc(V.tensor_max(out=stg1[0:1, 0:512], in0=stg1[0:1, 0:512], in1=modr[0:1, 0:512]))
                v_ = VE.inc(V.tensor_max(out=stg2[0:1, 0:512], in0=stg2[0:1, 0:512], in1=modr[0:1, 512:1024]))
                V.wait_ge(VE.sem, v_)
                v_ = VE.inc(V.tensor_scalar_max(out=stg1[0:1, 0:512], in0=stg1[0:1, 0:512], scalar1=0.25))
                v_ = VE.inc(V.tensor_scalar_max(out=stg2[0:1, 0:512], in0=stg2[0:1, 0:512], scalar1=0.25))
                V.wait_ge(VE.sem, v_)
                V.wait_ge(SC.sem, scs2)
                v_ = VE.inc(V.tensor_mul(out=modr[0:1, 0:512], in0=stg1[0:1, 0:512], in1=stg3[0:1, 0:512]))
                v_ = VE.inc(V.tensor_mul(out=modr[0:1, 512:1024], in0=stg2[0:1, 0:512], in1=rowf[0:1, 1536:2048]))
                V.wait_ge(VE.sem, v_)
            vrow = VE.inc(V.tensor_copy(out=rowb[0:1, 0:1024], in_=modr[0:1, 0:1024]))
            TN.wait_ge(VE.sem, vrow)
            bankc = pb[2]
            if vecidx >= 1:
                TN.wait_ge(SC.sem, colsc)
            for j in range(8):
                TN.ldweights(rowb[0:1, j * 128:(j + 1) * 128])
                tcol = TE.inc(TN.matmul(bankc[:, j:j + 1], rowb[0:1, j * 128:(j + 1) * 128],
                                        ones1c[0:1, 0:1], start=True, stop=True))
            S.wait_ge(TE.sem, tcol)
            dst = (sa_cb, caP1, ga_c, sb_c, cbP1, gb_c)[vecidx]
            if vecidx in (1, 4):
                colsc = SC.inc(S.activation(out=dst[:, 0:8], in_=bankc[:, 0:8], func=AF.Copy, bias=1.0))
            else:
                colsc = SC.inc(S.copy(out=dst[:, 0:8], in_=bankc[:, 0:8]))
    prol_colsc = colsc

    prologue = {s_.sem.name: s_.v for s_ in SEMS}
    prologue_vals = [(s_, s_.v) for s_ in SEMS]

    def emit_body(shalf):
        own_lo, own_hi = (0, SPLIT) if shalf == 0 else (SPLIT, T)
        kv_hi = SPLIT if shalf == 0 else T
        T_own = own_hi - own_lo
        T_oth = own_lo
        KT = kv_hi // 128

        for eng in (SY, V, S, TN):
            for s_, v_ in prologue_vals:
                if v_ > 0:
                    eng.wait_ge(s_.sem, v_)

        bst = ExitStack()
        sbb = lambda n, shape, d: bst.enter_context(nc.sbuf_tensor(f"{n}_{shalf}", shape, d))
        x_own = sbb("x_own", [128, 8 * T_own], BF16)

        def xslice(t0, w, cc):
            for (xt, glo, ghi, width, goff) in regions:
                if glo <= t0 and t0 + w <= ghi:
                    return xt[:, cc * width + (t0 - goff): cc * width + (t0 - goff) + w]
            raise AssertionError((t0, w))

        astk = ExitStack()
        sba = lambda n, shape, d: astk.enter_context(nc.sbuf_tensor(f"{n}_{shalf}", shape, d))
        regions = []
        if T_oth:
            x_oth = sba("x_oth", [128, 8 * T_oth], BF16)
            regions.append((x_oth, 0, T_oth, T_oth, 0))
        regions.append((x_own, own_lo, own_hi, T_own, own_lo))
        wq_bf = sba("wq_bf", [128, 8 * C], BF16)
        wk_bf = sba("wk_bf", [128, 8 * C], BF16)
        wv_bf = sba("wv_bf", [128, 8 * C], BF16)
        c_r = sba("c_r", [128, T], BF16)
        s_r = sba("s_r", [128, T], BF16)
        mstd2 = sba("mstd2", [2, T], BF16)
        corr_q = sba("corr_q", [2, C], BF16)
        corr_k = sba("corr_k", [2, C], BF16)
        corr_v = sba("corr_v", [2, C], BF16)

        def w_load(d_W, w_bf):
            for cc in range(8):
                GP.inc(G.dma_start(out=w_bf[:, cc * C:(cc + 1) * C],
                                   in_=d_W[cc * 128:(cc + 1) * 128, :]), 16)
            return gfence()

        def w_prep(w_bf, gws, corr, brow_src):
            ve_ch = {}
            V.wait_ge(GP.sem, gws)
            for cc in range(8):
                ve_ch[cc] = VE.inc(V.tensor_scalar(out=w_bf[:, cc * C:(cc + 1) * C],
                                                   in0=w_bf[:, cc * C:(cc + 1) * C],
                                                   scalar1=caP1[:, cc:cc + 1], scalar2=None, op0=ALU.mult))
            for cc in range(8):
                TN.wait_ge(VE.sem, ve_ch[cc])
                TN.ldweights(ones1c[:])
                TN.matmul(pb[4][0:1, 0:512], ones1c[:], w_bf[:, cc * C:cc * C + 512],
                          start=(cc == 0), stop=(cc == 7))
                tcs = TE.inc(TN.matmul(pb[5][0:1, 0:512], ones1c[:], w_bf[:, cc * C + 512:(cc + 1) * C],
                                       start=(cc == 0), stop=(cc == 7)))
            for cc in range(8):
                TN.ldweights(sa_cb[:, cc:cc + 1])
                TN.matmul(pb[6][0:1, 0:512], sa_cb[:, cc:cc + 1], w_bf[:, cc * C:cc * C + 512],
                          start=(cc == 0), stop=(cc == 7))
                tbp = TE.inc(TN.matmul(pb[7][0:1, 0:512], sa_cb[:, cc:cc + 1], w_bf[:, cc * C + 512:(cc + 1) * C],
                                       start=(cc == 0), stop=(cc == 7)))
            S.wait_ge(TE.sem, tbp)
            S.activation(out=corr[0:1, 0:512], in_=pb[4][0:1, 0:512], func=AF.Copy, scale=-1.0)
            sc1 = SC.inc(S.activation(out=corr[0:1, 512:C], in_=pb[5][0:1, 0:512], func=AF.Copy, scale=-1.0))
            V.wait_ge(TE.sem, tbp)
            V.tensor_add(out=rowb[0:1, 0:512], in0=pb[6][0:1, 0:512], in1=brow_src[0:1, 0:512])
            vb = VE.inc(V.tensor_add(out=rowb[0:1, 512:1024], in0=pb[7][0:1, 0:512], in1=brow_src[0:1, 512:C]))
            G.wait_ge(VE.sem, vb)
            gd = GP.inc(G.dma_start(out=corr[1:2, 0:C], in_=rowb[0:1, 0:1024]), 16)
            V.wait_ge(GP.sem, gd)
            TN.wait_ge(SC.sem, sc1)
            return gd

        gp_guard = {}
        swp_guard = [None, None, None, None]
        accq = pb[5]
        biasr = sba("biasr", [1, 3 * C], F32)
        LD.inc(SY.dma_start(out=biasr[0:1, 0:C], in_=d_bqr), 16)
        LD.inc(SY.dma_start(out=biasr[0:1, C:2 * C], in_=d_bkr), 16)
        LD.inc(SY.dma_start(out=biasr[0:1, 2 * C:3 * C], in_=d_bvr), 16)
        ldb = syfence()
        V.wait_ge(LD.sem, ldb)
        xblk = [(xt, width, goff, t0, w)
                for (xt, glo, ghi, width, goff) in regions
                for (t0, w) in _blocks(glo, ghi)]
        gxv = {}

        def x_load(n):
            for _ in range(n):
                if not xblk:
                    return
                xt, width, goff, t0, w = xblk.pop(0)
                for cc in range(8):
                    GP.inc(G.dma_start(out=xt[:, cc * width + (t0 - goff): cc * width + (t0 - goff) + w],
                                       in_=d_xT[cc * 128:(cc + 1) * 128, t0:t0 + w]), 16)
                gxv[(id(xt), t0)] = gfence()
        gws_q = w_load(d_Wq, wq_bf)
        x_load(2)
        gws_k = w_load(d_Wk, wk_bf)
        x_load(2)
        gws_v = w_load(d_Wv, wv_bf)
        x_load(99)
        gd_q = w_prep(wq_bf, gws_q, corr_q, biasr[0:1, 0:C])
        gd_k = w_prep(wk_bf, gws_k, corr_k, biasr[0:1, C:2 * C])
        gd_v = w_prep(wv_bf, gws_v, corr_v, biasr[0:1, 2 * C:3 * C])

        last_rcol_sc = None
        last_rope_ve = None
        last_mstd_gp = None
        prev_stat_te = None
        for (xt, glo, ghi, width, goff) in regions:
            for (t0, w) in _blocks(glo, ghi):
                gx_blk = gxv[(id(xt), t0)]
                for cc in range(8):
                    xsq = xsq1 if cc % 2 == 0 else xsq2
                    S.wait_ge(GP.sem, gx_blk)
                    if cc >= 2:
                        S.wait_ge(TE.sem, te_var[cc - 2])
                    elif prev_stat_te is not None:
                        S.wait_ge(TE.sem, prev_stat_te)
                    ssq = SC.inc(S.activation(out=xsq[:, 0:w], in_=xslice(t0, w, cc), func=AF.Square))
                    TN.wait_ge(SC.sem, ssq)
                    TN.ldweights(ones1c[:])
                    TN.matmul(pb[0][0:1, 0:w], ones1c[:], xslice(t0, w, cc),
                              start=(cc == 0), stop=(cc == 7))
                    tv = TE.inc(TN.matmul(pb[1][0:1, 0:w], ones1c[:], xsq[:, 0:w],
                                          start=(cc == 0), stop=(cc == 7)))
                    if cc == 0:
                        te_var = {}
                    te_var[cc] = tv
                prev_stat_te = tv
                S.wait_ge(TE.sem, tv)
                if last_rope_ve is not None:
                    S.wait_ge(VE.sem, last_rope_ve)
                S.activation(out=mrow[0:1, 0:w], in_=pb[0][0:1, 0:w], func=AF.Copy, scale=1.0 / 1024.0)
                scm = SC.inc(S.activation(out=srow[0:1, 0:w], in_=pb[1][0:1, 0:w],
                                          func=AF.Copy, scale=1.0 / 1024.0))
                V.wait_ge(SC.sem, scm)
                vm_ = VE.inc(V.tensor_mul(out=stg1[0:1, 0:w], in0=mrow[0:1, 0:w], in1=mrow[0:1, 0:w]))
                V.wait_ge(VE.sem, vm_)
                vv = VE.inc(V.tensor_sub(out=stg1[0:1, 0:w], in0=srow[0:1, 0:w], in1=stg1[0:1, 0:w]))
                S.wait_ge(VE.sem, vv)
                sl_ = SC.inc(S.activation(out=lrow[0:1, 0:w], in_=stg1[0:1, 0:w], func=AF.Ln, bias=LN_EPS))
                S.wait_ge(SC.sem, sl_)
                S.activation(out=rrow[0:1, 0:w], in_=lrow[0:1, 0:w], func=AF.Exp, scale=-0.5)
                scstd = SC.inc(S.activation(out=srow[0:1, 0:w], in_=lrow[0:1, 0:w], func=AF.Exp, scale=0.5))
                V.wait_ge(SC.sem, scstd)
                V.tensor_copy(out=rbf[0:1, 0:w], in_=rrow[0:1, 0:w])
                V.tensor_copy(out=mstd2[0:1, t0:t0 + w], in_=mrow[0:1, 0:w])
                if last_mstd_gp is not None:
                    V.wait_ge(GP.sem, last_mstd_gp)
                vstd = VE.inc(V.tensor_copy(out=scr_c[0:1, 0:w], in_=srow[0:1, 0:w]))
                G.wait_ge(VE.sem, vstd)
                last_mstd_gp = GP.inc(G.dma_start(out=mstd2[1:2, t0:t0 + w], in_=scr_c[0:1, 0:w]), 16)
                TN.wait_ge(VE.sem, vstd)
                if last_rcol_sc is not None:
                    TN.wait_ge(SC.sem, last_rcol_sc)
                nj = (w + 127) // 128
                for j in range(nj):
                    jw = min(128, w - j * 128)
                    TN.ldweights(rbf[0:1, j * 128: j * 128 + jw])
                    trc = TE.inc(TN.matmul(pb[2][0:jw, j:j + 1], rbf[0:1, j * 128: j * 128 + jw],
                                           ones1c[0:1, 0:1], start=True, stop=True))
                S.wait_ge(TE.sem, trc)
                last_rcol_sc = SC.inc(S.copy(out=r_col[:, t0 // 128: t0 // 128 + nj], in_=pb[2][:, 0:nj]))
                if last_rope_ve is not None:
                    TN.wait_ge(VE.sem, last_rope_ve)
                TN.ldweights(oner_lhs[:])
                trr = TE.inc(TN.matmul(pb[3][:, 0:w], oner_lhs[:], rbf[0:1, 0:w],
                                       start=True, stop=True))
                if last_rope_ve is not None:
                    SY.wait_ge(VE.sem, last_rope_ve)
                LD.inc(SY.dma_start(out=stg2[:, 0:w], in_=d_cos[:, t0:t0 + w]), 16)
                dcs = syfence()
                V.wait_ge(LD.sem, dcs)
                V.wait_ge(TE.sem, trr)
                vcr = VE.inc(V.tensor_mul(out=c_r[:, t0:t0 + w], in0=stg2[:, 0:w], in1=pb[3][:, 0:w]))
                SY.wait_ge(VE.sem, vcr)
                LD.inc(SY.dma_start(out=stg2[:, 0:w], in_=d_sin[:, t0:t0 + w]), 16)
                dsn = syfence()
                V.wait_ge(LD.sem, dsn)
                last_rope_ve = VE.inc(V.tensor_mul(out=s_r[:, t0:t0 + w], in0=stg2[:, 0:w], in1=pb[3][:, 0:w]))

        mstd_gate_gp = GP.v
        def qk_proj(w_bf, corr, dst, dst_stride, rlist):
            for eng in (TN,):
                eng.wait_ge(VE.sem, VE.v)
                eng.wait_ge(SC.sem, SC.v)
                eng.wait_ge(GP.sem, GP.v)
            bank_guard = {}
            seq = 0
            for (xt, glo, ghi, width, goff) in rlist:
                blocks = _blocks(glo, ghi)
                nb = len(blocks)
                for dc in range(8):
                    base = seq
                    for cc in range(8):
                        TN.ldweights(w_bf[:, cc * C + dc * 128: cc * C + (dc + 1) * 128])
                        for bi, (t0, w) in enumerate(blocks):
                            sl = (base + bi) % 8
                            if cc == 0 and bank_guard.get(sl) is not None:
                                TN.wait_ge(VE.sem, bank_guard[sl])
                            TN.matmul(pb[sl][:, 0:w], w_bf[:, cc * C + dc * 128: cc * C + (dc + 1) * 128],
                                      xt[:, cc * width + (t0 - goff): cc * width + (t0 - goff) + w],
                                      start=(cc == 0), stop=False)
                    seq += nb
                    TN.ldweights(corr[:, dc * 128:(dc + 1) * 128])
                    for bi, (t0, w) in enumerate(blocks):
                        sl = (base + bi) % 8
                        tlast = TE.inc(TN.matmul(pb[sl][:, 0:w], corr[:, dc * 128:(dc + 1) * 128],
                                                 mstd2[:, t0:t0 + w], start=False, stop=True))
                        V.wait_ge(TE.sem, tlast)
                        if swp_guard[0] is not None:
                            V.wait_ge(VE.sem, swp_guard[0])
                        vc1 = VE.inc(V.tensor_mul(out=stg1[:, 0:w], in0=pb[sl][:, 0:w], in1=c_r[:, t0:t0 + w]))
                        bw = bstg1 if (dc + bi) % 2 == 0 else bstg2
                        V.wait_ge(VE.sem, vc1)
                        if gp_guard.get(1 - (dc + bi) % 2) is not None:
                            V.wait_ge(GP.sem, gp_guard[1 - (dc + bi) % 2])
                        v2 = VE.inc(V.tensor_mul(out=bw[:, 0:w], in0=pb[sl][:, 0:w], in1=s_r[:, t0:t0 + w]))
                        bank_guard[sl] = v2
                        V.wait_ge(VE.sem, v2)
                        vsw = VE.inc(V.stream_shuffle(out=xsq1[:, 0:w], in_=bw[:, 0:w], mask=SWAPM))
                        V.wait_ge(VE.sem, vsw)
                        if gp_guard.get((dc + bi) % 2) is not None:
                            V.wait_ge(GP.sem, gp_guard[(dc + bi) % 2])
                        bw2 = bstg2 if (dc + bi) % 2 == 0 else bstg1
                        v3 = VE.inc(V.tensor_add(out=bw2[:, 0:w], in0=stg1[:, 0:w], in1=xsq1[:, 0:w]))
                        swp_guard[0] = v3
                        G.wait_ge(VE.sem, v3)
                        gp_guard[(dc + bi) % 2] = GP.inc(
                            G.dma_start(out=dst[:, dc * dst_stride + (t0 - (own_lo if dst is d_qd else 0)):
                                                 dc * dst_stride + (t0 - (own_lo if dst is d_qd else 0)) + w],
                                        in_=bw2[:, 0:w]), 16)

        qk_proj(wq_bf, corr_q, d_qd, TOmax, [regions[-1]])
        qk_proj(wk_bf, corr_k, d_kd, T, regions)

        TN.wait_ge(VE.sem, VE.v)
        TN.wait_ge(SC.sem, SC.v)
        S.wait_ge(GP.sem, GP.v)
        S.wait_ge(VE.sem, VE.v)
        vprev = {}
        for tc in range(KT):
            b0 = pb[(tc % 4) * 2]
            b1_ = pb[(tc % 4) * 2 + 1]
            tglob = tc * 128
            for cc in range(8):
                TN.ldweights(xslice(tglob, 128, cc))
                if cc == 0 and vprev.get(tc - 4) is not None:
                    TN.wait_ge(SC.sem, vprev[tc - 4])
                TN.matmul(b0[:, 0:512], xslice(tglob, 128, cc), wv_bf[:, cc * C: cc * C + 512],
                          start=(cc == 0), stop=False)
                TN.matmul(b1_[:, 0:512], xslice(tglob, 128, cc), wv_bf[:, cc * C + 512:(cc + 1) * C],
                          start=(cc == 0), stop=False)
            TN.ldweights(mstd2[:, tglob:tglob + 128])
            TN.matmul(b0[:, 0:512], mstd2[:, tglob:tglob + 128], corr_v[:, 0:512], start=False, stop=True)
            tv2 = TE.inc(TN.matmul(b1_[:, 0:512], mstd2[:, tglob:tglob + 128], corr_v[:, 512:C],
                                   start=False, stop=True))
            S.wait_ge(TE.sem, tv2)
            if vprev.get(tc - 1) is not None:
                S.wait_ge(GP.sem, vstore[tc - 1])
            S.activation(out=bstg1[:, 0:512], in_=b0[:, 0:512], func=AF.Copy, scale=r_col[:, tc:tc + 1])
            svv = SC.inc(S.activation(out=bstg2[:, 0:512], in_=b1_[:, 0:512], func=AF.Copy,
                                      scale=r_col[:, tc:tc + 1]))
            vprev[tc] = svv
            G.wait_ge(SC.sem, svv)
            for p in range(4):
                GP.inc(G.dma_start(out=d_vd[:, p * T + tglob: p * T + tglob + 128],
                                   in_=bstg1[:, p * 128:(p + 1) * 128]), 16)
            for p in range(3):
                GP.inc(G.dma_start(out=d_vd[:, (4 + p) * T + tglob: (4 + p) * T + tglob + 128],
                                   in_=bstg2[:, p * 128:(p + 1) * 128]), 16)
            gv = GP.inc(G.dma_start(out=d_vd[:, 7 * T + tglob: 7 * T + tglob + 128],
                                    in_=bstg2[:, 3 * 128:4 * 128]), 16)
            if tc == 0:
                vstore = {}
            vstore[tc] = gv

        astk.close()
        phase_barrier()
        if dbg:
            dbg_dump([(d_dmod[:, 0:8], caP1[:]), (d_dmod[:, 8:16], cbP1[:]),
                      (d_dmod[:, 16:24], sb_c[:]), (d_dmod[:, 24:32], ga_c[:]),
                      (d_dmod[:, 32:40], gb_c[:]), (d_dsa, sa_cb[:])])
            dbg_dump([(d_dxA[:, cc * T_own:(cc + 1) * T_own],
                       x_own[:, cc * T_own:(cc + 1) * T_own]) for cc in range(8)])

        bstk = ExitStack()
        sbt = lambda n, shape, d: bstk.enter_context(nc.sbuf_tensor(f"{n}_{shalf}", shape, d))
        obuf = sbt("obuf", [128, NPAIR * T_own], BF16)
        wp_bf = sbt("wp_bf", [128, 8 * C], BF16)
        qstg = [sbt(f"qstg{i}", [128, 2048], F32) for i in range(2)]
        f8w1 = [sbt(f"f8w1_{i}", [128, 2048], FP8) for i in range(2)]
        f8w2 = [sbt(f"f8w2_{i}", [128, 2048], FP8) for i in range(2)]
        attn_stk = ExitStack()
        sbn = lambda n, shape, d: attn_stk.enter_context(nc.sbuf_tensor(f"{n}_{shalf}", shape, d))
        kbuf = [sbn(f"kbuf{i}", [128, kv_hi], BF16) for i in range(2)]
        vbuf = [sbn(f"vbuf{i}", [128, kv_hi], BF16) for i in range(2)]
        qbuf = [sbn(f"qbuf{i}", [128, T_own], BF16) for i in range(2)]
        pbuf = [sbn(f"pbuf{i}", [128, 1024], BF16) for i in range(2)]
        ksq = sbn("ksq", [128, kv_hi], BF16)
        recipb = sbn("recipb", [128, 512], F32)
        recipb2 = sbn("recipb2", [128, 512], F32)
        Spair = [big0, big1]
        ObA = p4[:]
        ObB = p5[:]
        SbA = p6[:]
        SbB = p7[:]
        Baux = p6[:]

        for cc in range(8):
            GP.inc(G.dma_start(out=wp_bf[:, cc * C:(cc + 1) * C],
                               in_=d_Wp[cc * 128:(cc + 1) * 128, :]), 16)
        gfence()
        SY.wait_ge(GP.sem, GP.v)
        ldp = {}
        LD.inc(SY.dma_start(out=qbuf[0][:], in_=d_qd[:, 0:T_own]), 16)
        LD.inc(SY.dma_start(out=kbuf[0][:], in_=d_kd[:, 0:kv_hi]), 16)
        LD.inc(SY.dma_start(out=vbuf[0][:], in_=d_vd[:, 0:kv_hi]), 16)
        ldp[0] = syfence()

        te_pair_end = {}
        oblocks = _blocks(own_lo, own_hi)
        prev_tav = None
        prev_exp_last = None
        prev_evac = None

        qent = VE.v
        q1_ve, q1_sd, q1_st = {}, {}, {}
        q2_ve, q2_gp, q2_st = {}, {}, {}

        def emit_w1_quant(i):
            cc, hh = i // 2, i % 2
            stgw = stgA if i % 2 == 0 else stgB
            f8b = f8w1[i % 2]
            G.wait_ge(VE.sem, q1_ve[i - 2] if i >= 2 else qent)
            GP.inc(G.dma_start(
                out=stgw[:, 0:2048],
                in_=d_W1T[cc * 128:(cc + 1) * 128, hh * 2048:(hh + 1) * 2048]), 16)
            q1_sd[i] = gfence()
            V.wait_ge(GP.sem, q1_sd[i])
            if i >= 2:
                V.wait_ge(LD.sem, q1_st[i - 2])
            q1_ve[i] = vcustom(OPS["ANT_FQ8"], out=f8b[:, 0:2048], in0=stgw[:, 0:2048])
            SY.wait_ge(VE.sem, q1_ve[i])
            LD.inc(SY.dma_start(
                out=d_w1q8[cc * 128:(cc + 1) * 128, hh * 2048:(hh + 1) * 2048],
                in_=f8b[:, 0:2048]), 16)
            q1_st[i] = syfence()

        def emit_w2_quant(j):
            stgw = qstg[j % 2]
            f8b = f8w2[j % 2]
            G.wait_ge(VE.sem, q2_ve[j - 2] if j >= 2 else qent)
            for k in range(2):
                ncx = j * 2 + k
                GP.inc(G.dma_start(out=stgw[:, k * 1024:(k + 1) * 1024],
                                   in_=d_W2T[ncx * 128:(ncx + 1) * 128, :]), 16)
            q2_gp[j] = gfence()
            V.wait_ge(GP.sem, q2_gp[j])
            if j >= 2:
                V.wait_ge(LD.sem, q2_st[j - 2])
            q2_ve[j] = vcustom(OPS["ANT_FQ8"], out=f8b[:, 0:2048], in0=stgw[:, 0:2048])
            SY.wait_ge(VE.sem, q2_ve[j])
            for k in range(2):
                ncx = j * 2 + k
                LD.inc(SY.dma_start(out=d_w2q8[ncx * 128:(ncx + 1) * 128, :],
                                    in_=f8b[:, k * 1024:(k + 1) * 1024]), 16)
            q2_st[j] = syfence()

        for p in range(NPAIR):
            pk = p % 2
            if p + 1 < NPAIR:
                if p >= 1:
                    SY.wait_ge(TE.sem, te_pair_end[p - 1])
                    SY.wait_ge(SC.sem, sc_pair_end[p - 1])
                LD.inc(SY.dma_start(out=qbuf[1 - pk][:], in_=d_qd[:, (p + 1) * TOmax:(p + 1) * TOmax + T_own]), 16)
                LD.inc(SY.dma_start(out=kbuf[1 - pk][:], in_=d_kd[:, (p + 1) * T:(p + 1) * T + kv_hi]), 16)
                LD.inc(SY.dma_start(out=vbuf[1 - pk][:], in_=d_vd[:, (p + 1) * T:(p + 1) * T + kv_hi]), 16)
                ldp[p + 1] = syfence()
            S.wait_ge(LD.sem, ldp[p])
            S.wait_ge(TE.sem, TE.v)
            sksq = SC.inc(S.activation(out=ksq[:], in_=kbuf[pk][:], func=AF.Square))
            if prev_exp_last is not None:
                TN.wait_ge(SC.sem, prev_exp_last)
            if prev_evac is not None:
                TN.wait_ge(VE.sem, prev_evac)

            def warm(n):
                TN.ldweights(permsw[:])
                for _ in range(n):
                    TN.matmul(Spair[0][:, 0:512], permsw[:], qmask[:, 0:512],
                              start=True, stop=True)
            warm(16)
            TN.wait_ge(SC.sem, sksq)
            TN.wait_ge(VE.sem, VE.v)
            first = True
            vkm = None
            for (t0, w) in _blocks(0, kv_hi):
                TN.ldweights(ones1c[:])
                if vkm is not None:
                    TN.wait_ge(VE.sem, vkm)
                tkn = TE.inc(TN.matmul(Baux[0:1, 0:w], ones1c[:], ksq[:, t0:t0 + w], start=True, stop=True))
                V.wait_ge(TE.sem, tkn)
                if first:
                    vkm = VE.inc(V.reduce_max(out=kmax2[0:1, 0:1], in_=Baux[0:1, 0:w], axis=AX.X))
                    first = False
                else:
                    vk0 = VE.inc(V.reduce_max(out=kmax2[0:1, 1:2], in_=Baux[0:1, 0:w], axis=AX.X))
                    V.wait_ge(VE.sem, vk0)
                    vkm = VE.inc(V.tensor_max(out=kmax2[0:1, 0:1], in0=kmax2[0:1, 0:1], in1=kmax2[0:1, 1:2]))
            nob_p = len(oblocks)
            vqs = None
            tq2 = None
            knorm_te = TE.v
            for qi, (q0, qw) in enumerate(oblocks):
                qrel = q0 - own_lo
                S.wait_ge(TE.sem, tq2 if tq2 is not None else knorm_te)
                S.wait_ge(SC.sem, SC.v)
                sq2 = SC.inc(S.activation(out=ksq[:, 0:qw], in_=qbuf[pk][:, qrel:qrel + qw],
                                          func=AF.Square))
                TN.wait_ge(SC.sem, sq2)
                TN.wait_ge(VE.sem, vqs if vqs is not None else vkm)
                TN.ldweights(ones1c[:])
                tq2 = TE.inc(TN.matmul(Baux[0:1, 0:qw], ones1c[:], ksq[:, 0:qw], start=True, stop=True))
                V.wait_ge(TE.sem, tq2)
                if qi == 0:
                    V.wait_ge(VE.sem, vkm)
                vqs = VE.inc(V.reduce_max(out=kmax2[0:1, 2 + qi:3 + qi], in_=Baux[0:1, 0:qw], axis=AX.X))
            V.wait_ge(VE.sem, vqs)
            vq = VE.inc(V.tensor_scalar(out=kmax2[0:1, 8:8 + nob_p], in0=kmax2[0:1, 2:2 + nob_p],
                                        scalar1=kmax2[0:1, 0:1], scalar2=None, op0=ALU.mult))
            S.wait_ge(VE.sem, vq)
            sb0_ = SC.inc(S.activation(out=kmax2[0:1, 16:16 + nob_p], in_=kmax2[0:1, 8:8 + nob_p],
                                       func=AF.Ln))
            S.wait_ge(SC.sem, sb0_)
            ssb = SC.inc(S.activation(out=kmax2[0:1, 24:24 + nob_p], in_=kmax2[0:1, 16:16 + nob_p],
                                      func=AF.Exp, scale=0.5, bias=LNQ))
            V.wait_ge(SC.sem, ssb)
            vnb = VE.inc(V.tensor_scalar(out=scr_b[0:1, 0:nob_p], in0=kmax2[0:1, 24:24 + nob_p],
                                         scalar1=-1.0, scalar2=None, op0=ALU.mult))
            warm(12)
            TN.wait_ge(VE.sem, vnb)
            TN.ldweights(oner_lhs[:])
            tnb = TE.inc(TN.matmul(Baux[:, 0:nob_p], oner_lhs[:], scr_b[0:1, 0:nob_p],
                                   start=True, stop=True))
            S.wait_ge(TE.sem, tnb)
            snb = SC.inc(S.copy(out=negBall[:, 0:nob_p], in_=Baux[:, 0:nob_p]))
            TN.wait_ge(SC.sem, snb)
            emit_w1_quant(2 * p)
            emit_w1_quant(2 * p + 1)
            emit_w2_quant(2 * p)
            emit_w2_quant(2 * p + 1)
            for qi, (q0, qw) in enumerate(oblocks):
                qrel = q0 - own_lo
                KTQ = (q0 + qw + 127) // 128
                g0 = q0 // 128
                nbias = negBall[:, qi:qi + 1]
                if prev_evac is not None:
                    TN.wait_ge(VE.sem, prev_evac)
                if prev_exp_last is not None:
                    TN.wait_ge(SC.sem, prev_exp_last)
                exp_done = {}
                av_done = {}

                def emit_av(kt):
                    kk = kt % 2
                    TN.wait_ge(SC.sem, exp_done[kt])
                    TN.ldweights(vbuf[pk][:, kt * 128:kt * 128 + 64], tile_position=(0, 0))
                    TN.matmul(ObA[0:64, 0:qw], vbuf[pk][:, kt * 128:kt * 128 + 64], pbuf[kk][:, 0:qw],
                              start=(kt == 0), stop=(kt == KTQ - 1), tile_position=(0, 0))
                    TN.ldweights(vbuf[pk][:, kt * 128 + 64:(kt + 1) * 128], tile_position=(0, 64))
                    TN.matmul(ObB[64:128, 0:qw], vbuf[pk][:, kt * 128 + 64:(kt + 1) * 128],
                              pbuf[kk][:, 512:512 + qw], start=(kt == 0), stop=(kt == KTQ - 1),
                              tile_position=(0, 64))
                    TN.ldweights(ones128[:, 0:64], tile_position=(0, 0))
                    TN.matmul(SbA[0:64, 0:qw], ones128[:, 0:64], pbuf[kk][:, 0:qw],
                              start=(kt == 0), stop=(kt == KTQ - 1), tile_position=(0, 0))
                    TN.ldweights(ones128[:, 64:128], tile_position=(0, 64))
                    tav = TE.inc(TN.matmul(SbB[64:128, 0:qw], ones128[:, 64:128],
                                           pbuf[kk][:, 512:512 + qw],
                                           start=(kt == 0), stop=(kt == KTQ - 1),
                                           tile_position=(0, 64)))
                    av_done[kt] = tav
                    return tav

                for kt in range(KTQ):
                    kk = kt % 2
                    j = kt - g0
                    masked = j >= 0
                    if exp_done.get(kt - 2) is not None:
                        TN.wait_ge(SC.sem, exp_done[kt - 2])
                    TN.ldweights(kbuf[pk][0:64, kt * 128:(kt + 1) * 128], tile_position=(0, 0))
                    TN.matmul(Spair[kk][:, 0:qw], kbuf[pk][0:64, kt * 128:(kt + 1) * 128],
                              qbuf[pk][0:64, qrel:qrel + qw], start=True, stop=not masked,
                              tile_position=(0, 0))
                    TN.ldweights(kbuf[pk][64:128, kt * 128:(kt + 1) * 128], tile_position=(64, 0))
                    tqk0 = TN.matmul(Spair[kk][:, 512:512 + qw], kbuf[pk][64:128, kt * 128:(kt + 1) * 128],
                                     qbuf[pk][64:128, qrel:qrel + qw], start=True, stop=not masked,
                                     tile_position=(64, 0))
                    if masked:
                        jw = min(128, qw - j * 128)
                        last = (j == 0)
                        TN.ldweights(trib[:], tile_position=(0, 0))
                        TN.matmul(Spair[kk][:, j * 128: j * 128 + jw], trib[:], eyeb[:, 0:jw],
                                  start=False, stop=last)
                        tqk0 = TN.matmul(Spair[kk][:, 512 + j * 128: 512 + j * 128 + jw],
                                         trib[:], eyeb[:, 0:jw], start=False, stop=last)
                        if j > 0:
                            TN.ldweights(negrow[:])
                            TN.matmul(Spair[kk][:, 0:j * 128], negrow[:], onesr[0:1, 0:j * 128],
                                      start=False, stop=True)
                            tqk0 = TN.matmul(Spair[kk][:, 512:512 + j * 128], negrow[:],
                                             onesr[0:1, 0:j * 128], start=False, stop=True)
                    tqk = TE.inc(tqk0)
                    S.wait_ge(TE.sem, tqk)
                    if av_done.get(kt - 2) is not None:
                        S.wait_ge(TE.sem, av_done[kt - 2])
                    elif kt < 2 and prev_tav is not None:
                        S.wait_ge(TE.sem, prev_tav)
                    if qw == 512:
                        se = SC.inc(S.activation(out=pbuf[kk][:], in_=Spair[kk][:], func=AF.Exp,
                                                 scale=0.125, bias=nbias))
                    else:
                        S.activation(out=pbuf[kk][:, 0:qw], in_=Spair[kk][:, 0:qw], func=AF.Exp,
                                     scale=0.125, bias=nbias)
                        se = SC.inc(S.activation(out=pbuf[kk][:, 512:512 + qw], in_=Spair[kk][:, 512:512 + qw],
                                                 func=AF.Exp, scale=0.125, bias=nbias))
                    exp_done[kt] = se
                    if kt >= 1:
                        emit_av(kt - 1)
                tav = emit_av(KTQ - 1)
                prev_tav = tav
                prev_exp_last = exp_done[KTQ - 1]
                V.wait_ge(TE.sem, tav)
                VE.inc(V.reciprocal_approx_fast(out=recipb[0:64, 0:qw], in_=SbA[0:64, 0:qw]))
                vrc = VE.inc(V.reciprocal_approx_fast(out=recipb2[:, 0:qw], in_=SbB[:, 0:qw]))
                V.wait_ge(VE.sem, vrc)
                VE.inc(V.tensor_mul(out=obuf[0:64, p * T_own + qrel: p * T_own + qrel + qw],
                                    in0=ObA[0:64, 0:qw], in1=recipb[0:64, 0:qw]))
                prev_evac = VE.inc(V.tensor_mul(out=obuf[64:128, p * T_own + qrel: p * T_own + qrel + qw],
                                                in0=ObB[64:128, 0:qw], in1=recipb2[64:128, 0:qw]))
            te_pair_end[p] = TE.v
            if p == 0:
                sc_pair_end = {}
            sc_pair_end[p] = SC.v

        attn_stk.close()
        phase_barrier()
        if dbg:
            dbg_dump([(d_dxB[:, cc * T_own:(cc + 1) * T_own],
                       x_own[:, cc * T_own:(cc + 1) * T_own]) for cc in range(8)])
            dbg_dump([(d_dob[:, pp * T_own:(pp + 1) * T_own],
                       obuf[:, pp * T_own:(pp + 1) * T_own]) for pp in range(NPAIR)])

        TN.wait_ge(VE.sem, VE.v)
        TN.wait_ge(SC.sem, SC.v)
        nob = len(oblocks)
        cx_ve = {}
        for cc in range(8):
            for pidx in range(NPAIR):
                TN.ldweights(wp_bf[:, pidx * C + cc * 128: pidx * C + (cc + 1) * 128])
                for bi, (q0, qw) in enumerate(oblocks):
                    qrel = q0 - own_lo
                    if pidx == 0 and cc > 0:
                        TN.wait_ge(VE.sem, cx_ve[(cc - 1, bi)])
                    TN.matmul(pb[bi][:, 0:qw], wp_bf[:, pidx * C + cc * 128: pidx * C + (cc + 1) * 128],
                              obuf[:, pidx * T_own + qrel: pidx * T_own + qrel + qw],
                              start=(pidx == 0), stop=False)
            TN.wait_ge(VE.sem, ve_bp)
            TN.ldweights(bprow[0:1, cc * 128:(cc + 1) * 128])
            for bi, (q0, qw) in enumerate(oblocks):
                qrel = q0 - own_lo
                tpj = TE.inc(TN.matmul(pb[bi][:, 0:qw], bprow[0:1, cc * 128:(cc + 1) * 128],
                                       onesr[0:1, 0:qw], start=False, stop=True))
                V.wait_ge(TE.sem, tpj)
                cx_ve[(cc, bi)] = VE.inc(V.scalar_tensor_tensor(
                    out=x_own[:, cc * T_own + qrel: cc * T_own + qrel + qw],
                    in0=pb[bi][:, 0:qw], scalar=ga_c[:, cc:cc + 1],
                    in1=x_own[:, cc * T_own + qrel: cc * T_own + qrel + qw],
                    op0=ALU.mult, op1=ALU.add))

        bstk.close()
        phase_barrier()
        if dbg:
            dbg_dump([(d_dx2[:, cc * T_own:(cc + 1) * T_own],
                       x_own[:, cc * T_own:(cc + 1) * T_own]) for cc in range(8)])

        dstk = ExitStack()
        sbd = lambda n, shape, d: dstk.enter_context(nc.sbuf_tensor(f"{n}_{shalf}", shape, d))
        w1q = sbd("w1q", [128, 8 * 4096], FP8)
        w2q = sbd("w2q", [128, 32 * C], FP8)
        fqh2 = sbd("fqh2", [128, 8 * T_own], FP8)
        GW = 1216 if shalf == 0 else 1664
        fqh1 = sbd("fqh1", [128, 32 * GW], FP8)

        dv0 = syfence()
        for cc in range(8):
            LD.inc(SY.dma_start(out=w1q[:, cc * 4096:(cc + 1) * 4096],
                                in_=d_w1q8[cc * 128:(cc + 1) * 128, :]), 16)
        ldw1 = syfence()
        G.wait_ge(LD.sem, dv0)
        for ncx in range(32):
            GP.inc(G.dma_start(out=w2q[:, ncx * 1024:(ncx + 1) * 1024],
                               in_=d_w2q8[ncx * 128:(ncx + 1) * 128, :]), 16)
            if ncx % 8 == 7:
                gfence()
        gw2 = gfence()
        TN.wait_ge(LD.sem, ldw1)
        TN.wait_ge(GP.sem, gw2)
        bguard = {}

        prev_d1_te = None
        prev_d1_ve = None
        fq_ve = {}
        for bi, (q0, qw) in enumerate(oblocks):
            qrel = q0 - own_lo
            for cc in range(8):
                xsq = xsq1 if cc % 2 == 0 else xsq2
                S.wait_ge(VE.sem, cx_ve[(cc, bi)])
                if cc >= 2:
                    S.wait_ge(TE.sem, te_var2[cc - 2])
                elif prev_d1_te is not None:
                    S.wait_ge(TE.sem, prev_d1_te)
                ssq = SC.inc(S.activation(out=xsq[:, 0:qw],
                                          in_=x_own[:, cc * T_own + qrel: cc * T_own + qrel + qw],
                                          func=AF.Square))
                TN.wait_ge(SC.sem, ssq)
                if cc == 0 and prev_d1_ve is not None:
                    TN.wait_ge(VE.sem, prev_d1_ve)
                TN.ldweights(ones1c[:])
                TN.matmul(pb[6][0:1, 0:qw], ones1c[:],
                          x_own[:, cc * T_own + qrel: cc * T_own + qrel + qw],
                          start=(cc == 0), stop=(cc == 7))
                tvq = TE.inc(TN.matmul(pb[7][0:1, 0:qw], ones1c[:], xsq[:, 0:qw],
                                       start=(cc == 0), stop=(cc == 7)))
                if cc == 0:
                    te_var2 = {}
                te_var2[cc] = tvq
            S.wait_ge(TE.sem, tvq)
            if prev_d1_te is not None:
                S.wait_ge(VE.sem, prev_d1_ve)
            S.activation(out=mrow[0:1, 0:qw], in_=pb[6][0:1, 0:qw], func=AF.Copy, scale=1.0 / 1024.0)
            scm2 = SC.inc(S.activation(out=srow[0:1, 0:qw], in_=pb[7][0:1, 0:qw],
                                       func=AF.Copy, scale=1.0 / 1024.0))
            V.wait_ge(SC.sem, scm2)
            vm2 = VE.inc(V.tensor_mul(out=stg1[0:1, 0:qw], in0=mrow[0:1, 0:qw], in1=mrow[0:1, 0:qw]))
            V.wait_ge(VE.sem, vm2)
            vvv = VE.inc(V.tensor_sub(out=stg1[0:1, 0:qw], in0=srow[0:1, 0:qw], in1=stg1[0:1, 0:qw]))
            S.wait_ge(VE.sem, vvv)
            sl2 = SC.inc(S.activation(out=lrow[0:1, 0:qw], in_=stg1[0:1, 0:qw], func=AF.Ln, bias=LN_EPS))
            S.wait_ge(SC.sem, sl2)
            scr2b = SC.inc(S.activation(out=rrow[0:1, 0:qw], in_=lrow[0:1, 0:qw], func=AF.Exp, scale=-0.5))
            V.wait_ge(SC.sem, scr2b)
            V.tensor_copy(out=rbf[0:1, 0:qw], in_=rrow[0:1, 0:qw])
            vms = VE.inc(V.tensor_mul(out=stg2[0:1, 0:qw], in0=mrow[0:1, 0:qw], in1=rrow[0:1, 0:qw]))
            V.wait_ge(VE.sem, vms)
            vmr = VE.inc(V.tensor_copy(out=mr2b[0:1, 0:qw], in_=stg2[0:1, 0:qw]))
            TN.wait_ge(VE.sem, vmr)
            TN.ldweights(oner_lhs[:])
            TN.matmul(pb[6][:, 0:qw], oner_lhs[:], rbf[0:1, 0:qw], start=True, stop=True)
            trp = TE.inc(TN.matmul(pb[7][:, 0:qw], oner_lhs[:], mr2b[0:1, 0:qw],
                                   start=True, stop=True))
            prev_d1_te = trp
            for cc in range(8):
                V.wait_ge(TE.sem, trp)
                vd2 = VE.inc(V.tensor_mul(out=stg1[:, 0:qw],
                             in0=x_own[:, cc * T_own + qrel: cc * T_own + qrel + qw], in1=pb[6][:, 0:qw]))
                V.wait_ge(VE.sem, vd2)
                fq_ve[(bi, cc)] = vcustom(
                    OPS["ANT_FQ8MOD2"], out=fqh2[:, cc * T_own + qrel: cc * T_own + qrel + qw],
                    in0=stg1[:, 0:qw], in1=pb[7][:, 0:qw],
                    s0=cbP1[:, cc:cc + 1], s1=sb_c[:, cc:cc + 1])
            prev_d1_ve = fq_ve[(bi, 7)]

        te_d1_end = TE.v
        for (g0t, gw) in _blocks(own_lo, own_hi, GW):
            grel = g0t - own_lo
            subs = _blocks(0, gw)
            ns = len(subs)
            TN.wait_ge(VE.sem, VE.v)
            TN.wait_ge(SC.sem, SC.v)
            DR = mybir.MatmulPerfMode.DoubleRow
            w1q3 = w1q.rearrange("p (c n) -> p c n", c=8)
            fqh23 = fqh2.rearrange("p (c t) -> p c t", c=8)
            h1_ve = {}
            for ncx in range(32):
                for j in range(4):
                    TN.ldweights(w1q3[:, 2 * j:2 * j + 2, ncx * 128:(ncx + 1) * 128], perf_mode=DR)
                    for sbi, (sb0, sw) in enumerate(subs):
                        if j == 0 and h1_ve.get((ncx - 2, sbi)) is not None:
                            TN.wait_ge(VE.sem, h1_ve[(ncx - 2, sbi)])
                        ti = TN.matmul(pb[(ncx % 2) * 4 + sbi][:, 0:sw],
                                       w1q3[:, 2 * j:2 * j + 2, ncx * 128:(ncx + 1) * 128],
                                       fqh23[:, 2 * j:2 * j + 2, grel + sb0: grel + sb0 + sw],
                                       start=(j == 0), stop=(j == 3), perf_mode=DR)
                tm1 = TE.inc(ti)
                for sbi, (sb0, sw) in enumerate(subs):
                    bst_ = bstg1 if sbi % 2 == 0 else bstg2
                    S.wait_ge(TE.sem, tm1)
                    if bguard.get(sbi % 2) is not None:
                        S.wait_ge(VE.sem, bguard[sbi % 2])
                    srl = SC.inc(S.activation(out=bst_[:, 0:sw], in_=pb[(ncx % 2) * 4 + sbi][:, 0:sw],
                                              func=AF.Relu, bias=fqb1c[:, ncx:ncx + 1]))
                    V.wait_ge(SC.sem, srl)
                    h1_ve[(ncx, sbi)] = vcustom(
                        OPS["ANT_FQ8"], out=fqh1[:, ncx * GW + sb0: ncx * GW + sb0 + sw],
                        in0=bst_[:, 0:sw])
                    bguard[sbi % 2] = h1_ve[(ncx, sbi)]
            y2_ve = {}
            st_guard = {}
            mk_guard = {}
            w2q3 = w2q.rearrange("p (x c) -> p x c", x=32)
            fqh13 = fqh1.rearrange("p (x g) -> p x g", x=32)
            if ns < 4:
                TN.ldweights(permsw[:])
                for _ in range(8):
                    TN.matmul(pb[7][:, 0:512], permsw[:], qmask[:, 0:512],
                              start=True, stop=True)
            for cc in range(8):
                for j in range(16):
                    TN.ldweights(w2q3[:, 2 * j:2 * j + 2, cc * 128:(cc + 1) * 128], perf_mode=DR)
                    for sbi, (sb0, sw) in enumerate(subs):
                        if j == 0:
                            if cc == 0:
                                TN.wait_ge(VE.sem, h1_ve[(31, sbi)])
                            elif y2_ve.get(cc - 1) is not None:
                                TN.wait_ge(VE.sem, y2_ve[cc - 1])
                        ti2 = TN.matmul(pb[sbi][:, 0:sw],
                                        w2q3[:, 2 * j:2 * j + 2, cc * 128:(cc + 1) * 128],
                                        fqh13[:, 2 * j:2 * j + 2, sb0: sb0 + sw],
                                        start=(j == 0), stop=(j == 15), perf_mode=DR)
                tm2 = TE.inc(ti2)
                for sbi, (sb0, sw) in enumerate(subs):
                    ostg = stg1 if sbi % 2 == 0 else stg2
                    msk = xsq1 if sbi % 2 == 0 else xsq2
                    SY.wait_ge(TE.sem, te_d1_end)
                    if mk_guard.get(sbi % 2) is not None:
                        SY.wait_ge(VE.sem, mk_guard[sbi % 2])
                    LD.inc(SY.dma_start(out=msk[:, 0:sw],
                                        in_=d_maskz[cc * 128:(cc + 1) * 128,
                                                    grel + sb0:grel + sb0 + sw]), 16)
                    dmk = syfence()
                    bst_ = bstg1 if sbi % 2 == 0 else bstg2
                    V.wait_ge(TE.sem, tm2)
                    V.wait_ge(LD.sem, dmk)
                    if ST.v > 0:
                        V.wait_ge(ST.sem, ST.v)
                    if bguard.get(sbi % 2) is not None:
                        V.wait_ge(VE.sem, bguard[sbi % 2])
                    vcustom(OPS["ANT_RM"], out=bst_[:, 0:sw], in0=pb[sbi][:, 0:sw],
                            s0=fqb2c[:, cc:cc + 1])
                    vmm = VE.inc(V.tensor_mul(out=msk[:, 0:sw], in0=bst_[:, 0:sw], in1=msk[:, 0:sw]))
                    bguard[sbi % 2] = vmm
                    V.wait_ge(VE.sem, vmm)
                    yv = VE.inc(V.scalar_tensor_tensor(
                        out=ostg[:, 0:sw], in0=msk[:, 0:sw], scalar=gb_c[:, cc:cc + 1],
                        in1=x_own[:, cc * T_own + grel + sb0: cc * T_own + grel + sb0 + sw],
                        op0=ALU.mult, op1=ALU.add))
                    mk_guard[sbi % 2] = yv
                    SY.wait_ge(VE.sem, yv)
                    st_guard[sbi % 2] = ST.inc(
                        SY.dma_start(out=d_out[cc * 128:(cc + 1) * 128, grel + sb0: grel + sb0 + sw],
                                     in_=ostg[:, 0:sw]), 16)
                y2_ve[cc] = yv

        dstk.close()
        bst.close()
        for eng in ALL_ENG:
            eng.wait_ge(ST.sem, ST.v)

    prev_cast_guard = {}
    idxs = [eng.compute_val(eng.partition_id() // 4) for eng in ALL_ENG]
    for i in nc.Switch(engines=ALL_ENG, index=idxs, n=2):
        for s_, v_ in prologue_vals:
            s_.v = v_
        prev_cast_guard = {}
        emit_body(i)

    top.close()
    mybir.codegen_inst_isa_subclasses(nc)
    return nc



def kernel(L0_attn_x, L0_attn_Wq, L0_attn_bq, L0_attn_Wk, L0_attn_bk,
           L0_attn_Wv, L0_attn_bv, L0_attn_Wp, L0_attn_bp,
           L0_ffn_W1, L0_ffn_b1, L0_ffn_W2, L0_ffn_b2, L0_c, L0_wmod):
    _register_ops()
    f32 = np.float32
    xT = np.ascontiguousarray(np.asarray(L0_attn_x, f32).transpose(0, 2, 1))
    W1T = np.ascontiguousarray(np.asarray(L0_ffn_W1, f32).T)
    W2T = np.ascontiguousarray(np.asarray(L0_ffn_W2, f32).T)
    cosT, sinT = _rope_tables()
    qm = np.ascontiguousarray(_boundary_masks().reshape(4 * 128, 512)).astype(ml_dtypes.bfloat16)
    perm = np.zeros((128, 128), np.float32)
    idx = np.arange(128)
    perm[idx, idx ^ 1] = 1.0
    perm = perm.astype(ml_dtypes.bfloat16)
    keep = _threefry_keep_mask()
    maskzT = np.ascontiguousarray(
        (keep.astype(np.float32) * np.float32(1.25)).astype(ml_dtypes.bfloat16).transpose(0, 2, 1))
    ccols = [np.ascontiguousarray(np.asarray(L0_c[b], f32).reshape(8, 128).T) for b in range(B)]
    b1col = np.ascontiguousarray(np.asarray(L0_ffn_b1, f32).reshape(32, 128).T)
    b2col = np.ascontiguousarray(np.asarray(L0_ffn_b2, f32).reshape(8, 128).T)
    row = lambda v: np.ascontiguousarray(np.asarray(v, f32).reshape(1, -1))

    in_maps = []
    for i in range(8):
        b, s = i % 4, i // 4
        lo, hi = (0, SPLIT) if s == 0 else (SPLIT, T)
        mz = np.zeros((C, SPLIT), ml_dtypes.bfloat16)
        mz[:, 0:hi - lo] = maskzT[b][:, lo:hi]
        in_maps.append({
            "xT": xT[b], "cosT": cosT, "sinT": sinT, "qmask4": qm, "maskz": mz,
            "permswap": perm,
            "eye128": np.eye(128, dtype=np.float32).astype(ml_dtypes.bfloat16),
            "tri128": np.where(np.arange(128)[:, None] < np.arange(128)[None, :],
                               np.float32(NEG), np.float32(0.0)).astype(ml_dtypes.bfloat16),
            "Wq": np.asarray(L0_attn_Wq, f32), "Wk": np.asarray(L0_attn_Wk, f32),
            "Wv": np.asarray(L0_attn_Wv, f32), "Wp": np.asarray(L0_attn_Wp, f32),
            "W1T": W1T, "W2T": W2T, "wmod": np.asarray(L0_wmod, f32),
            "ccol": ccols[b],
            "bq_row": row(L0_attn_bq), "bk_row": row(L0_attn_bk),
            "bv_row": row(L0_attn_bv), "bp_row": row(L0_attn_bp),
            "b1col": b1col, "b2col": b2col,
        })

    nc = build_program()
    res = run_bass_kernel_spmd(nc, in_maps, core_ids=list(range(8)))
    out = np.empty((B, T, C), np.float32)
    for i in range(8):
        b, s = i % 4, i // 4
        lo, hi = (0, SPLIT) if s == 0 else (SPLIT, T)
        out[b, lo:hi, :] = res.results[i]["outT"][:, 0:hi - lo].T
    return out


# version 17 (final)
# speedup vs baseline: 1.0137x; 1.0137x over previous
import numpy as np
import ml_dtypes
from contextlib import ExitStack

import concourse.bass as bass
import concourse.mybir as mybir
from concourse.bass_utils import run_bass_kernel_spmd
import concourse.dve_ops as dve_ops
from concourse.dve_op import DveOp
from concourse.dve_spec import Spec, Src0, Src1, C0, C1, Zero, maxx, lower

F32 = mybir.dt.float32
BF16 = mybir.dt.bfloat16
FP8 = mybir.dt.float8e5
AF = mybir.ActivationFunctionType
ALU = mybir.AluOpType
AX = mybir.AxisListType

B, T, C, NH, HS = 4, 4096, 1024, 16, 64
NPAIR = 8
SPLIT = 2432
SWAPM = [(i + 1) if i % 2 == 0 else (i - 1) for i in range(32)]
NEG = -8.0e10
LN_EPS = 1e-5
LNQ = float(np.log(0.125))


def _threefry_keep_mask():
    n = B * T * C
    rot = ((13, 15, 26, 6), (17, 29, 16, 24))
    k0, k1 = np.uint32(0), np.uint32(42)
    ks = (k0, k1, k0 ^ k1 ^ np.uint32(0x1BD11BDA))
    x0 = np.zeros(n, np.uint32)
    x1 = np.arange(n, dtype=np.uint32)
    x0 += ks[0]; x1 += ks[1]
    for i in range(5):
        r = rot[i % 2]
        for j in range(4):
            x0 += x1
            x1 = (x1 << np.uint32(r[j])) | (x1 >> np.uint32(32 - r[j]))
            x1 ^= x0
        x0 += ks[(i + 1) % 3]; x1 += ks[(i + 2) % 3]
        x1 += np.uint32(i + 1)
    bits = x0 ^ x1
    u = ((bits >> np.uint32(9)) | np.uint32(0x3F800000)).view(np.float32) - np.float32(1.0)
    return (u < np.float32(0.8)).reshape(B, T, C)


def _rope_tables():
    half = HS // 2
    inv_freq = (np.float32(1.0) / np.float32(10000.0) **
                (np.arange(half, dtype=np.float32) * np.float32(2.0) / np.float32(HS))).astype(np.float32)
    ang = (np.arange(T, dtype=np.float32)[:, None] * inv_freq[None, :]).astype(np.float32)
    cos = np.cos(ang).astype(np.float32)
    sin = np.sin(ang).astype(np.float32)
    Cfm = np.empty((HS, T), np.float32)
    Sfm = np.empty((HS, T), np.float32)
    Cfm[0::2, :] = cos.T
    Cfm[1::2, :] = cos.T
    Sfm[0::2, :] = sin.T
    Sfm[1::2, :] = -sin.T
    return np.vstack([Cfm, Cfm]), np.vstack([Sfm, Sfm])


def _boundary_masks():
    out = np.zeros((4, 128, 512), np.float32)
    tri = np.where(np.arange(128)[:, None] <= np.arange(128)[None, :], 0.0, NEG).astype(np.float32)
    for j in range(4):
        for jq in range(4):
            blk = out[j][:, 128 * jq:128 * (jq + 1)]
            if jq < j:
                blk[:] = NEG
            elif jq == j:
                blk[:] = tri
    return out



def _fq_exact(x):
    x = np.asarray(x)
    dt = x.dtype
    ax = np.abs(x)
    with np.errstate(divide="ignore"):
        e = np.floor(np.log2(np.maximum(ax, np.array(1e-38, dt))))
    e = np.maximum(e, np.array(-14.0, dt))
    scale = np.exp2(e - 2)
    q = np.round(x / scale) * scale
    return np.where(ax == 0, np.zeros_like(x), q).astype(dt)


def _bf16_round_exact(x):
    x = np.asarray(x)
    dt = x.dtype
    ax = np.abs(x)
    with np.errstate(divide="ignore"):
        e = np.floor(np.log2(np.maximum(ax, np.array(1e-38, dt))))
    e = np.maximum(e, np.array(-126.0, dt))
    scale = np.exp2(e - 7)
    q = np.round(x / scale) * scale
    return np.where(ax == 0, x, q).astype(dt)


_OPS = {}


def _register_ops():
    if _OPS:
        return _OPS

    def mk(name, spec, ref):
        sha = {"v3": lower(spec, ver="v3", perf_en=False, name=name).sha("v3")}
        return DveOp(name, spec, subdim=False, uops_sha=sha, reference=ref)

    ops = [
        mk("ANT_FQ8", Spec(body=Src0 + Zero),
           lambda in0, in1, s0, s1, imm2: _fq_exact(in0)),
        mk("ANT_FQ8MOD", Spec(body=Src0 * C0 + C1),
           lambda in0, in1, s0, s1, imm2: _fq_exact(in0 * s0 + s1)),
        mk("ANT_RM", Spec(body=Src0 + C0),
           lambda in0, in1, s0, s1, imm2: _bf16_round_exact(in0 + s0)),
        mk("ANT_FQ8MOD2", Spec(body=(Src0 - Src1) * C0 + C1),
           lambda in0, in1, s0, s1, imm2: _fq_exact((in0 - in1) * s0 + s1)),
    ]
    for i, op in enumerate(ops):
        old = dve_ops.OPS[i]
        if old.name != op.name:
            dve_ops.OPS[i] = op
            dve_ops._SUB_OPCODE_FOR_NAME[op.name] = dve_ops._SUB_OPCODE_FOR_NAME.pop(old.name)
            dve_ops.CUSTOM_DVE_REFS.pop(old.name, None)
            dve_ops.CUSTOM_DVE_SPECS.pop(old.name, None)
            dve_ops.CUSTOM_DVE_REFS[op.name] = op.ref
            dve_ops.CUSTOM_DVE_SPECS[op.name] = op.spec
        _OPS[op.name] = op
    return _OPS


def _blocks(lo, hi, step=512):
    out, t = [], lo
    while t < hi:
        w = min(step, hi - t)
        out.append((t, w))
        t += w
    return out


class Ctr:
    def __init__(self, sem):
        self.sem = sem
        self.v = 0

    def inc(self, inst, n=1):
        inst.then_inc(self.sem, n)
        self.v += n
        return self.v



def build_program(dbg=False):
    OPS = _register_ops()
    nc = bass.Bass("TRN2", target_bir_lowering=False)
    top = ExitStack()
    sb = lambda n, shape, d: top.enter_context(nc.sbuf_tensor(n, shape, d))
    V, S, TN, G, SY = nc.vector, nc.scalar, nc.tensor, nc.gpsimd, nc.sync
    ALL_ENG = [SY, V, S, TN, G]
    TOmax = SPLIT

    ap = lambda h: h.ap()
    d_xT = ap(nc.dram_tensor("xT", [C, T], F32, kind="ExternalInput"))
    d_cos = ap(nc.dram_tensor("cosT", [128, T], F32, kind="ExternalInput"))
    d_sin = ap(nc.dram_tensor("sinT", [128, T], F32, kind="ExternalInput"))
    d_qm = ap(nc.dram_tensor("qmask4", [4 * 128, 512], BF16, kind="ExternalInput"))
    d_perm = ap(nc.dram_tensor("permswap", [128, 128], BF16, kind="ExternalInput"))
    d_eye = ap(nc.dram_tensor("eye128", [128, 128], BF16, kind="ExternalInput"))
    d_tri = ap(nc.dram_tensor("tri128", [128, 128], BF16, kind="ExternalInput"))
    d_maskz = ap(nc.dram_tensor("maskz", [C, TOmax], BF16, kind="ExternalInput"))
    d_Wq = ap(nc.dram_tensor("Wq", [C, C], F32, kind="ExternalInput"))
    d_Wk = ap(nc.dram_tensor("Wk", [C, C], F32, kind="ExternalInput"))
    d_Wv = ap(nc.dram_tensor("Wv", [C, C], F32, kind="ExternalInput"))
    d_Wp = ap(nc.dram_tensor("Wp", [C, C], F32, kind="ExternalInput"))
    d_W1T = ap(nc.dram_tensor("W1T", [C, 4 * C], F32, kind="ExternalInput"))
    d_W2T = ap(nc.dram_tensor("W2T", [4 * C, C], F32, kind="ExternalInput"))
    d_wmod = ap(nc.dram_tensor("wmod", [C, 6 * C], F32, kind="ExternalInput"))
    d_ccol = ap(nc.dram_tensor("ccol", [128, 8], F32, kind="ExternalInput"))
    d_bqr = ap(nc.dram_tensor("bq_row", [1, C], F32, kind="ExternalInput"))
    d_bkr = ap(nc.dram_tensor("bk_row", [1, C], F32, kind="ExternalInput"))
    d_bvr = ap(nc.dram_tensor("bv_row", [1, C], F32, kind="ExternalInput"))
    d_bpr = ap(nc.dram_tensor("bp_row", [1, C], F32, kind="ExternalInput"))
    d_b1c = ap(nc.dram_tensor("b1col", [128, 32], F32, kind="ExternalInput"))
    d_b2c = ap(nc.dram_tensor("b2col", [128, 8], F32, kind="ExternalInput"))
    d_out = ap(nc.dram_tensor("outT", [C, TOmax], F32, kind="ExternalOutput"))
    sk = "ExternalOutput" if dbg else "Internal"
    d_kd = ap(nc.dram_tensor("kd_s", [128, NPAIR * T], BF16, kind=sk))
    d_vd = ap(nc.dram_tensor("vd_s", [128, NPAIR * T], BF16, kind=sk))
    d_qd = ap(nc.dram_tensor("qd_s", [128, NPAIR * TOmax], BF16, kind=sk))
    d_w1q8 = ap(nc.dram_tensor("w1q8_s", [C, 4 * C], FP8, kind="Internal"))
    d_w2q8 = ap(nc.dram_tensor("w2q8_s", [4 * C, C], FP8, kind="Internal"))
    if dbg:
        d_dmod = ap(nc.dram_tensor("dmod", [128, 40], F32, kind="ExternalOutput"))
        d_dsa = ap(nc.dram_tensor("dsa", [128, 8], BF16, kind="ExternalOutput"))
        d_dxA = ap(nc.dram_tensor("dxA", [128, 8 * TOmax], BF16, kind="ExternalOutput"))
        d_dxB = ap(nc.dram_tensor("dxB", [128, 8 * TOmax], BF16, kind="ExternalOutput"))
        d_dob = ap(nc.dram_tensor("dob", [128, 8 * TOmax], BF16, kind="ExternalOutput"))
        d_dx2 = ap(nc.dram_tensor("dx2", [128, 8 * TOmax], BF16, kind="ExternalOutput"))

    sem = lambda n: top.enter_context(nc.semaphore(n))
    LD = Ctr(sem("LD"))
    GP = Ctr(sem("GP"))
    VD = Ctr(sem("VD"))
    SD = Ctr(sem("SD"))
    VE = Ctr(sem("VE"))
    SC = Ctr(sem("SC"))
    TE = Ctr(sem("TE"))
    ST = Ctr(sem("ST"))
    SEMS = [LD, GP, VD, SD, VE, SC, TE, ST]

    qmask = sb("qmaskS", [128, 4 * 512], BF16)
    rowf = sb("rowf", [1, 4 * 512], F32)
    rowb = sb("rowb", [1, 4 * 512], BF16)
    stgA = sb("stgA", [128, 2048], F32)
    stgB = sb("stgB", [128, 2048], F32)
    stg1 = sb("stg1", [128, 512], F32)
    stg2 = sb("stg2", [128, 512], F32)
    stg3 = sb("stg3", [128, 512], F32)
    bstg1 = sb("bstg1", [128, 512], BF16)
    bstg2 = sb("bstg2", [128, 512], BF16)
    xsq1 = sb("xsq1", [128, 512], BF16)
    xsq2 = sb("xsq2", [128, 512], BF16)
    bprow = sb("bprow", [1, C], BF16)
    caP1 = sb("caP1", [128, 8], F32)
    cbP1 = sb("cbP1", [128, 8], F32)
    sb_c = sb("sb_c", [128, 8], F32)
    ga_c = sb("ga_c", [128, 8], F32)
    gb_c = sb("gb_c", [128, 8], F32)
    sa_cb = sb("sa_cb", [128, 8], BF16)
    silu_cb = sb("silu_cb", [128, 8], BF16)
    fqb1c = sb("fqb1c", [128, 32], F32)
    fqb2c = sb("fqb2c", [128, 8], F32)
    f8tmp = sb("f8tmp", [128, 32], FP8)
    ones1c = sb("ones1c", [128, 1], BF16)
    oner_lhs = sb("oner_lhs", [1, 128], BF16)
    onesr = sb("onesr", [1, 512], BF16)
    ones128 = sb("ones128", [128, 128], BF16)
    silu_lo = sb("silu_lo", [128, 8], BF16)
    permsw = sb("permsw", [128, 128], BF16)
    eyeb = sb("eyeb", [128, 128], BF16)
    trib = sb("trib", [128, 128], BF16)
    negrow = sb("negrow", [1, 128], BF16)
    r_col = sb("r_col", [128, 32], F32)
    negBall = sb("negBall", [128, 8], F32)
    kmax2 = sb("kmax2", [1, 32], F32)
    big0 = top.enter_context(nc.psum_tensor("big0", [128, 1024], F32))
    big1 = top.enter_context(nc.psum_tensor("big1", [128, 1024], F32))
    p4 = top.enter_context(nc.psum_tensor("p4", [128, 512], F32))
    p5 = top.enter_context(nc.psum_tensor("p5", [128, 512], F32))
    p6 = top.enter_context(nc.psum_tensor("p6", [128, 512], F32))
    p7 = top.enter_context(nc.psum_tensor("p7", [128, 512], F32))
    pb = [big0[:, 0:512], big0[:, 512:1024], big1[:, 0:512], big1[:, 512:1024],
          p4[:], p5[:], p6[:], p7[:]]

    def syfence():
        SY.wait_ge(LD.sem, LD.v)
        return LD.v

    def gfence():
        G.wait_ge(GP.sem, GP.v)
        return GP.v

    def phase_barrier():
        for eng in ALL_ENG:
            for ctr in (VE, SC, TE, GP, SD, LD, VD):
                if ctr.v > 0:
                    eng.wait_ge(ctr.sem, ctr.v)

    def vcustom(op, **kw):
        v = VE.inc(V._custom_dve(op, **kw))
        V.wait_ge(VE.sem, v)
        return v

    def dbg_dump(pairs):
        if not dbg:
            return
        for dst, src in pairs:
            LD.inc(SY.dma_start(out=dst, in_=src), 16)
        dv = syfence()
        for eng in ALL_ENG:
            eng.wait_ge(LD.sem, dv)

    mrow = rowf[0:1, 0:512]
    srow = rowf[0:1, 512:1024]
    lrow = rowf[0:1, 1024:1536]
    rrow = rowf[0:1, 1536:2048]
    rbf = rowb[0:1, 0:512]
    mr2b = rowb[0:1, 512:1024]
    scr_b = rowb[0:1, 1024:1536]
    scr_c = rowb[0:1, 1536:2048]

    V.memset(ones1c[:], 1.0)
    V.memset(oner_lhs[:], 1.0)
    V.memset(onesr[:], 1.0)
    V.memset(ones128[:], 1.0)
    cpool = sb("cpool", [128, 4], F32)
    V.memset(cpool[:, 0:1], LN_EPS)
    V.memset(cpool[:, 1:2], 1.0)
    ve_consts = VE.inc(V.memset(cpool[:, 2:3], LNQ))
    nc.const_aps.register(F32, LN_EPS, cpool[:, 0:1])
    nc.const_aps.register(F32, 1.0, cpool[:, 1:2])
    nc.const_aps.register(F32, LNQ, cpool[:, 2:3])
    S.wait_ge(VE.sem, ve_consts)
    TN.wait_ge(VE.sem, ve_consts)

    for j in range(4):
        LD.inc(SY.dma_start(out=qmask[:, j * 512:(j + 1) * 512], in_=d_qm[j * 128:(j + 1) * 128, :]), 16)
    LD.inc(SY.dma_start(out=permsw[:], in_=d_perm), 16)
    LD.inc(SY.dma_start(out=eyeb[:], in_=d_eye), 16)
    LD.inc(SY.dma_start(out=trib[:], in_=d_tri), 16)
    V.memset(negrow[:], NEG)
    LD.inc(SY.dma_start(out=stg1[:, 0:8], in_=d_ccol), 16)
    LD.inc(SY.dma_start(out=stg2[:, 0:32], in_=d_b1c), 16)
    LD.inc(SY.dma_start(out=stg2[:, 32:40], in_=d_b2c), 16)
    LD.inc(SY.dma_start(out=stgA[0:1, 1024:2048], in_=d_bpr), 16)
    ld0 = syfence()

    V.wait_ge(LD.sem, ld0)
    vcustom(OPS["ANT_FQ8"], out=f8tmp[:, 0:32], in0=stg2[:, 0:32])
    vcp = VE.inc(V.tensor_copy(out=fqb1c[:], in_=f8tmp[:, 0:32]))
    V.wait_ge(VE.sem, vcp)
    vcustom(OPS["ANT_FQ8"], out=f8tmp[:, 0:8], in0=stg2[:, 32:40])
    V.tensor_copy(out=fqb2c[:], in_=f8tmp[:, 0:8])
    ve_bp = VE.inc(V.tensor_copy(out=bprow[0:1, 0:C], in_=stgA[0:1, 1024:2048]))

    S.wait_ge(LD.sem, ld0)
    sc_si = SC.inc(S.activation(out=stg1[:, 8:16], in_=stg1[:, 0:8], func=AF.Silu))
    V.wait_ge(SC.sem, sc_si)
    ve_si0 = VE.inc(V.tensor_copy(out=silu_cb[:], in_=stg1[:, 8:16]))
    V.wait_ge(VE.sem, ve_si0)
    ve_si = VE.inc(V.tensor_sub(out=silu_lo[:], in0=stg1[:, 8:16], in1=silu_cb[:]))

    mod_te = {}
    mod_sc = {}
    for mb in range(12):
        bank = pb[mb % 2]
        for cc in range(8):
            k = mb * 8 + cc
            stgw = stgA if k % 2 == 0 else stgB
            bw = bstg1 if k % 2 == 0 else bstg2
            if k % 2 == 0:
                if k >= 2:
                    SY.wait_ge(TE.sem, mod_te[k - 2])
                LD.inc(SY.dma_start(out=stgw[:, 0:512],
                                    in_=d_wmod[cc * 128:(cc + 1) * 128, mb * 512:(mb + 1) * 512]), 16)
                dv = syfence()
                V.wait_ge(LD.sem, dv)
            else:
                if k >= 2:
                    S.wait_ge(TE.sem, mod_te[k - 2])
                SD.inc(S.dma_start(out=stgw[:, 0:512],
                                   in_=d_wmod[cc * 128:(cc + 1) * 128, mb * 512:(mb + 1) * 512]), 16)
                S.wait_ge(SD.sem, SD.v)
                V.wait_ge(SD.sem, SD.v)
            if k >= 2:
                V.wait_ge(TE.sem, mod_te[k - 2])
            gmb = mb in (4, 5, 10, 11)
            cv = VE.inc(V.tensor_copy(out=bw[:, 0:512], in_=stgw[:, 0:512]))
            if gmb:
                bwlo = xsq1 if k % 2 == 0 else xsq2
                V.wait_ge(VE.sem, cv)
                cv = VE.inc(V.tensor_sub(out=bwlo[:, 0:512], in0=stgw[:, 0:512], in1=bw[:, 0:512]))
            TN.wait_ge(VE.sem, cv)
            TN.wait_ge(VE.sem, ve_si)
            if cc == 0 and mb >= 2:
                TN.wait_ge(SC.sem, mod_sc[mb - 2])
            TN.ldweights(silu_cb[:, cc:cc + 1])
            if gmb:
                TN.matmul(bank[0:1, 0:512], silu_cb[:, cc:cc + 1], bw[:, 0:512],
                          start=(cc == 0), stop=False)
                TN.matmul(bank[0:1, 0:512], silu_cb[:, cc:cc + 1], bwlo[:, 0:512],
                          start=False, stop=False)
                TN.ldweights(silu_lo[:, cc:cc + 1])
                mod_te[k] = TE.inc(TN.matmul(bank[0:1, 0:512], silu_lo[:, cc:cc + 1], bw[:, 0:512],
                                             start=False, stop=(cc == 7)))
            else:
                mod_te[k] = TE.inc(TN.matmul(bank[0:1, 0:512], silu_cb[:, cc:cc + 1], bw[:, 0:512],
                                             start=(cc == 0), stop=(cc == 7)))
        S.wait_ge(TE.sem, mod_te[mb * 8 + 7])
        mod_sc[mb] = SC.inc(S.copy(out=rowf[0:1, (mb % 2) * 512:(mb % 2) * 512 + 512],
                                   in_=bank[0:1, 0:512]))
        if mb % 2 == 1:
            vecidx = mb // 2
            modr = rowf[0:1, 0:1024]
            V.wait_ge(SC.sem, mod_sc[mb])
            if vecidx in (2, 5):
                scs = SC.inc(S.activation(out=stg3[0:1, 0:512], in_=modr[0:1, 0:512], func=AF.Sign))
                scs2 = SC.inc(S.activation(out=rowf[0:1, 1536:2048], in_=modr[0:1, 512:1024], func=AF.Sign))
                v_ = VE.inc(V.tensor_scalar_mul(out=stg1[0:1, 0:512], in0=modr[0:1, 0:512], scalar1=-1.0))
                v_ = VE.inc(V.tensor_scalar_mul(out=stg2[0:1, 0:512], in0=modr[0:1, 512:1024], scalar1=-1.0))
                V.wait_ge(VE.sem, v_)
                v_ = VE.inc(V.tensor_max(out=stg1[0:1, 0:512], in0=stg1[0:1, 0:512], in1=modr[0:1, 0:512]))
                v_ = VE.inc(V.tensor_max(out=stg2[0:1, 0:512], in0=stg2[0:1, 0:512], in1=modr[0:1, 512:1024]))
                V.wait_ge(VE.sem, v_)
                v_ = VE.inc(V.tensor_scalar_max(out=stg1[0:1, 0:512], in0=stg1[0:1, 0:512], scalar1=0.25))
                v_ = VE.inc(V.tensor_scalar_max(out=stg2[0:1, 0:512], in0=stg2[0:1, 0:512], scalar1=0.25))
                V.wait_ge(VE.sem, v_)
                V.wait_ge(SC.sem, scs2)
                v_ = VE.inc(V.tensor_mul(out=modr[0:1, 0:512], in0=stg1[0:1, 0:512], in1=stg3[0:1, 0:512]))
                v_ = VE.inc(V.tensor_mul(out=modr[0:1, 512:1024], in0=stg2[0:1, 0:512], in1=rowf[0:1, 1536:2048]))
                V.wait_ge(VE.sem, v_)
            vrow = VE.inc(V.tensor_copy(out=rowb[0:1, 0:1024], in_=modr[0:1, 0:1024]))
            TN.wait_ge(VE.sem, vrow)
            bankc = pb[2]
            if vecidx >= 1:
                TN.wait_ge(SC.sem, colsc)
            for j in range(8):
                TN.ldweights(rowb[0:1, j * 128:(j + 1) * 128])
                tcol = TE.inc(TN.matmul(bankc[:, j:j + 1], rowb[0:1, j * 128:(j + 1) * 128],
                                        ones1c[0:1, 0:1], start=True, stop=True))
            S.wait_ge(TE.sem, tcol)
            dst = (sa_cb, caP1, ga_c, sb_c, cbP1, gb_c)[vecidx]
            if vecidx in (1, 4):
                colsc = SC.inc(S.activation(out=dst[:, 0:8], in_=bankc[:, 0:8], func=AF.Copy, bias=1.0))
            else:
                colsc = SC.inc(S.copy(out=dst[:, 0:8], in_=bankc[:, 0:8]))
    prol_colsc = colsc

    prologue = {s_.sem.name: s_.v for s_ in SEMS}
    prologue_vals = [(s_, s_.v) for s_ in SEMS]

    def emit_body(shalf):
        own_lo, own_hi = (0, SPLIT) if shalf == 0 else (SPLIT, T)
        kv_hi = SPLIT if shalf == 0 else T
        T_own = own_hi - own_lo
        T_oth = own_lo
        KT = kv_hi // 128

        for eng in (SY, V, S, TN):
            for s_, v_ in prologue_vals:
                if v_ > 0:
                    eng.wait_ge(s_.sem, v_)

        bst = ExitStack()
        sbb = lambda n, shape, d: bst.enter_context(nc.sbuf_tensor(f"{n}_{shalf}", shape, d))
        x_own = sbb("x_own", [128, 8 * T_own], BF16)

        def xslice(t0, w, cc):
            for (xt, glo, ghi, width, goff) in regions:
                if glo <= t0 and t0 + w <= ghi:
                    return xt[:, cc * width + (t0 - goff): cc * width + (t0 - goff) + w]
            raise AssertionError((t0, w))

        astk = ExitStack()
        sba = lambda n, shape, d: astk.enter_context(nc.sbuf_tensor(f"{n}_{shalf}", shape, d))
        regions = []
        if T_oth:
            x_oth = sba("x_oth", [128, 8 * T_oth], BF16)
            regions.append((x_oth, 0, T_oth, T_oth, 0))
        regions.append((x_own, own_lo, own_hi, T_own, own_lo))
        wq_bf = sba("wq_bf", [128, 8 * C], BF16)
        wk_bf = sba("wk_bf", [128, 8 * C], BF16)
        wv_bf = sba("wv_bf", [128, 8 * C], BF16)
        c_r = sba("c_r", [128, T], BF16)
        s_r = sba("s_r", [128, T], BF16)
        mstd2 = sba("mstd2", [2, T], BF16)
        corr_q = sba("corr_q", [2, C], BF16)
        corr_k = sba("corr_k", [2, C], BF16)
        corr_v = sba("corr_v", [2, C], BF16)

        def w_load(d_W, w_bf):
            for cc in range(8):
                GP.inc(G.dma_start(out=w_bf[:, cc * C:(cc + 1) * C],
                                   in_=d_W[cc * 128:(cc + 1) * 128, :]), 16)
            return gfence()

        def w_prep(w_bf, gws, corr, brow_src):
            ve_ch = {}
            V.wait_ge(GP.sem, gws)
            for cc in range(8):
                ve_ch[cc] = VE.inc(V.tensor_scalar(out=w_bf[:, cc * C:(cc + 1) * C],
                                                   in0=w_bf[:, cc * C:(cc + 1) * C],
                                                   scalar1=caP1[:, cc:cc + 1], scalar2=None, op0=ALU.mult))
            for cc in range(8):
                TN.wait_ge(VE.sem, ve_ch[cc])
                TN.ldweights(ones1c[:])
                TN.matmul(pb[4][0:1, 0:512], ones1c[:], w_bf[:, cc * C:cc * C + 512],
                          start=(cc == 0), stop=(cc == 7))
                tcs = TE.inc(TN.matmul(pb[5][0:1, 0:512], ones1c[:], w_bf[:, cc * C + 512:(cc + 1) * C],
                                       start=(cc == 0), stop=(cc == 7)))
            for cc in range(8):
                TN.ldweights(sa_cb[:, cc:cc + 1])
                TN.matmul(pb[6][0:1, 0:512], sa_cb[:, cc:cc + 1], w_bf[:, cc * C:cc * C + 512],
                          start=(cc == 0), stop=(cc == 7))
                tbp = TE.inc(TN.matmul(pb[7][0:1, 0:512], sa_cb[:, cc:cc + 1], w_bf[:, cc * C + 512:(cc + 1) * C],
                                       start=(cc == 0), stop=(cc == 7)))
            S.wait_ge(TE.sem, tbp)
            S.activation(out=corr[0:1, 0:512], in_=pb[4][0:1, 0:512], func=AF.Copy, scale=-1.0)
            sc1 = SC.inc(S.activation(out=corr[0:1, 512:C], in_=pb[5][0:1, 0:512], func=AF.Copy, scale=-1.0))
            V.wait_ge(TE.sem, tbp)
            V.tensor_add(out=rowb[0:1, 0:512], in0=pb[6][0:1, 0:512], in1=brow_src[0:1, 0:512])
            vb = VE.inc(V.tensor_add(out=rowb[0:1, 512:1024], in0=pb[7][0:1, 0:512], in1=brow_src[0:1, 512:C]))
            G.wait_ge(VE.sem, vb)
            gd = GP.inc(G.dma_start(out=corr[1:2, 0:C], in_=rowb[0:1, 0:1024]), 16)
            V.wait_ge(GP.sem, gd)
            TN.wait_ge(SC.sem, sc1)
            return gd

        gp_guard = {}
        swp_guard = [None, None, None, None]
        accq = pb[5]
        biasr = sba("biasr", [1, 3 * C], F32)
        LD.inc(SY.dma_start(out=biasr[0:1, 0:C], in_=d_bqr), 16)
        LD.inc(SY.dma_start(out=biasr[0:1, C:2 * C], in_=d_bkr), 16)
        LD.inc(SY.dma_start(out=biasr[0:1, 2 * C:3 * C], in_=d_bvr), 16)
        ldb = syfence()
        V.wait_ge(LD.sem, ldb)
        xblk = [(xt, width, goff, t0, w)
                for (xt, glo, ghi, width, goff) in regions
                for (t0, w) in _blocks(glo, ghi)]
        gxv = {}

        def x_load(n):
            for _ in range(n):
                if not xblk:
                    return
                xt, width, goff, t0, w = xblk.pop(0)
                for cc in range(8):
                    GP.inc(G.dma_start(out=xt[:, cc * width + (t0 - goff): cc * width + (t0 - goff) + w],
                                       in_=d_xT[cc * 128:(cc + 1) * 128, t0:t0 + w]), 16)
                gxv[(id(xt), t0)] = gfence()
        gws_q = w_load(d_Wq, wq_bf)
        x_load(2)
        gws_k = w_load(d_Wk, wk_bf)
        x_load(2)
        gws_v = w_load(d_Wv, wv_bf)
        x_load(99)
        gd_q = w_prep(wq_bf, gws_q, corr_q, biasr[0:1, 0:C])
        gd_k = w_prep(wk_bf, gws_k, corr_k, biasr[0:1, C:2 * C])
        gd_v = w_prep(wv_bf, gws_v, corr_v, biasr[0:1, 2 * C:3 * C])

        last_rcol_sc = None
        last_rope_ve = None
        last_mstd_gp = None
        prev_stat_te = None
        for (xt, glo, ghi, width, goff) in regions:
            for (t0, w) in _blocks(glo, ghi):
                gx_blk = gxv[(id(xt), t0)]
                for cc in range(8):
                    xsq = xsq1 if cc % 2 == 0 else xsq2
                    S.wait_ge(GP.sem, gx_blk)
                    if cc >= 2:
                        S.wait_ge(TE.sem, te_var[cc - 2])
                    elif prev_stat_te is not None:
                        S.wait_ge(TE.sem, prev_stat_te)
                    ssq = SC.inc(S.activation(out=xsq[:, 0:w], in_=xslice(t0, w, cc), func=AF.Square))
                    TN.wait_ge(SC.sem, ssq)
                    TN.ldweights(ones1c[:])
                    TN.matmul(pb[0][0:1, 0:w], ones1c[:], xslice(t0, w, cc),
                              start=(cc == 0), stop=(cc == 7))
                    tv = TE.inc(TN.matmul(pb[1][0:1, 0:w], ones1c[:], xsq[:, 0:w],
                                          start=(cc == 0), stop=(cc == 7)))
                    if cc == 0:
                        te_var = {}
                    te_var[cc] = tv
                prev_stat_te = tv
                S.wait_ge(TE.sem, tv)
                if last_rope_ve is not None:
                    S.wait_ge(VE.sem, last_rope_ve)
                S.activation(out=mrow[0:1, 0:w], in_=pb[0][0:1, 0:w], func=AF.Copy, scale=1.0 / 1024.0)
                scm = SC.inc(S.activation(out=srow[0:1, 0:w], in_=pb[1][0:1, 0:w],
                                          func=AF.Copy, scale=1.0 / 1024.0))
                V.wait_ge(SC.sem, scm)
                vm_ = VE.inc(V.tensor_mul(out=stg1[0:1, 0:w], in0=mrow[0:1, 0:w], in1=mrow[0:1, 0:w]))
                V.wait_ge(VE.sem, vm_)
                vv = VE.inc(V.tensor_sub(out=stg1[0:1, 0:w], in0=srow[0:1, 0:w], in1=stg1[0:1, 0:w]))
                S.wait_ge(VE.sem, vv)
                sl_ = SC.inc(S.activation(out=lrow[0:1, 0:w], in_=stg1[0:1, 0:w], func=AF.Ln, bias=LN_EPS))
                S.wait_ge(SC.sem, sl_)
                S.activation(out=rrow[0:1, 0:w], in_=lrow[0:1, 0:w], func=AF.Exp, scale=-0.5)
                scstd = SC.inc(S.activation(out=srow[0:1, 0:w], in_=lrow[0:1, 0:w], func=AF.Exp, scale=0.5))
                V.wait_ge(SC.sem, scstd)
                V.tensor_copy(out=rbf[0:1, 0:w], in_=rrow[0:1, 0:w])
                V.tensor_copy(out=mstd2[0:1, t0:t0 + w], in_=mrow[0:1, 0:w])
                if last_mstd_gp is not None:
                    V.wait_ge(GP.sem, last_mstd_gp)
                vstd = VE.inc(V.tensor_copy(out=scr_c[0:1, 0:w], in_=srow[0:1, 0:w]))
                G.wait_ge(VE.sem, vstd)
                last_mstd_gp = GP.inc(G.dma_start(out=mstd2[1:2, t0:t0 + w], in_=scr_c[0:1, 0:w]), 16)
                TN.wait_ge(VE.sem, vstd)
                if last_rcol_sc is not None:
                    TN.wait_ge(SC.sem, last_rcol_sc)
                nj = (w + 127) // 128
                for j in range(nj):
                    jw = min(128, w - j * 128)
                    TN.ldweights(rbf[0:1, j * 128: j * 128 + jw])
                    trc = TE.inc(TN.matmul(pb[2][0:jw, j:j + 1], rbf[0:1, j * 128: j * 128 + jw],
                                           ones1c[0:1, 0:1], start=True, stop=True))
                S.wait_ge(TE.sem, trc)
                last_rcol_sc = SC.inc(S.copy(out=r_col[:, t0 // 128: t0 // 128 + nj], in_=pb[2][:, 0:nj]))
                if last_rope_ve is not None:
                    TN.wait_ge(VE.sem, last_rope_ve)
                TN.ldweights(oner_lhs[:])
                trr = TE.inc(TN.matmul(pb[3][:, 0:w], oner_lhs[:], rbf[0:1, 0:w],
                                       start=True, stop=True))
                if last_rope_ve is not None:
                    SY.wait_ge(VE.sem, last_rope_ve)
                LD.inc(SY.dma_start(out=stg2[:, 0:w], in_=d_cos[:, t0:t0 + w]), 16)
                dcs = syfence()
                V.wait_ge(LD.sem, dcs)
                V.wait_ge(TE.sem, trr)
                vcr = VE.inc(V.tensor_mul(out=c_r[:, t0:t0 + w], in0=stg2[:, 0:w], in1=pb[3][:, 0:w]))
                SY.wait_ge(VE.sem, vcr)
                LD.inc(SY.dma_start(out=stg2[:, 0:w], in_=d_sin[:, t0:t0 + w]), 16)
                dsn = syfence()
                V.wait_ge(LD.sem, dsn)
                last_rope_ve = VE.inc(V.tensor_mul(out=s_r[:, t0:t0 + w], in0=stg2[:, 0:w], in1=pb[3][:, 0:w]))

        mstd_gate_gp = GP.v
        def qk_proj(w_bf, corr, dst, dst_stride, rlist):
            for eng in (TN,):
                eng.wait_ge(VE.sem, VE.v)
                eng.wait_ge(SC.sem, SC.v)
                eng.wait_ge(GP.sem, GP.v)
            bank_guard = {}
            seq = 0
            for (xt, glo, ghi, width, goff) in rlist:
                blocks = _blocks(glo, ghi)
                nb = len(blocks)
                for dc in range(8):
                    base = seq
                    for cc in range(8):
                        TN.ldweights(w_bf[:, cc * C + dc * 128: cc * C + (dc + 1) * 128])
                        for bi, (t0, w) in enumerate(blocks):
                            sl = (base + bi) % 8
                            if cc == 0 and bank_guard.get(sl) is not None:
                                TN.wait_ge(VE.sem, bank_guard[sl])
                            TN.matmul(pb[sl][:, 0:w], w_bf[:, cc * C + dc * 128: cc * C + (dc + 1) * 128],
                                      xt[:, cc * width + (t0 - goff): cc * width + (t0 - goff) + w],
                                      start=(cc == 0), stop=False)
                    seq += nb
                    TN.ldweights(corr[:, dc * 128:(dc + 1) * 128])
                    for bi, (t0, w) in enumerate(blocks):
                        sl = (base + bi) % 8
                        tlast = TE.inc(TN.matmul(pb[sl][:, 0:w], corr[:, dc * 128:(dc + 1) * 128],
                                                 mstd2[:, t0:t0 + w], start=False, stop=True))
                        V.wait_ge(TE.sem, tlast)
                        if swp_guard[0] is not None:
                            V.wait_ge(VE.sem, swp_guard[0])
                        vc1 = VE.inc(V.tensor_mul(out=stg1[:, 0:w], in0=pb[sl][:, 0:w], in1=c_r[:, t0:t0 + w]))
                        bw = bstg1 if (dc + bi) % 2 == 0 else bstg2
                        V.wait_ge(VE.sem, vc1)
                        if gp_guard.get(1 - (dc + bi) % 2) is not None:
                            V.wait_ge(GP.sem, gp_guard[1 - (dc + bi) % 2])
                        v2 = VE.inc(V.tensor_mul(out=bw[:, 0:w], in0=pb[sl][:, 0:w], in1=s_r[:, t0:t0 + w]))
                        bank_guard[sl] = v2
                        V.wait_ge(VE.sem, v2)
                        vsw = VE.inc(V.stream_shuffle(out=xsq1[:, 0:w], in_=bw[:, 0:w], mask=SWAPM))
                        V.wait_ge(VE.sem, vsw)
                        if gp_guard.get((dc + bi) % 2) is not None:
                            V.wait_ge(GP.sem, gp_guard[(dc + bi) % 2])
                        bw2 = bstg2 if (dc + bi) % 2 == 0 else bstg1
                        v3 = VE.inc(V.tensor_add(out=bw2[:, 0:w], in0=stg1[:, 0:w], in1=xsq1[:, 0:w]))
                        swp_guard[0] = v3
                        G.wait_ge(VE.sem, v3)
                        gp_guard[(dc + bi) % 2] = GP.inc(
                            G.dma_start(out=dst[:, dc * dst_stride + (t0 - (own_lo if dst is d_qd else 0)):
                                                 dc * dst_stride + (t0 - (own_lo if dst is d_qd else 0)) + w],
                                        in_=bw2[:, 0:w]), 16)

        qk_proj(wq_bf, corr_q, d_qd, TOmax, [regions[-1]])
        qk_proj(wk_bf, corr_k, d_kd, T, regions)

        TN.wait_ge(VE.sem, VE.v)
        TN.wait_ge(SC.sem, SC.v)
        S.wait_ge(GP.sem, GP.v)
        S.wait_ge(VE.sem, VE.v)
        vprev = {}
        for tc in range(KT):
            b0 = pb[(tc % 4) * 2]
            b1_ = pb[(tc % 4) * 2 + 1]
            tglob = tc * 128
            for cc in range(8):
                TN.ldweights(xslice(tglob, 128, cc))
                if cc == 0 and vprev.get(tc - 4) is not None:
                    TN.wait_ge(SC.sem, vprev[tc - 4])
                TN.matmul(b0[:, 0:512], xslice(tglob, 128, cc), wv_bf[:, cc * C: cc * C + 512],
                          start=(cc == 0), stop=False)
                TN.matmul(b1_[:, 0:512], xslice(tglob, 128, cc), wv_bf[:, cc * C + 512:(cc + 1) * C],
                          start=(cc == 0), stop=False)
            TN.ldweights(mstd2[:, tglob:tglob + 128])
            TN.matmul(b0[:, 0:512], mstd2[:, tglob:tglob + 128], corr_v[:, 0:512], start=False, stop=True)
            tv2 = TE.inc(TN.matmul(b1_[:, 0:512], mstd2[:, tglob:tglob + 128], corr_v[:, 512:C],
                                   start=False, stop=True))
            S.wait_ge(TE.sem, tv2)
            if vprev.get(tc - 1) is not None:
                S.wait_ge(GP.sem, vstore[tc - 1])
            S.activation(out=bstg1[:, 0:512], in_=b0[:, 0:512], func=AF.Copy, scale=r_col[:, tc:tc + 1])
            svv = SC.inc(S.activation(out=bstg2[:, 0:512], in_=b1_[:, 0:512], func=AF.Copy,
                                      scale=r_col[:, tc:tc + 1]))
            vprev[tc] = svv
            G.wait_ge(SC.sem, svv)
            for p in range(4):
                GP.inc(G.dma_start(out=d_vd[:, p * T + tglob: p * T + tglob + 128],
                                   in_=bstg1[:, p * 128:(p + 1) * 128]), 16)
            for p in range(3):
                GP.inc(G.dma_start(out=d_vd[:, (4 + p) * T + tglob: (4 + p) * T + tglob + 128],
                                   in_=bstg2[:, p * 128:(p + 1) * 128]), 16)
            gv = GP.inc(G.dma_start(out=d_vd[:, 7 * T + tglob: 7 * T + tglob + 128],
                                    in_=bstg2[:, 3 * 128:4 * 128]), 16)
            if tc == 0:
                vstore = {}
            vstore[tc] = gv

        astk.close()
        phase_barrier()
        if dbg:
            dbg_dump([(d_dmod[:, 0:8], caP1[:]), (d_dmod[:, 8:16], cbP1[:]),
                      (d_dmod[:, 16:24], sb_c[:]), (d_dmod[:, 24:32], ga_c[:]),
                      (d_dmod[:, 32:40], gb_c[:]), (d_dsa, sa_cb[:])])
            dbg_dump([(d_dxA[:, cc * T_own:(cc + 1) * T_own],
                       x_own[:, cc * T_own:(cc + 1) * T_own]) for cc in range(8)])

        bstk = ExitStack()
        sbt = lambda n, shape, d: bstk.enter_context(nc.sbuf_tensor(f"{n}_{shalf}", shape, d))
        obuf = sbt("obuf", [128, NPAIR * T_own], BF16)
        wp_bf = sbt("wp_bf", [128, 8 * C], BF16)
        qstg = [sbt(f"qstg{i}", [128, 2048], F32) for i in range(2)]
        f8w1 = [sbt(f"f8w1_{i}", [128, 2048], FP8) for i in range(2)]
        f8w2 = [sbt(f"f8w2_{i}", [128, 2048], FP8) for i in range(2)]
        attn_stk = ExitStack()
        sbn = lambda n, shape, d: attn_stk.enter_context(nc.sbuf_tensor(f"{n}_{shalf}", shape, d))
        kbuf = [sbn(f"kbuf{i}", [128, kv_hi], BF16) for i in range(2)]
        vbuf = [sbn(f"vbuf{i}", [128, kv_hi], BF16) for i in range(2)]
        qbuf = [sbn(f"qbuf{i}", [128, T_own], BF16) for i in range(2)]
        pbuf = [sbn(f"pbuf{i}", [128, 1024], BF16) for i in range(2)]
        ksq = sbn("ksq", [128, kv_hi], BF16)
        recipb = sbn("recipb", [128, 512], F32)
        recipb2 = sbn("recipb2", [128, 512], F32)
        Spair = [big0, big1]
        ObA = p4[:]
        ObB = p5[:]
        SbA = p6[:]
        SbB = p7[:]
        Baux = p6[:]

        for cc in range(8):
            GP.inc(G.dma_start(out=wp_bf[:, cc * C:(cc + 1) * C],
                               in_=d_Wp[cc * 128:(cc + 1) * 128, :]), 16)
        gfence()
        SY.wait_ge(GP.sem, GP.v)
        ldp = {}
        LD.inc(SY.dma_start(out=qbuf[0][:], in_=d_qd[:, 0:T_own]), 16)
        LD.inc(SY.dma_start(out=kbuf[0][:], in_=d_kd[:, 0:kv_hi]), 16)
        LD.inc(SY.dma_start(out=vbuf[0][:], in_=d_vd[:, 0:kv_hi]), 16)
        ldp[0] = syfence()

        te_pair_end = {}
        oblocks = _blocks(own_lo, own_hi)
        prev_tav = None
        prev_exp_last = None
        prev_evac = None

        qent = VE.v
        q1_ve, q1_sd, q1_st = {}, {}, {}
        q2_ve, q2_gp, q2_st = {}, {}, {}

        def emit_w1_quant(i):
            cc, hh = i // 2, i % 2
            stgw = stgA if i % 2 == 0 else stgB
            f8b = f8w1[i % 2]
            G.wait_ge(VE.sem, q1_ve[i - 2] if i >= 2 else qent)
            GP.inc(G.dma_start(
                out=stgw[:, 0:2048],
                in_=d_W1T[cc * 128:(cc + 1) * 128, hh * 2048:(hh + 1) * 2048]), 16)
            q1_sd[i] = gfence()
            V.wait_ge(GP.sem, q1_sd[i])
            if i >= 2:
                V.wait_ge(LD.sem, q1_st[i - 2])
            q1_ve[i] = vcustom(OPS["ANT_FQ8"], out=f8b[:, 0:2048], in0=stgw[:, 0:2048])
            SY.wait_ge(VE.sem, q1_ve[i])
            LD.inc(SY.dma_start(
                out=d_w1q8[cc * 128:(cc + 1) * 128, hh * 2048:(hh + 1) * 2048],
                in_=f8b[:, 0:2048]), 16)
            q1_st[i] = syfence()

        def emit_w2_quant(j):
            stgw = qstg[j % 2]
            f8b = f8w2[j % 2]
            G.wait_ge(VE.sem, q2_ve[j - 2] if j >= 2 else qent)
            for k in range(2):
                ncx = j * 2 + k
                GP.inc(G.dma_start(out=stgw[:, k * 1024:(k + 1) * 1024],
                                   in_=d_W2T[ncx * 128:(ncx + 1) * 128, :]), 16)
            q2_gp[j] = gfence()
            V.wait_ge(GP.sem, q2_gp[j])
            if j >= 2:
                V.wait_ge(LD.sem, q2_st[j - 2])
            q2_ve[j] = vcustom(OPS["ANT_FQ8"], out=f8b[:, 0:2048], in0=stgw[:, 0:2048])
            SY.wait_ge(VE.sem, q2_ve[j])
            for k in range(2):
                ncx = j * 2 + k
                LD.inc(SY.dma_start(out=d_w2q8[ncx * 128:(ncx + 1) * 128, :],
                                    in_=f8b[:, k * 1024:(k + 1) * 1024]), 16)
            q2_st[j] = syfence()

        for p in range(NPAIR):
            pk = p % 2
            if p + 1 < NPAIR:
                if p >= 1:
                    SY.wait_ge(TE.sem, te_pair_end[p - 1])
                    SY.wait_ge(SC.sem, sc_pair_end[p - 1])
                LD.inc(SY.dma_start(out=qbuf[1 - pk][:], in_=d_qd[:, (p + 1) * TOmax:(p + 1) * TOmax + T_own]), 16)
                LD.inc(SY.dma_start(out=kbuf[1 - pk][:], in_=d_kd[:, (p + 1) * T:(p + 1) * T + kv_hi]), 16)
                LD.inc(SY.dma_start(out=vbuf[1 - pk][:], in_=d_vd[:, (p + 1) * T:(p + 1) * T + kv_hi]), 16)
                ldp[p + 1] = syfence()
            S.wait_ge(LD.sem, ldp[p])
            S.wait_ge(TE.sem, TE.v)
            sksq = SC.inc(S.activation(out=ksq[:], in_=kbuf[pk][:], func=AF.Square))
            if prev_exp_last is not None:
                TN.wait_ge(SC.sem, prev_exp_last)
            if prev_evac is not None:
                TN.wait_ge(VE.sem, prev_evac)

            def warm(n):
                TN.ldweights(permsw[:])
                for _ in range(n):
                    TN.matmul(Spair[0][:, 0:512], permsw[:], qmask[:, 0:512],
                              start=True, stop=True)
            warm(6)
            TN.wait_ge(SC.sem, sksq)
            TN.wait_ge(VE.sem, VE.v)
            first = True
            vkm = None
            for (t0, w) in _blocks(0, kv_hi):
                TN.ldweights(ones1c[:])
                if vkm is not None:
                    TN.wait_ge(VE.sem, vkm)
                tkn = TE.inc(TN.matmul(Baux[0:1, 0:w], ones1c[:], ksq[:, t0:t0 + w], start=True, stop=True))
                V.wait_ge(TE.sem, tkn)
                if first:
                    vkm = VE.inc(V.reduce_max(out=kmax2[0:1, 0:1], in_=Baux[0:1, 0:w], axis=AX.X))
                    first = False
                else:
                    vk0 = VE.inc(V.reduce_max(out=kmax2[0:1, 1:2], in_=Baux[0:1, 0:w], axis=AX.X))
                    V.wait_ge(VE.sem, vk0)
                    vkm = VE.inc(V.tensor_max(out=kmax2[0:1, 0:1], in0=kmax2[0:1, 0:1], in1=kmax2[0:1, 1:2]))
            nob_p = len(oblocks)
            vqs = None
            tq2 = None
            knorm_te = TE.v
            for qi, (q0, qw) in enumerate(oblocks):
                qrel = q0 - own_lo
                S.wait_ge(TE.sem, tq2 if tq2 is not None else knorm_te)
                S.wait_ge(SC.sem, SC.v)
                sq2 = SC.inc(S.activation(out=ksq[:, 0:qw], in_=qbuf[pk][:, qrel:qrel + qw],
                                          func=AF.Square))
                TN.wait_ge(SC.sem, sq2)
                TN.wait_ge(VE.sem, vqs if vqs is not None else vkm)
                TN.ldweights(ones1c[:])
                tq2 = TE.inc(TN.matmul(Baux[0:1, 0:qw], ones1c[:], ksq[:, 0:qw], start=True, stop=True))
                V.wait_ge(TE.sem, tq2)
                if qi == 0:
                    V.wait_ge(VE.sem, vkm)
                vqs = VE.inc(V.reduce_max(out=kmax2[0:1, 2 + qi:3 + qi], in_=Baux[0:1, 0:qw], axis=AX.X))
            V.wait_ge(VE.sem, vqs)
            vq = VE.inc(V.tensor_scalar(out=kmax2[0:1, 8:8 + nob_p], in0=kmax2[0:1, 2:2 + nob_p],
                                        scalar1=kmax2[0:1, 0:1], scalar2=None, op0=ALU.mult))
            S.wait_ge(VE.sem, vq)
            sb0_ = SC.inc(S.activation(out=kmax2[0:1, 16:16 + nob_p], in_=kmax2[0:1, 8:8 + nob_p],
                                       func=AF.Ln))
            S.wait_ge(SC.sem, sb0_)
            ssb = SC.inc(S.activation(out=kmax2[0:1, 24:24 + nob_p], in_=kmax2[0:1, 16:16 + nob_p],
                                      func=AF.Exp, scale=0.5, bias=LNQ))
            V.wait_ge(SC.sem, ssb)
            vnb = VE.inc(V.tensor_scalar(out=scr_b[0:1, 0:nob_p], in0=kmax2[0:1, 24:24 + nob_p],
                                         scalar1=-1.0, scalar2=None, op0=ALU.mult))
            warm(5)
            TN.wait_ge(VE.sem, vnb)
            TN.ldweights(oner_lhs[:])
            tnb = TE.inc(TN.matmul(Baux[:, 0:nob_p], oner_lhs[:], scr_b[0:1, 0:nob_p],
                                   start=True, stop=True))
            S.wait_ge(TE.sem, tnb)
            snb = SC.inc(S.copy(out=negBall[:, 0:nob_p], in_=Baux[:, 0:nob_p]))
            TN.wait_ge(SC.sem, snb)
            emit_w1_quant(2 * p)
            emit_w1_quant(2 * p + 1)
            emit_w2_quant(2 * p)
            emit_w2_quant(2 * p + 1)
            for qi, (q0, qw) in enumerate(oblocks):
                qrel = q0 - own_lo
                KTQ = (q0 + qw + 127) // 128
                g0 = q0 // 128
                nbias = negBall[:, qi:qi + 1]
                if prev_evac is not None:
                    TN.wait_ge(VE.sem, prev_evac)
                if prev_exp_last is not None:
                    TN.wait_ge(SC.sem, prev_exp_last)
                exp_done = {}
                av_done = {}

                def emit_av(kt):
                    kk = kt % 2
                    TN.wait_ge(SC.sem, exp_done[kt])
                    TN.ldweights(vbuf[pk][:, kt * 128:kt * 128 + 64], tile_position=(0, 0))
                    TN.matmul(ObA[0:64, 0:qw], vbuf[pk][:, kt * 128:kt * 128 + 64], pbuf[kk][:, 0:qw],
                              start=(kt == 0), stop=(kt == KTQ - 1), tile_position=(0, 0))
                    TN.ldweights(vbuf[pk][:, kt * 128 + 64:(kt + 1) * 128], tile_position=(0, 64))
                    TN.matmul(ObB[64:128, 0:qw], vbuf[pk][:, kt * 128 + 64:(kt + 1) * 128],
                              pbuf[kk][:, 512:512 + qw], start=(kt == 0), stop=(kt == KTQ - 1),
                              tile_position=(0, 64))
                    TN.ldweights(ones128[:, 0:64], tile_position=(0, 0))
                    TN.matmul(SbA[0:64, 0:qw], ones128[:, 0:64], pbuf[kk][:, 0:qw],
                              start=(kt == 0), stop=(kt == KTQ - 1), tile_position=(0, 0))
                    TN.ldweights(ones128[:, 64:128], tile_position=(0, 64))
                    tav = TE.inc(TN.matmul(SbB[64:128, 0:qw], ones128[:, 64:128],
                                           pbuf[kk][:, 512:512 + qw],
                                           start=(kt == 0), stop=(kt == KTQ - 1),
                                           tile_position=(0, 64)))
                    av_done[kt] = tav
                    return tav

                for kt in range(KTQ):
                    kk = kt % 2
                    j = kt - g0
                    masked = j >= 0
                    if exp_done.get(kt - 2) is not None:
                        TN.wait_ge(SC.sem, exp_done[kt - 2])
                    TN.ldweights(kbuf[pk][0:64, kt * 128:(kt + 1) * 128], tile_position=(0, 0))
                    TN.matmul(Spair[kk][:, 0:qw], kbuf[pk][0:64, kt * 128:(kt + 1) * 128],
                              qbuf[pk][0:64, qrel:qrel + qw], start=True, stop=not masked,
                              tile_position=(0, 0))
                    TN.ldweights(kbuf[pk][64:128, kt * 128:(kt + 1) * 128], tile_position=(64, 0))
                    tqk0 = TN.matmul(Spair[kk][:, 512:512 + qw], kbuf[pk][64:128, kt * 128:(kt + 1) * 128],
                                     qbuf[pk][64:128, qrel:qrel + qw], start=True, stop=not masked,
                                     tile_position=(64, 0))
                    if masked:
                        jw = min(128, qw - j * 128)
                        last = (j == 0)
                        TN.ldweights(trib[:], tile_position=(0, 0))
                        TN.matmul(Spair[kk][:, j * 128: j * 128 + jw], trib[:], eyeb[:, 0:jw],
                                  start=False, stop=last)
                        tqk0 = TN.matmul(Spair[kk][:, 512 + j * 128: 512 + j * 128 + jw],
                                         trib[:], eyeb[:, 0:jw], start=False, stop=last)
                        if j > 0:
                            TN.ldweights(negrow[:])
                            TN.matmul(Spair[kk][:, 0:j * 128], negrow[:], onesr[0:1, 0:j * 128],
                                      start=False, stop=True)
                            tqk0 = TN.matmul(Spair[kk][:, 512:512 + j * 128], negrow[:],
                                             onesr[0:1, 0:j * 128], start=False, stop=True)
                    tqk = TE.inc(tqk0)
                    S.wait_ge(TE.sem, tqk)
                    if av_done.get(kt - 2) is not None:
                        S.wait_ge(TE.sem, av_done[kt - 2])
                    elif kt < 2 and prev_tav is not None:
                        S.wait_ge(TE.sem, prev_tav)
                    if qw == 512:
                        se = SC.inc(S.activation(out=pbuf[kk][:], in_=Spair[kk][:], func=AF.Exp,
                                                 scale=0.125, bias=nbias))
                    else:
                        S.activation(out=pbuf[kk][:, 0:qw], in_=Spair[kk][:, 0:qw], func=AF.Exp,
                                     scale=0.125, bias=nbias)
                        se = SC.inc(S.activation(out=pbuf[kk][:, 512:512 + qw], in_=Spair[kk][:, 512:512 + qw],
                                                 func=AF.Exp, scale=0.125, bias=nbias))
                    exp_done[kt] = se
                    if kt >= 1:
                        emit_av(kt - 1)
                tav = emit_av(KTQ - 1)
                prev_tav = tav
                prev_exp_last = exp_done[KTQ - 1]
                V.wait_ge(TE.sem, tav)
                VE.inc(V.reciprocal_approx_fast(out=recipb[0:64, 0:qw], in_=SbA[0:64, 0:qw]))
                vrc = VE.inc(V.reciprocal_approx_fast(out=recipb2[:, 0:qw], in_=SbB[:, 0:qw]))
                V.wait_ge(VE.sem, vrc)
                VE.inc(V.tensor_mul(out=obuf[0:64, p * T_own + qrel: p * T_own + qrel + qw],
                                    in0=ObA[0:64, 0:qw], in1=recipb[0:64, 0:qw]))
                prev_evac = VE.inc(V.tensor_mul(out=obuf[64:128, p * T_own + qrel: p * T_own + qrel + qw],
                                                in0=ObB[64:128, 0:qw], in1=recipb2[64:128, 0:qw]))
            te_pair_end[p] = TE.v
            if p == 0:
                sc_pair_end = {}
            sc_pair_end[p] = SC.v

        attn_stk.close()
        phase_barrier()
        if dbg:
            dbg_dump([(d_dxB[:, cc * T_own:(cc + 1) * T_own],
                       x_own[:, cc * T_own:(cc + 1) * T_own]) for cc in range(8)])
            dbg_dump([(d_dob[:, pp * T_own:(pp + 1) * T_own],
                       obuf[:, pp * T_own:(pp + 1) * T_own]) for pp in range(NPAIR)])

        TN.wait_ge(VE.sem, VE.v)
        TN.wait_ge(SC.sem, SC.v)
        nob = len(oblocks)
        cx_ve = {}
        for cc in range(8):
            for pidx in range(NPAIR):
                TN.ldweights(wp_bf[:, pidx * C + cc * 128: pidx * C + (cc + 1) * 128])
                for bi, (q0, qw) in enumerate(oblocks):
                    qrel = q0 - own_lo
                    if pidx == 0 and cc > 0:
                        TN.wait_ge(VE.sem, cx_ve[(cc - 1, bi)])
                    TN.matmul(pb[bi][:, 0:qw], wp_bf[:, pidx * C + cc * 128: pidx * C + (cc + 1) * 128],
                              obuf[:, pidx * T_own + qrel: pidx * T_own + qrel + qw],
                              start=(pidx == 0), stop=False)
            TN.wait_ge(VE.sem, ve_bp)
            TN.ldweights(bprow[0:1, cc * 128:(cc + 1) * 128])
            for bi, (q0, qw) in enumerate(oblocks):
                qrel = q0 - own_lo
                tpj = TE.inc(TN.matmul(pb[bi][:, 0:qw], bprow[0:1, cc * 128:(cc + 1) * 128],
                                       onesr[0:1, 0:qw], start=False, stop=True))
                V.wait_ge(TE.sem, tpj)
                cx_ve[(cc, bi)] = VE.inc(V.scalar_tensor_tensor(
                    out=x_own[:, cc * T_own + qrel: cc * T_own + qrel + qw],
                    in0=pb[bi][:, 0:qw], scalar=ga_c[:, cc:cc + 1],
                    in1=x_own[:, cc * T_own + qrel: cc * T_own + qrel + qw],
                    op0=ALU.mult, op1=ALU.add))

        bstk.close()
        phase_barrier()
        if dbg:
            dbg_dump([(d_dx2[:, cc * T_own:(cc + 1) * T_own],
                       x_own[:, cc * T_own:(cc + 1) * T_own]) for cc in range(8)])

        dstk = ExitStack()
        sbd = lambda n, shape, d: dstk.enter_context(nc.sbuf_tensor(f"{n}_{shalf}", shape, d))
        w1q = sbd("w1q", [128, 8 * 4096], FP8)
        w2q = sbd("w2q", [128, 32 * C], FP8)
        fqh2 = sbd("fqh2", [128, 8 * T_own], FP8)
        GW = 1216 if shalf == 0 else 1664
        fqh1 = sbd("fqh1", [128, 32 * GW], FP8)

        dv0 = syfence()
        for cc in range(8):
            LD.inc(SY.dma_start(out=w1q[:, cc * 4096:(cc + 1) * 4096],
                                in_=d_w1q8[cc * 128:(cc + 1) * 128, :]), 16)
        ldw1 = syfence()
        G.wait_ge(LD.sem, dv0)
        for ncx in range(32):
            GP.inc(G.dma_start(out=w2q[:, ncx * 1024:(ncx + 1) * 1024],
                               in_=d_w2q8[ncx * 128:(ncx + 1) * 128, :]), 16)
            if ncx % 8 == 7:
                gfence()
        gw2 = gfence()
        TN.wait_ge(LD.sem, ldw1)
        TN.wait_ge(GP.sem, gw2)
        bguard = {}

        prev_d1_te = None
        prev_d1_ve = None
        fq_ve = {}
        for bi, (q0, qw) in enumerate(oblocks):
            qrel = q0 - own_lo
            for cc in range(8):
                xsq = xsq1 if cc % 2 == 0 else xsq2
                S.wait_ge(VE.sem, cx_ve[(cc, bi)])
                if cc >= 2:
                    S.wait_ge(TE.sem, te_var2[cc - 2])
                elif prev_d1_te is not None:
                    S.wait_ge(TE.sem, prev_d1_te)
                ssq = SC.inc(S.activation(out=xsq[:, 0:qw],
                                          in_=x_own[:, cc * T_own + qrel: cc * T_own + qrel + qw],
                                          func=AF.Square))
                TN.wait_ge(SC.sem, ssq)
                if cc == 0 and prev_d1_ve is not None:
                    TN.wait_ge(VE.sem, prev_d1_ve)
                TN.ldweights(ones1c[:])
                TN.matmul(pb[6][0:1, 0:qw], ones1c[:],
                          x_own[:, cc * T_own + qrel: cc * T_own + qrel + qw],
                          start=(cc == 0), stop=(cc == 7))
                tvq = TE.inc(TN.matmul(pb[7][0:1, 0:qw], ones1c[:], xsq[:, 0:qw],
                                       start=(cc == 0), stop=(cc == 7)))
                if cc == 0:
                    te_var2 = {}
                te_var2[cc] = tvq
            S.wait_ge(TE.sem, tvq)
            if prev_d1_te is not None:
                S.wait_ge(VE.sem, prev_d1_ve)
            S.activation(out=mrow[0:1, 0:qw], in_=pb[6][0:1, 0:qw], func=AF.Copy, scale=1.0 / 1024.0)
            scm2 = SC.inc(S.activation(out=srow[0:1, 0:qw], in_=pb[7][0:1, 0:qw],
                                       func=AF.Copy, scale=1.0 / 1024.0))
            V.wait_ge(SC.sem, scm2)
            vm2 = VE.inc(V.tensor_mul(out=stg1[0:1, 0:qw], in0=mrow[0:1, 0:qw], in1=mrow[0:1, 0:qw]))
            V.wait_ge(VE.sem, vm2)
            vvv = VE.inc(V.tensor_sub(out=stg1[0:1, 0:qw], in0=srow[0:1, 0:qw], in1=stg1[0:1, 0:qw]))
            S.wait_ge(VE.sem, vvv)
            sl2 = SC.inc(S.activation(out=lrow[0:1, 0:qw], in_=stg1[0:1, 0:qw], func=AF.Ln, bias=LN_EPS))
            S.wait_ge(SC.sem, sl2)
            scr2b = SC.inc(S.activation(out=rrow[0:1, 0:qw], in_=lrow[0:1, 0:qw], func=AF.Exp, scale=-0.5))
            V.wait_ge(SC.sem, scr2b)
            V.tensor_copy(out=rbf[0:1, 0:qw], in_=rrow[0:1, 0:qw])
            vms = VE.inc(V.tensor_mul(out=stg2[0:1, 0:qw], in0=mrow[0:1, 0:qw], in1=rrow[0:1, 0:qw]))
            V.wait_ge(VE.sem, vms)
            vmr = VE.inc(V.tensor_copy(out=mr2b[0:1, 0:qw], in_=stg2[0:1, 0:qw]))
            TN.wait_ge(VE.sem, vmr)
            TN.ldweights(oner_lhs[:])
            TN.matmul(pb[6][:, 0:qw], oner_lhs[:], rbf[0:1, 0:qw], start=True, stop=True)
            trp = TE.inc(TN.matmul(pb[7][:, 0:qw], oner_lhs[:], mr2b[0:1, 0:qw],
                                   start=True, stop=True))
            prev_d1_te = trp
            for cc in range(8):
                V.wait_ge(TE.sem, trp)
                vd2 = VE.inc(V.tensor_mul(out=stg1[:, 0:qw],
                             in0=x_own[:, cc * T_own + qrel: cc * T_own + qrel + qw], in1=pb[6][:, 0:qw]))
                V.wait_ge(VE.sem, vd2)
                fq_ve[(bi, cc)] = vcustom(
                    OPS["ANT_FQ8MOD2"], out=fqh2[:, cc * T_own + qrel: cc * T_own + qrel + qw],
                    in0=stg1[:, 0:qw], in1=pb[7][:, 0:qw],
                    s0=cbP1[:, cc:cc + 1], s1=sb_c[:, cc:cc + 1])
            prev_d1_ve = fq_ve[(bi, 7)]

        te_d1_end = TE.v
        for (g0t, gw) in _blocks(own_lo, own_hi, GW):
            grel = g0t - own_lo
            subs = _blocks(0, gw)
            ns = len(subs)
            TN.wait_ge(VE.sem, VE.v)
            TN.wait_ge(SC.sem, SC.v)
            DR = mybir.MatmulPerfMode.DoubleRow
            w1q3 = w1q.rearrange("p (c n) -> p c n", c=8)
            fqh23 = fqh2.rearrange("p (c t) -> p c t", c=8)
            h1_ve = {}
            for ncx in range(32):
                for j in range(4):
                    TN.ldweights(w1q3[:, 2 * j:2 * j + 2, ncx * 128:(ncx + 1) * 128], perf_mode=DR)
                    for sbi, (sb0, sw) in enumerate(subs):
                        if j == 0 and h1_ve.get((ncx - 2, sbi)) is not None:
                            TN.wait_ge(VE.sem, h1_ve[(ncx - 2, sbi)])
                        ti = TN.matmul(pb[(ncx % 2) * 4 + sbi][:, 0:sw],
                                       w1q3[:, 2 * j:2 * j + 2, ncx * 128:(ncx + 1) * 128],
                                       fqh23[:, 2 * j:2 * j + 2, grel + sb0: grel + sb0 + sw],
                                       start=(j == 0), stop=(j == 3), perf_mode=DR)
                tm1 = TE.inc(ti)
                for sbi, (sb0, sw) in enumerate(subs):
                    bst_ = bstg1 if sbi % 2 == 0 else bstg2
                    S.wait_ge(TE.sem, tm1)
                    if bguard.get(sbi % 2) is not None:
                        S.wait_ge(VE.sem, bguard[sbi % 2])
                    srl = SC.inc(S.activation(out=bst_[:, 0:sw], in_=pb[(ncx % 2) * 4 + sbi][:, 0:sw],
                                              func=AF.Relu, bias=fqb1c[:, ncx:ncx + 1]))
                    V.wait_ge(SC.sem, srl)
                    h1_ve[(ncx, sbi)] = vcustom(
                        OPS["ANT_FQ8"], out=fqh1[:, ncx * GW + sb0: ncx * GW + sb0 + sw],
                        in0=bst_[:, 0:sw])
                    bguard[sbi % 2] = h1_ve[(ncx, sbi)]
            y2_ve = {}
            st_guard = {}
            mk_guard = {}
            w2q3 = w2q.rearrange("p (x c) -> p x c", x=32)
            fqh13 = fqh1.rearrange("p (x g) -> p x g", x=32)
            if ns < 4:
                TN.ldweights(permsw[:])
                for _ in range(8):
                    TN.matmul(pb[7][:, 0:512], permsw[:], qmask[:, 0:512],
                              start=True, stop=True)
            for cc in range(8):
                for j in range(16):
                    TN.ldweights(w2q3[:, 2 * j:2 * j + 2, cc * 128:(cc + 1) * 128], perf_mode=DR)
                    for sbi, (sb0, sw) in enumerate(subs):
                        if j == 0:
                            if cc == 0:
                                TN.wait_ge(VE.sem, h1_ve[(31, sbi)])
                            elif y2_ve.get(cc - 1) is not None:
                                TN.wait_ge(VE.sem, y2_ve[cc - 1])
                        ti2 = TN.matmul(pb[sbi][:, 0:sw],
                                        w2q3[:, 2 * j:2 * j + 2, cc * 128:(cc + 1) * 128],
                                        fqh13[:, 2 * j:2 * j + 2, sb0: sb0 + sw],
                                        start=(j == 0), stop=(j == 15), perf_mode=DR)
                tm2 = TE.inc(ti2)
                for sbi, (sb0, sw) in enumerate(subs):
                    ostg = stg1 if sbi % 2 == 0 else stg2
                    msk = xsq1 if sbi % 2 == 0 else xsq2
                    SY.wait_ge(TE.sem, te_d1_end)
                    if mk_guard.get(sbi % 2) is not None:
                        SY.wait_ge(VE.sem, mk_guard[sbi % 2])
                    LD.inc(SY.dma_start(out=msk[:, 0:sw],
                                        in_=d_maskz[cc * 128:(cc + 1) * 128,
                                                    grel + sb0:grel + sb0 + sw]), 16)
                    dmk = syfence()
                    bst_ = bstg1 if sbi % 2 == 0 else bstg2
                    V.wait_ge(TE.sem, tm2)
                    V.wait_ge(LD.sem, dmk)
                    if ST.v > 0:
                        V.wait_ge(ST.sem, ST.v)
                    if bguard.get(sbi % 2) is not None:
                        V.wait_ge(VE.sem, bguard[sbi % 2])
                    vcustom(OPS["ANT_RM"], out=bst_[:, 0:sw], in0=pb[sbi][:, 0:sw],
                            s0=fqb2c[:, cc:cc + 1])
                    vmm = VE.inc(V.tensor_mul(out=msk[:, 0:sw], in0=bst_[:, 0:sw], in1=msk[:, 0:sw]))
                    bguard[sbi % 2] = vmm
                    V.wait_ge(VE.sem, vmm)
                    yv = VE.inc(V.scalar_tensor_tensor(
                        out=ostg[:, 0:sw], in0=msk[:, 0:sw], scalar=gb_c[:, cc:cc + 1],
                        in1=x_own[:, cc * T_own + grel + sb0: cc * T_own + grel + sb0 + sw],
                        op0=ALU.mult, op1=ALU.add))
                    mk_guard[sbi % 2] = yv
                    SY.wait_ge(VE.sem, yv)
                    st_guard[sbi % 2] = ST.inc(
                        SY.dma_start(out=d_out[cc * 128:(cc + 1) * 128, grel + sb0: grel + sb0 + sw],
                                     in_=ostg[:, 0:sw]), 16)
                y2_ve[cc] = yv

        dstk.close()
        bst.close()
        for eng in ALL_ENG:
            eng.wait_ge(ST.sem, ST.v)

    prev_cast_guard = {}
    idxs = [eng.compute_val(eng.partition_id() // 4) for eng in ALL_ENG]
    for i in nc.Switch(engines=ALL_ENG, index=idxs, n=2):
        for s_, v_ in prologue_vals:
            s_.v = v_
        prev_cast_guard = {}
        emit_body(i)

    top.close()
    mybir.codegen_inst_isa_subclasses(nc)
    return nc



def kernel(L0_attn_x, L0_attn_Wq, L0_attn_bq, L0_attn_Wk, L0_attn_bk,
           L0_attn_Wv, L0_attn_bv, L0_attn_Wp, L0_attn_bp,
           L0_ffn_W1, L0_ffn_b1, L0_ffn_W2, L0_ffn_b2, L0_c, L0_wmod):
    _register_ops()
    f32 = np.float32
    xT = np.ascontiguousarray(np.asarray(L0_attn_x, f32).transpose(0, 2, 1))
    W1T = np.ascontiguousarray(np.asarray(L0_ffn_W1, f32).T)
    W2T = np.ascontiguousarray(np.asarray(L0_ffn_W2, f32).T)
    cosT, sinT = _rope_tables()
    qm = np.ascontiguousarray(_boundary_masks().reshape(4 * 128, 512)).astype(ml_dtypes.bfloat16)
    perm = np.zeros((128, 128), np.float32)
    idx = np.arange(128)
    perm[idx, idx ^ 1] = 1.0
    perm = perm.astype(ml_dtypes.bfloat16)
    keep = _threefry_keep_mask()
    maskzT = np.ascontiguousarray(
        (keep.astype(np.float32) * np.float32(1.25)).astype(ml_dtypes.bfloat16).transpose(0, 2, 1))
    ccols = [np.ascontiguousarray(np.asarray(L0_c[b], f32).reshape(8, 128).T) for b in range(B)]
    b1col = np.ascontiguousarray(np.asarray(L0_ffn_b1, f32).reshape(32, 128).T)
    b2col = np.ascontiguousarray(np.asarray(L0_ffn_b2, f32).reshape(8, 128).T)
    row = lambda v: np.ascontiguousarray(np.asarray(v, f32).reshape(1, -1))

    in_maps = []
    for i in range(8):
        b, s = i % 4, i // 4
        lo, hi = (0, SPLIT) if s == 0 else (SPLIT, T)
        mz = np.zeros((C, SPLIT), ml_dtypes.bfloat16)
        mz[:, 0:hi - lo] = maskzT[b][:, lo:hi]
        in_maps.append({
            "xT": xT[b], "cosT": cosT, "sinT": sinT, "qmask4": qm, "maskz": mz,
            "permswap": perm,
            "eye128": np.eye(128, dtype=np.float32).astype(ml_dtypes.bfloat16),
            "tri128": np.where(np.arange(128)[:, None] < np.arange(128)[None, :],
                               np.float32(NEG), np.float32(0.0)).astype(ml_dtypes.bfloat16),
            "Wq": np.asarray(L0_attn_Wq, f32), "Wk": np.asarray(L0_attn_Wk, f32),
            "Wv": np.asarray(L0_attn_Wv, f32), "Wp": np.asarray(L0_attn_Wp, f32),
            "W1T": W1T, "W2T": W2T, "wmod": np.asarray(L0_wmod, f32),
            "ccol": ccols[b],
            "bq_row": row(L0_attn_bq), "bk_row": row(L0_attn_bk),
            "bv_row": row(L0_attn_bv), "bp_row": row(L0_attn_bp),
            "b1col": b1col, "b2col": b2col,
        })

    nc = build_program()
    res = run_bass_kernel_spmd(nc, in_maps, core_ids=list(range(8)))
    out = np.empty((B, T, C), np.float32)
    for i in range(8):
        b, s = i % 4, i // 4
        lo, hi = (0, SPLIT) if s == 0 else (SPLIT, T)
        out[b, lo:hi, :] = res.results[i]["outT"][:, 0:hi - lo].T
    return out


# version 18
# speedup vs baseline: 1.0143x; 1.0006x over previous
import numpy as np
import ml_dtypes
from contextlib import ExitStack

import concourse.bass as bass
import concourse.mybir as mybir
from concourse.bass_utils import run_bass_kernel_spmd
import concourse.dve_ops as dve_ops
from concourse.dve_op import DveOp
from concourse.dve_spec import Spec, Src0, Src1, C0, C1, Zero, maxx, lower

F32 = mybir.dt.float32
BF16 = mybir.dt.bfloat16
FP8 = mybir.dt.float8e5
AF = mybir.ActivationFunctionType
ALU = mybir.AluOpType
AX = mybir.AxisListType

B, T, C, NH, HS = 4, 4096, 1024, 16, 64
NPAIR = 8
SPLIT = 2432
SWAPM = [(i + 1) if i % 2 == 0 else (i - 1) for i in range(32)]
NEG = -8.0e10
LN_EPS = 1e-5
LNQ = float(np.log(0.125))


def _threefry_keep_mask():
    n = B * T * C
    rot = ((13, 15, 26, 6), (17, 29, 16, 24))
    k0, k1 = np.uint32(0), np.uint32(42)
    ks = (k0, k1, k0 ^ k1 ^ np.uint32(0x1BD11BDA))
    x0 = np.zeros(n, np.uint32)
    x1 = np.arange(n, dtype=np.uint32)
    x0 += ks[0]; x1 += ks[1]
    for i in range(5):
        r = rot[i % 2]
        for j in range(4):
            x0 += x1
            x1 = (x1 << np.uint32(r[j])) | (x1 >> np.uint32(32 - r[j]))
            x1 ^= x0
        x0 += ks[(i + 1) % 3]; x1 += ks[(i + 2) % 3]
        x1 += np.uint32(i + 1)
    bits = x0 ^ x1
    u = ((bits >> np.uint32(9)) | np.uint32(0x3F800000)).view(np.float32) - np.float32(1.0)
    return (u < np.float32(0.8)).reshape(B, T, C)


def _rope_tables():
    half = HS // 2
    inv_freq = (np.float32(1.0) / np.float32(10000.0) **
                (np.arange(half, dtype=np.float32) * np.float32(2.0) / np.float32(HS))).astype(np.float32)
    ang = (np.arange(T, dtype=np.float32)[:, None] * inv_freq[None, :]).astype(np.float32)
    cos = np.cos(ang).astype(np.float32)
    sin = np.sin(ang).astype(np.float32)
    Cfm = np.empty((HS, T), np.float32)
    Sfm = np.empty((HS, T), np.float32)
    Cfm[0::2, :] = cos.T
    Cfm[1::2, :] = cos.T
    Sfm[0::2, :] = sin.T
    Sfm[1::2, :] = -sin.T
    return np.vstack([Cfm, Cfm]), np.vstack([Sfm, Sfm])


def _boundary_masks():
    out = np.zeros((4, 128, 512), np.float32)
    tri = np.where(np.arange(128)[:, None] <= np.arange(128)[None, :], 0.0, NEG).astype(np.float32)
    for j in range(4):
        for jq in range(4):
            blk = out[j][:, 128 * jq:128 * (jq + 1)]
            if jq < j:
                blk[:] = NEG
            elif jq == j:
                blk[:] = tri
    return out



def _fq_exact(x):
    x = np.asarray(x)
    dt = x.dtype
    ax = np.abs(x)
    with np.errstate(divide="ignore"):
        e = np.floor(np.log2(np.maximum(ax, np.array(1e-38, dt))))
    e = np.maximum(e, np.array(-14.0, dt))
    scale = np.exp2(e - 2)
    q = np.round(x / scale) * scale
    return np.where(ax == 0, np.zeros_like(x), q).astype(dt)


def _bf16_round_exact(x):
    x = np.asarray(x)
    dt = x.dtype
    ax = np.abs(x)
    with np.errstate(divide="ignore"):
        e = np.floor(np.log2(np.maximum(ax, np.array(1e-38, dt))))
    e = np.maximum(e, np.array(-126.0, dt))
    scale = np.exp2(e - 7)
    q = np.round(x / scale) * scale
    return np.where(ax == 0, x, q).astype(dt)


_OPS = {}


def _register_ops():
    if _OPS:
        return _OPS

    def mk(name, spec, ref):
        sha = {"v3": lower(spec, ver="v3", perf_en=False, name=name).sha("v3")}
        return DveOp(name, spec, subdim=False, uops_sha=sha, reference=ref)

    ops = [
        mk("ANT_FQ8", Spec(body=Src0 + Zero),
           lambda in0, in1, s0, s1, imm2: _fq_exact(in0)),
        mk("ANT_FQ8MOD", Spec(body=Src0 * C0 + C1),
           lambda in0, in1, s0, s1, imm2: _fq_exact(in0 * s0 + s1)),
        mk("ANT_RM", Spec(body=Src0 + C0),
           lambda in0, in1, s0, s1, imm2: _bf16_round_exact(in0 + s0)),
        mk("ANT_FQ8MOD2", Spec(body=(Src0 - Src1) * C0 + C1),
           lambda in0, in1, s0, s1, imm2: _fq_exact((in0 - in1) * s0 + s1)),
    ]
    for i, op in enumerate(ops):
        old = dve_ops.OPS[i]
        if old.name != op.name:
            dve_ops.OPS[i] = op
            dve_ops._SUB_OPCODE_FOR_NAME[op.name] = dve_ops._SUB_OPCODE_FOR_NAME.pop(old.name)
            dve_ops.CUSTOM_DVE_REFS.pop(old.name, None)
            dve_ops.CUSTOM_DVE_SPECS.pop(old.name, None)
            dve_ops.CUSTOM_DVE_REFS[op.name] = op.ref
            dve_ops.CUSTOM_DVE_SPECS[op.name] = op.spec
        _OPS[op.name] = op
    return _OPS


def _blocks(lo, hi, step=512):
    out, t = [], lo
    while t < hi:
        w = min(step, hi - t)
        out.append((t, w))
        t += w
    return out


class Ctr:
    def __init__(self, sem):
        self.sem = sem
        self.v = 0

    def inc(self, inst, n=1):
        inst.then_inc(self.sem, n)
        self.v += n
        return self.v



def build_program(dbg=False):
    OPS = _register_ops()
    nc = bass.Bass("TRN2", target_bir_lowering=False)
    top = ExitStack()
    sb = lambda n, shape, d: top.enter_context(nc.sbuf_tensor(n, shape, d))
    V, S, TN, G, SY = nc.vector, nc.scalar, nc.tensor, nc.gpsimd, nc.sync
    ALL_ENG = [SY, V, S, TN, G]
    TOmax = SPLIT

    ap = lambda h: h.ap()
    d_xT = ap(nc.dram_tensor("xT", [C, T], F32, kind="ExternalInput"))
    d_cos = ap(nc.dram_tensor("cosT", [128, T], F32, kind="ExternalInput"))
    d_sin = ap(nc.dram_tensor("sinT", [128, T], F32, kind="ExternalInput"))
    d_qm = ap(nc.dram_tensor("qmask4", [4 * 128, 512], BF16, kind="ExternalInput"))
    d_perm = ap(nc.dram_tensor("permswap", [128, 128], BF16, kind="ExternalInput"))
    d_eye = ap(nc.dram_tensor("eye128", [128, 128], BF16, kind="ExternalInput"))
    d_tri = ap(nc.dram_tensor("tri128", [128, 128], BF16, kind="ExternalInput"))
    d_maskz = ap(nc.dram_tensor("maskz", [C, TOmax], BF16, kind="ExternalInput"))
    d_Wq = ap(nc.dram_tensor("Wq", [C, C], F32, kind="ExternalInput"))
    d_Wk = ap(nc.dram_tensor("Wk", [C, C], F32, kind="ExternalInput"))
    d_Wv = ap(nc.dram_tensor("Wv", [C, C], F32, kind="ExternalInput"))
    d_Wp = ap(nc.dram_tensor("Wp", [C, C], F32, kind="ExternalInput"))
    d_W1T = ap(nc.dram_tensor("W1T", [C, 4 * C], F32, kind="ExternalInput"))
    d_W2T = ap(nc.dram_tensor("W2T", [4 * C, C], F32, kind="ExternalInput"))
    d_wmod = ap(nc.dram_tensor("wmod", [C, 6 * C], F32, kind="ExternalInput"))
    d_ccol = ap(nc.dram_tensor("ccol", [128, 8], F32, kind="ExternalInput"))
    d_bqr = ap(nc.dram_tensor("bq_row", [1, C], F32, kind="ExternalInput"))
    d_bkr = ap(nc.dram_tensor("bk_row", [1, C], F32, kind="ExternalInput"))
    d_bvr = ap(nc.dram_tensor("bv_row", [1, C], F32, kind="ExternalInput"))
    d_bpr = ap(nc.dram_tensor("bp_row", [1, C], F32, kind="ExternalInput"))
    d_b1c = ap(nc.dram_tensor("b1col", [128, 32], F32, kind="ExternalInput"))
    d_b2c = ap(nc.dram_tensor("b2col", [128, 8], F32, kind="ExternalInput"))
    d_out = ap(nc.dram_tensor("outT", [C, TOmax], F32, kind="ExternalOutput"))
    sk = "ExternalOutput" if dbg else "Internal"
    d_kd = ap(nc.dram_tensor("kd_s", [128, NPAIR * T], BF16, kind=sk))
    d_vd = ap(nc.dram_tensor("vd_s", [128, NPAIR * T], BF16, kind=sk))
    d_qd = ap(nc.dram_tensor("qd_s", [128, NPAIR * TOmax], BF16, kind=sk))
    d_w1q8 = ap(nc.dram_tensor("w1q8_s", [C, 4 * C], FP8, kind="Internal"))
    d_w2q8 = ap(nc.dram_tensor("w2q8_s", [4 * C, C], FP8, kind="Internal"))
    if dbg:
        d_dmod = ap(nc.dram_tensor("dmod", [128, 40], F32, kind="ExternalOutput"))
        d_dsa = ap(nc.dram_tensor("dsa", [128, 8], BF16, kind="ExternalOutput"))
        d_dxA = ap(nc.dram_tensor("dxA", [128, 8 * TOmax], BF16, kind="ExternalOutput"))
        d_dxB = ap(nc.dram_tensor("dxB", [128, 8 * TOmax], BF16, kind="ExternalOutput"))
        d_dob = ap(nc.dram_tensor("dob", [128, 8 * TOmax], BF16, kind="ExternalOutput"))
        d_dx2 = ap(nc.dram_tensor("dx2", [128, 8 * TOmax], BF16, kind="ExternalOutput"))

    sem = lambda n: top.enter_context(nc.semaphore(n))
    LD = Ctr(sem("LD"))
    GP = Ctr(sem("GP"))
    VD = Ctr(sem("VD"))
    SD = Ctr(sem("SD"))
    VE = Ctr(sem("VE"))
    SC = Ctr(sem("SC"))
    TE = Ctr(sem("TE"))
    ST = Ctr(sem("ST"))
    SEMS = [LD, GP, VD, SD, VE, SC, TE, ST]

    qmask = sb("qmaskS", [128, 4 * 512], BF16)
    rowf = sb("rowf", [1, 4 * 512], F32)
    rowb = sb("rowb", [1, 4 * 512], BF16)
    stgA = sb("stgA", [128, 2048], F32)
    stgB = sb("stgB", [128, 2048], F32)
    stg1 = sb("stg1", [128, 512], F32)
    stg2 = sb("stg2", [128, 512], F32)
    stg3 = sb("stg3", [128, 512], F32)
    bstg1 = sb("bstg1", [128, 512], BF16)
    bstg2 = sb("bstg2", [128, 512], BF16)
    xsq1 = sb("xsq1", [128, 512], BF16)
    xsq2 = sb("xsq2", [128, 512], BF16)
    bprow = sb("bprow", [1, C], BF16)
    caP1 = sb("caP1", [128, 8], F32)
    cbP1 = sb("cbP1", [128, 8], F32)
    sb_c = sb("sb_c", [128, 8], F32)
    ga_c = sb("ga_c", [128, 8], F32)
    gb_c = sb("gb_c", [128, 8], F32)
    sa_cb = sb("sa_cb", [128, 8], BF16)
    silu_cb = sb("silu_cb", [128, 8], BF16)
    fqb1c = sb("fqb1c", [128, 32], F32)
    fqb2c = sb("fqb2c", [128, 8], F32)
    f8tmp = sb("f8tmp", [128, 32], FP8)
    ones1c = sb("ones1c", [128, 1], BF16)
    oner_lhs = sb("oner_lhs", [1, 128], BF16)
    onesr = sb("onesr", [1, 512], BF16)
    ones128 = sb("ones128", [128, 128], BF16)
    silu_lo = sb("silu_lo", [128, 8], BF16)
    permsw = sb("permsw", [128, 128], BF16)
    eyeb = sb("eyeb", [128, 128], BF16)
    trib = sb("trib", [128, 128], BF16)
    negrow = sb("negrow", [1, 128], BF16)
    r_col = sb("r_col", [128, 32], F32)
    negBall = sb("negBall", [128, 8], F32)
    kmax2 = sb("kmax2", [1, 32], F32)
    big0 = top.enter_context(nc.psum_tensor("big0", [128, 1024], F32))
    big1 = top.enter_context(nc.psum_tensor("big1", [128, 1024], F32))
    p4 = top.enter_context(nc.psum_tensor("p4", [128, 512], F32))
    p5 = top.enter_context(nc.psum_tensor("p5", [128, 512], F32))
    p6 = top.enter_context(nc.psum_tensor("p6", [128, 512], F32))
    p7 = top.enter_context(nc.psum_tensor("p7", [128, 512], F32))
    pb = [big0[:, 0:512], big0[:, 512:1024], big1[:, 0:512], big1[:, 512:1024],
          p4[:], p5[:], p6[:], p7[:]]

    def syfence():
        SY.wait_ge(LD.sem, LD.v)
        return LD.v

    def gfence():
        G.wait_ge(GP.sem, GP.v)
        return GP.v

    def phase_barrier():
        for eng in ALL_ENG:
            for ctr in (VE, SC, TE, GP, SD, LD, VD):
                if ctr.v > 0:
                    eng.wait_ge(ctr.sem, ctr.v)

    def vcustom(op, **kw):
        v = VE.inc(V._custom_dve(op, **kw))
        V.wait_ge(VE.sem, v)
        return v

    def dbg_dump(pairs):
        if not dbg:
            return
        for dst, src in pairs:
            LD.inc(SY.dma_start(out=dst, in_=src), 16)
        dv = syfence()
        for eng in ALL_ENG:
            eng.wait_ge(LD.sem, dv)

    mrow = rowf[0:1, 0:512]
    srow = rowf[0:1, 512:1024]
    lrow = rowf[0:1, 1024:1536]
    rrow = rowf[0:1, 1536:2048]
    rbf = rowb[0:1, 0:512]
    mr2b = rowb[0:1, 512:1024]
    scr_b = rowb[0:1, 1024:1536]
    scr_c = rowb[0:1, 1536:2048]

    V.memset(ones1c[:], 1.0)
    V.memset(oner_lhs[:], 1.0)
    V.memset(onesr[:], 1.0)
    V.memset(ones128[:], 1.0)
    cpool = sb("cpool", [128, 4], F32)
    V.memset(cpool[:, 0:1], LN_EPS)
    V.memset(cpool[:, 1:2], 1.0)
    ve_consts = VE.inc(V.memset(cpool[:, 2:3], LNQ))
    nc.const_aps.register(F32, LN_EPS, cpool[:, 0:1])
    nc.const_aps.register(F32, 1.0, cpool[:, 1:2])
    nc.const_aps.register(F32, LNQ, cpool[:, 2:3])
    S.wait_ge(VE.sem, ve_consts)
    TN.wait_ge(VE.sem, ve_consts)

    for j in range(4):
        LD.inc(SY.dma_start(out=qmask[:, j * 512:(j + 1) * 512], in_=d_qm[j * 128:(j + 1) * 128, :]), 16)
    LD.inc(SY.dma_start(out=permsw[:], in_=d_perm), 16)
    LD.inc(SY.dma_start(out=eyeb[:], in_=d_eye), 16)
    LD.inc(SY.dma_start(out=trib[:], in_=d_tri), 16)
    V.memset(negrow[:], NEG)
    LD.inc(SY.dma_start(out=stg1[:, 0:8], in_=d_ccol), 16)
    LD.inc(SY.dma_start(out=stg2[:, 0:32], in_=d_b1c), 16)
    LD.inc(SY.dma_start(out=stg2[:, 32:40], in_=d_b2c), 16)
    LD.inc(SY.dma_start(out=stgA[0:1, 1024:2048], in_=d_bpr), 16)
    ld0 = syfence()

    V.wait_ge(LD.sem, ld0)
    vcustom(OPS["ANT_FQ8"], out=f8tmp[:, 0:32], in0=stg2[:, 0:32])
    vcp = VE.inc(V.tensor_copy(out=fqb1c[:], in_=f8tmp[:, 0:32]))
    V.wait_ge(VE.sem, vcp)
    vcustom(OPS["ANT_FQ8"], out=f8tmp[:, 0:8], in0=stg2[:, 32:40])
    V.tensor_copy(out=fqb2c[:], in_=f8tmp[:, 0:8])
    ve_bp = VE.inc(V.tensor_copy(out=bprow[0:1, 0:C], in_=stgA[0:1, 1024:2048]))

    S.wait_ge(LD.sem, ld0)
    sc_si = SC.inc(S.activation(out=stg1[:, 8:16], in_=stg1[:, 0:8], func=AF.Silu))
    V.wait_ge(SC.sem, sc_si)
    ve_si0 = VE.inc(V.tensor_copy(out=silu_cb[:], in_=stg1[:, 8:16]))
    V.wait_ge(VE.sem, ve_si0)
    ve_si = VE.inc(V.tensor_sub(out=silu_lo[:], in0=stg1[:, 8:16], in1=silu_cb[:]))

    mod_te = {}
    mod_sc = {}
    for mb in range(12):
        bank = pb[mb % 2]
        for cc in range(8):
            k = mb * 8 + cc
            stgw = stgA if k % 2 == 0 else stgB
            bw = bstg1 if k % 2 == 0 else bstg2
            if k % 2 == 0:
                if k >= 2:
                    SY.wait_ge(TE.sem, mod_te[k - 2])
                LD.inc(SY.dma_start(out=stgw[:, 0:512],
                                    in_=d_wmod[cc * 128:(cc + 1) * 128, mb * 512:(mb + 1) * 512]), 16)
                dv = syfence()
                V.wait_ge(LD.sem, dv)
            else:
                if k >= 2:
                    S.wait_ge(TE.sem, mod_te[k - 2])
                SD.inc(S.dma_start(out=stgw[:, 0:512],
                                   in_=d_wmod[cc * 128:(cc + 1) * 128, mb * 512:(mb + 1) * 512]), 16)
                S.wait_ge(SD.sem, SD.v)
                V.wait_ge(SD.sem, SD.v)
            if k >= 2:
                V.wait_ge(TE.sem, mod_te[k - 2])
            gmb = mb in (4, 5, 10, 11)
            cv = VE.inc(V.tensor_copy(out=bw[:, 0:512], in_=stgw[:, 0:512]))
            if gmb:
                bwlo = xsq1 if k % 2 == 0 else xsq2
                V.wait_ge(VE.sem, cv)
                cv = VE.inc(V.tensor_sub(out=bwlo[:, 0:512], in0=stgw[:, 0:512], in1=bw[:, 0:512]))
            TN.wait_ge(VE.sem, cv)
            TN.wait_ge(VE.sem, ve_si)
            if cc == 0 and mb >= 2:
                TN.wait_ge(SC.sem, mod_sc[mb - 2])
            TN.ldweights(silu_cb[:, cc:cc + 1])
            if gmb:
                TN.matmul(bank[0:1, 0:512], silu_cb[:, cc:cc + 1], bw[:, 0:512],
                          start=(cc == 0), stop=False)
                TN.matmul(bank[0:1, 0:512], silu_cb[:, cc:cc + 1], bwlo[:, 0:512],
                          start=False, stop=False)
                TN.ldweights(silu_lo[:, cc:cc + 1])
                mod_te[k] = TE.inc(TN.matmul(bank[0:1, 0:512], silu_lo[:, cc:cc + 1], bw[:, 0:512],
                                             start=False, stop=(cc == 7)))
            else:
                mod_te[k] = TE.inc(TN.matmul(bank[0:1, 0:512], silu_cb[:, cc:cc + 1], bw[:, 0:512],
                                             start=(cc == 0), stop=(cc == 7)))
        S.wait_ge(TE.sem, mod_te[mb * 8 + 7])
        mod_sc[mb] = SC.inc(S.copy(out=rowf[0:1, (mb % 2) * 512:(mb % 2) * 512 + 512],
                                   in_=bank[0:1, 0:512]))
        if mb % 2 == 1:
            vecidx = mb // 2
            modr = rowf[0:1, 0:1024]
            V.wait_ge(SC.sem, mod_sc[mb])
            if vecidx in (2, 5):
                scs = SC.inc(S.activation(out=stg3[0:1, 0:512], in_=modr[0:1, 0:512], func=AF.Sign))
                scs2 = SC.inc(S.activation(out=rowf[0:1, 1536:2048], in_=modr[0:1, 512:1024], func=AF.Sign))
                v_ = VE.inc(V.tensor_scalar_mul(out=stg1[0:1, 0:512], in0=modr[0:1, 0:512], scalar1=-1.0))
                v_ = VE.inc(V.tensor_scalar_mul(out=stg2[0:1, 0:512], in0=modr[0:1, 512:1024], scalar1=-1.0))
                V.wait_ge(VE.sem, v_)
                v_ = VE.inc(V.tensor_max(out=stg1[0:1, 0:512], in0=stg1[0:1, 0:512], in1=modr[0:1, 0:512]))
                v_ = VE.inc(V.tensor_max(out=stg2[0:1, 0:512], in0=stg2[0:1, 0:512], in1=modr[0:1, 512:1024]))
                V.wait_ge(VE.sem, v_)
                v_ = VE.inc(V.tensor_scalar_max(out=stg1[0:1, 0:512], in0=stg1[0:1, 0:512], scalar1=0.25))
                v_ = VE.inc(V.tensor_scalar_max(out=stg2[0:1, 0:512], in0=stg2[0:1, 0:512], scalar1=0.25))
                V.wait_ge(VE.sem, v_)
                V.wait_ge(SC.sem, scs2)
                v_ = VE.inc(V.tensor_mul(out=modr[0:1, 0:512], in0=stg1[0:1, 0:512], in1=stg3[0:1, 0:512]))
                v_ = VE.inc(V.tensor_mul(out=modr[0:1, 512:1024], in0=stg2[0:1, 0:512], in1=rowf[0:1, 1536:2048]))
                V.wait_ge(VE.sem, v_)
            vrow = VE.inc(V.tensor_copy(out=rowb[0:1, 0:1024], in_=modr[0:1, 0:1024]))
            TN.wait_ge(VE.sem, vrow)
            bankc = pb[2]
            if vecidx >= 1:
                TN.wait_ge(SC.sem, colsc)
            for j in range(8):
                TN.ldweights(rowb[0:1, j * 128:(j + 1) * 128])
                tcol = TE.inc(TN.matmul(bankc[:, j:j + 1], rowb[0:1, j * 128:(j + 1) * 128],
                                        ones1c[0:1, 0:1], start=True, stop=True))
            S.wait_ge(TE.sem, tcol)
            dst = (sa_cb, caP1, ga_c, sb_c, cbP1, gb_c)[vecidx]
            if vecidx in (1, 4):
                colsc = SC.inc(S.activation(out=dst[:, 0:8], in_=bankc[:, 0:8], func=AF.Copy, bias=1.0))
            else:
                colsc = SC.inc(S.copy(out=dst[:, 0:8], in_=bankc[:, 0:8]))
    prol_colsc = colsc

    prologue = {s_.sem.name: s_.v for s_ in SEMS}
    prologue_vals = [(s_, s_.v) for s_ in SEMS]

    def emit_body(shalf):
        own_lo, own_hi = (0, SPLIT) if shalf == 0 else (SPLIT, T)
        kv_hi = SPLIT if shalf == 0 else T
        T_own = own_hi - own_lo
        T_oth = own_lo
        KT = kv_hi // 128

        for eng in (SY, V, S, TN):
            for s_, v_ in prologue_vals:
                if v_ > 0:
                    eng.wait_ge(s_.sem, v_)

        bst = ExitStack()
        sbb = lambda n, shape, d: bst.enter_context(nc.sbuf_tensor(f"{n}_{shalf}", shape, d))
        x_own = sbb("x_own", [128, 8 * T_own], BF16)

        def xslice(t0, w, cc):
            for (xt, glo, ghi, width, goff) in regions:
                if glo <= t0 and t0 + w <= ghi:
                    return xt[:, cc * width + (t0 - goff): cc * width + (t0 - goff) + w]
            raise AssertionError((t0, w))

        astk = ExitStack()
        sba = lambda n, shape, d: astk.enter_context(nc.sbuf_tensor(f"{n}_{shalf}", shape, d))
        regions = []
        if T_oth:
            x_oth = sba("x_oth", [128, 8 * T_oth], BF16)
            regions.append((x_oth, 0, T_oth, T_oth, 0))
        regions.append((x_own, own_lo, own_hi, T_own, own_lo))
        wq_bf = sba("wq_bf", [128, 8 * C], BF16)
        wk_bf = sba("wk_bf", [128, 8 * C], BF16)
        wv_bf = sba("wv_bf", [128, 8 * C], BF16)
        c_r = sba("c_r", [128, T], BF16)
        s_r = sba("s_r", [128, T], BF16)
        mstd2 = sba("mstd2", [2, T], BF16)
        corr_q = sba("corr_q", [2, C], BF16)
        corr_k = sba("corr_k", [2, C], BF16)
        corr_v = sba("corr_v", [2, C], BF16)

        def w_load(d_W, w_bf):
            for cc in range(8):
                GP.inc(G.dma_start(out=w_bf[:, cc * C:(cc + 1) * C],
                                   in_=d_W[cc * 128:(cc + 1) * 128, :]), 16)
            return gfence()

        def w_prep(w_bf, gws, corr, brow_src):
            ve_ch = {}
            V.wait_ge(GP.sem, gws)
            for cc in range(8):
                ve_ch[cc] = VE.inc(V.tensor_scalar(out=w_bf[:, cc * C:(cc + 1) * C],
                                                   in0=w_bf[:, cc * C:(cc + 1) * C],
                                                   scalar1=caP1[:, cc:cc + 1], scalar2=None, op0=ALU.mult))
            for cc in range(8):
                TN.wait_ge(VE.sem, ve_ch[cc])
                TN.ldweights(ones1c[:])
                TN.matmul(pb[4][0:1, 0:512], ones1c[:], w_bf[:, cc * C:cc * C + 512],
                          start=(cc == 0), stop=(cc == 7))
                tcs = TE.inc(TN.matmul(pb[5][0:1, 0:512], ones1c[:], w_bf[:, cc * C + 512:(cc + 1) * C],
                                       start=(cc == 0), stop=(cc == 7)))
            for cc in range(8):
                TN.ldweights(sa_cb[:, cc:cc + 1])
                TN.matmul(pb[6][0:1, 0:512], sa_cb[:, cc:cc + 1], w_bf[:, cc * C:cc * C + 512],
                          start=(cc == 0), stop=(cc == 7))
                tbp = TE.inc(TN.matmul(pb[7][0:1, 0:512], sa_cb[:, cc:cc + 1], w_bf[:, cc * C + 512:(cc + 1) * C],
                                       start=(cc == 0), stop=(cc == 7)))
            S.wait_ge(TE.sem, tbp)
            S.activation(out=corr[0:1, 0:512], in_=pb[4][0:1, 0:512], func=AF.Copy, scale=-1.0)
            sc1 = SC.inc(S.activation(out=corr[0:1, 512:C], in_=pb[5][0:1, 0:512], func=AF.Copy, scale=-1.0))
            V.wait_ge(TE.sem, tbp)
            V.tensor_add(out=rowb[0:1, 0:512], in0=pb[6][0:1, 0:512], in1=brow_src[0:1, 0:512])
            vb = VE.inc(V.tensor_add(out=rowb[0:1, 512:1024], in0=pb[7][0:1, 0:512], in1=brow_src[0:1, 512:C]))
            G.wait_ge(VE.sem, vb)
            gd = GP.inc(G.dma_start(out=corr[1:2, 0:C], in_=rowb[0:1, 0:1024]), 16)
            V.wait_ge(GP.sem, gd)
            TN.wait_ge(SC.sem, sc1)
            return gd

        gp_guard = {}
        swp_guard = [None, None, None, None]
        accq = pb[5]
        biasr = sba("biasr", [1, 3 * C], F32)
        LD.inc(SY.dma_start(out=biasr[0:1, 0:C], in_=d_bqr), 16)
        LD.inc(SY.dma_start(out=biasr[0:1, C:2 * C], in_=d_bkr), 16)
        LD.inc(SY.dma_start(out=biasr[0:1, 2 * C:3 * C], in_=d_bvr), 16)
        ldb = syfence()
        V.wait_ge(LD.sem, ldb)
        xblk = [(xt, width, goff, t0, w)
                for (xt, glo, ghi, width, goff) in regions
                for (t0, w) in _blocks(glo, ghi)]
        gxv = {}

        def x_load(n):
            for _ in range(n):
                if not xblk:
                    return
                xt, width, goff, t0, w = xblk.pop(0)
                for cc in range(8):
                    GP.inc(G.dma_start(out=xt[:, cc * width + (t0 - goff): cc * width + (t0 - goff) + w],
                                       in_=d_xT[cc * 128:(cc + 1) * 128, t0:t0 + w]), 16)
                gxv[(id(xt), t0)] = gfence()
        gws_q = w_load(d_Wq, wq_bf)
        x_load(2)
        gws_k = w_load(d_Wk, wk_bf)
        x_load(2)
        gws_v = w_load(d_Wv, wv_bf)
        x_load(99)
        gd_q = w_prep(wq_bf, gws_q, corr_q, biasr[0:1, 0:C])
        gd_k = w_prep(wk_bf, gws_k, corr_k, biasr[0:1, C:2 * C])
        gd_v = w_prep(wv_bf, gws_v, corr_v, biasr[0:1, 2 * C:3 * C])

        last_rcol_sc = None
        last_rope_ve = None
        last_mstd_gp = None
        prev_stat_te = None
        for (xt, glo, ghi, width, goff) in regions:
            for (t0, w) in _blocks(glo, ghi):
                gx_blk = gxv[(id(xt), t0)]
                for cc in range(8):
                    xsq = xsq1 if cc % 2 == 0 else xsq2
                    S.wait_ge(GP.sem, gx_blk)
                    if cc >= 2:
                        S.wait_ge(TE.sem, te_var[cc - 2])
                    elif prev_stat_te is not None:
                        S.wait_ge(TE.sem, prev_stat_te)
                    ssq = SC.inc(S.activation(out=xsq[:, 0:w], in_=xslice(t0, w, cc), func=AF.Square))
                    TN.wait_ge(SC.sem, ssq)
                    TN.ldweights(ones1c[:])
                    TN.matmul(pb[0][0:1, 0:w], ones1c[:], xslice(t0, w, cc),
                              start=(cc == 0), stop=(cc == 7))
                    tv = TE.inc(TN.matmul(pb[1][0:1, 0:w], ones1c[:], xsq[:, 0:w],
                                          start=(cc == 0), stop=(cc == 7)))
                    if cc == 0:
                        te_var = {}
                    te_var[cc] = tv
                prev_stat_te = tv
                S.wait_ge(TE.sem, tv)
                if last_rope_ve is not None:
                    S.wait_ge(VE.sem, last_rope_ve)
                S.activation(out=mrow[0:1, 0:w], in_=pb[0][0:1, 0:w], func=AF.Copy, scale=1.0 / 1024.0)
                scm = SC.inc(S.activation(out=srow[0:1, 0:w], in_=pb[1][0:1, 0:w],
                                          func=AF.Copy, scale=1.0 / 1024.0))
                V.wait_ge(SC.sem, scm)
                vm_ = VE.inc(V.tensor_mul(out=stg1[0:1, 0:w], in0=mrow[0:1, 0:w], in1=mrow[0:1, 0:w]))
                V.wait_ge(VE.sem, vm_)
                vv = VE.inc(V.tensor_sub(out=stg1[0:1, 0:w], in0=srow[0:1, 0:w], in1=stg1[0:1, 0:w]))
                S.wait_ge(VE.sem, vv)
                sl_ = SC.inc(S.activation(out=lrow[0:1, 0:w], in_=stg1[0:1, 0:w], func=AF.Ln, bias=LN_EPS))
                S.wait_ge(SC.sem, sl_)
                S.activation(out=rrow[0:1, 0:w], in_=lrow[0:1, 0:w], func=AF.Exp, scale=-0.5)
                scstd = SC.inc(S.activation(out=srow[0:1, 0:w], in_=lrow[0:1, 0:w], func=AF.Exp, scale=0.5))
                V.wait_ge(SC.sem, scstd)
                V.tensor_copy(out=rbf[0:1, 0:w], in_=rrow[0:1, 0:w])
                V.tensor_copy(out=mstd2[0:1, t0:t0 + w], in_=mrow[0:1, 0:w])
                if last_mstd_gp is not None:
                    V.wait_ge(GP.sem, last_mstd_gp)
                vstd = VE.inc(V.tensor_copy(out=scr_c[0:1, 0:w], in_=srow[0:1, 0:w]))
                G.wait_ge(VE.sem, vstd)
                last_mstd_gp = GP.inc(G.dma_start(out=mstd2[1:2, t0:t0 + w], in_=scr_c[0:1, 0:w]), 16)
                TN.wait_ge(VE.sem, vstd)
                if last_rcol_sc is not None:
                    TN.wait_ge(SC.sem, last_rcol_sc)
                nj = (w + 127) // 128
                for j in range(nj):
                    jw = min(128, w - j * 128)
                    TN.ldweights(rbf[0:1, j * 128: j * 128 + jw])
                    trc = TE.inc(TN.matmul(pb[2][0:jw, j:j + 1], rbf[0:1, j * 128: j * 128 + jw],
                                           ones1c[0:1, 0:1], start=True, stop=True))
                S.wait_ge(TE.sem, trc)
                last_rcol_sc = SC.inc(S.copy(out=r_col[:, t0 // 128: t0 // 128 + nj], in_=pb[2][:, 0:nj]))
                if last_rope_ve is not None:
                    TN.wait_ge(VE.sem, last_rope_ve)
                TN.ldweights(oner_lhs[:])
                trr = TE.inc(TN.matmul(pb[3][:, 0:w], oner_lhs[:], rbf[0:1, 0:w],
                                       start=True, stop=True))
                if last_rope_ve is not None:
                    SY.wait_ge(VE.sem, last_rope_ve)
                LD.inc(SY.dma_start(out=stg2[:, 0:w], in_=d_cos[:, t0:t0 + w]), 16)
                dcs = syfence()
                V.wait_ge(LD.sem, dcs)
                V.wait_ge(TE.sem, trr)
                vcr = VE.inc(V.tensor_mul(out=c_r[:, t0:t0 + w], in0=stg2[:, 0:w], in1=pb[3][:, 0:w]))
                SY.wait_ge(VE.sem, vcr)
                LD.inc(SY.dma_start(out=stg2[:, 0:w], in_=d_sin[:, t0:t0 + w]), 16)
                dsn = syfence()
                V.wait_ge(LD.sem, dsn)
                last_rope_ve = VE.inc(V.tensor_mul(out=s_r[:, t0:t0 + w], in0=stg2[:, 0:w], in1=pb[3][:, 0:w]))

        mstd_gate_gp = GP.v
        def qk_proj(w_bf, corr, dst, dst_stride, rlist):
            for eng in (TN,):
                eng.wait_ge(VE.sem, VE.v)
                eng.wait_ge(SC.sem, SC.v)
                eng.wait_ge(GP.sem, GP.v)
            bank_guard = {}
            seq = 0
            for (xt, glo, ghi, width, goff) in rlist:
                blocks = _blocks(glo, ghi)
                nb = len(blocks)
                for dc in range(8):
                    base = seq
                    for cc in range(8):
                        TN.ldweights(w_bf[:, cc * C + dc * 128: cc * C + (dc + 1) * 128])
                        for bi, (t0, w) in enumerate(blocks):
                            sl = (base + bi) % 8
                            if cc == 0 and bank_guard.get(sl) is not None:
                                TN.wait_ge(VE.sem, bank_guard[sl])
                            TN.matmul(pb[sl][:, 0:w], w_bf[:, cc * C + dc * 128: cc * C + (dc + 1) * 128],
                                      xt[:, cc * width + (t0 - goff): cc * width + (t0 - goff) + w],
                                      start=(cc == 0), stop=False)
                    seq += nb
                    TN.ldweights(corr[:, dc * 128:(dc + 1) * 128])
                    for bi, (t0, w) in enumerate(blocks):
                        sl = (base + bi) % 8
                        tlast = TE.inc(TN.matmul(pb[sl][:, 0:w], corr[:, dc * 128:(dc + 1) * 128],
                                                 mstd2[:, t0:t0 + w], start=False, stop=True))
                        V.wait_ge(TE.sem, tlast)
                        if swp_guard[0] is not None:
                            V.wait_ge(VE.sem, swp_guard[0])
                        vc1 = VE.inc(V.tensor_mul(out=stg1[:, 0:w], in0=pb[sl][:, 0:w], in1=c_r[:, t0:t0 + w]))
                        bw = bstg1 if (dc + bi) % 2 == 0 else bstg2
                        V.wait_ge(VE.sem, vc1)
                        if gp_guard.get(1 - (dc + bi) % 2) is not None:
                            V.wait_ge(GP.sem, gp_guard[1 - (dc + bi) % 2])
                        v2 = VE.inc(V.tensor_mul(out=bw[:, 0:w], in0=pb[sl][:, 0:w], in1=s_r[:, t0:t0 + w]))
                        bank_guard[sl] = v2
                        V.wait_ge(VE.sem, v2)
                        vsw = VE.inc(V.stream_shuffle(out=xsq1[:, 0:w], in_=bw[:, 0:w], mask=SWAPM))
                        V.wait_ge(VE.sem, vsw)
                        if gp_guard.get((dc + bi) % 2) is not None:
                            V.wait_ge(GP.sem, gp_guard[(dc + bi) % 2])
                        bw2 = bstg2 if (dc + bi) % 2 == 0 else bstg1
                        v3 = VE.inc(V.tensor_add(out=bw2[:, 0:w], in0=stg1[:, 0:w], in1=xsq1[:, 0:w]))
                        swp_guard[0] = v3
                        G.wait_ge(VE.sem, v3)
                        gp_guard[(dc + bi) % 2] = GP.inc(
                            G.dma_start(out=dst[:, dc * dst_stride + (t0 - (own_lo if dst is d_qd else 0)):
                                                 dc * dst_stride + (t0 - (own_lo if dst is d_qd else 0)) + w],
                                        in_=bw2[:, 0:w]), 16)

        qk_proj(wq_bf, corr_q, d_qd, TOmax, [regions[-1]])
        qk_proj(wk_bf, corr_k, d_kd, T, regions)

        TN.wait_ge(VE.sem, VE.v)
        TN.wait_ge(SC.sem, SC.v)
        S.wait_ge(GP.sem, GP.v)
        S.wait_ge(VE.sem, VE.v)
        vprev = {}
        for tc in range(KT):
            b0 = pb[(tc % 4) * 2]
            b1_ = pb[(tc % 4) * 2 + 1]
            tglob = tc * 128
            for cc in range(8):
                TN.ldweights(xslice(tglob, 128, cc))
                if cc == 0 and vprev.get(tc - 4) is not None:
                    TN.wait_ge(SC.sem, vprev[tc - 4])
                TN.matmul(b0[:, 0:512], xslice(tglob, 128, cc), wv_bf[:, cc * C: cc * C + 512],
                          start=(cc == 0), stop=False)
                TN.matmul(b1_[:, 0:512], xslice(tglob, 128, cc), wv_bf[:, cc * C + 512:(cc + 1) * C],
                          start=(cc == 0), stop=False)
            TN.ldweights(mstd2[:, tglob:tglob + 128])
            TN.matmul(b0[:, 0:512], mstd2[:, tglob:tglob + 128], corr_v[:, 0:512], start=False, stop=True)
            tv2 = TE.inc(TN.matmul(b1_[:, 0:512], mstd2[:, tglob:tglob + 128], corr_v[:, 512:C],
                                   start=False, stop=True))
            S.wait_ge(TE.sem, tv2)
            if vprev.get(tc - 1) is not None:
                S.wait_ge(GP.sem, vstore[tc - 1])
            S.activation(out=bstg1[:, 0:512], in_=b0[:, 0:512], func=AF.Copy, scale=r_col[:, tc:tc + 1])
            svv = SC.inc(S.activation(out=bstg2[:, 0:512], in_=b1_[:, 0:512], func=AF.Copy,
                                      scale=r_col[:, tc:tc + 1]))
            vprev[tc] = svv
            G.wait_ge(SC.sem, svv)
            for p in range(4):
                GP.inc(G.dma_start(out=d_vd[:, p * T + tglob: p * T + tglob + 128],
                                   in_=bstg1[:, p * 128:(p + 1) * 128]), 16)
            for p in range(3):
                GP.inc(G.dma_start(out=d_vd[:, (4 + p) * T + tglob: (4 + p) * T + tglob + 128],
                                   in_=bstg2[:, p * 128:(p + 1) * 128]), 16)
            gv = GP.inc(G.dma_start(out=d_vd[:, 7 * T + tglob: 7 * T + tglob + 128],
                                    in_=bstg2[:, 3 * 128:4 * 128]), 16)
            if tc == 0:
                vstore = {}
            vstore[tc] = gv

        astk.close()
        phase_barrier()
        if dbg:
            dbg_dump([(d_dmod[:, 0:8], caP1[:]), (d_dmod[:, 8:16], cbP1[:]),
                      (d_dmod[:, 16:24], sb_c[:]), (d_dmod[:, 24:32], ga_c[:]),
                      (d_dmod[:, 32:40], gb_c[:]), (d_dsa, sa_cb[:])])
            dbg_dump([(d_dxA[:, cc * T_own:(cc + 1) * T_own],
                       x_own[:, cc * T_own:(cc + 1) * T_own]) for cc in range(8)])

        bstk = ExitStack()
        sbt = lambda n, shape, d: bstk.enter_context(nc.sbuf_tensor(f"{n}_{shalf}", shape, d))
        obuf = sbt("obuf", [128, NPAIR * T_own], BF16)
        wp_bf = sbt("wp_bf", [128, 8 * C], BF16)
        qstg = [sbt(f"qstg{i}", [128, 2048], F32) for i in range(2)]
        f8w1 = [sbt(f"f8w1_{i}", [128, 2048], FP8) for i in range(2)]
        f8w2 = [sbt(f"f8w2_{i}", [128, 2048], FP8) for i in range(2)]
        attn_stk = ExitStack()
        sbn = lambda n, shape, d: attn_stk.enter_context(nc.sbuf_tensor(f"{n}_{shalf}", shape, d))
        kbuf = [sbn(f"kbuf{i}", [128, kv_hi], BF16) for i in range(2)]
        vbuf = [sbn(f"vbuf{i}", [128, kv_hi], BF16) for i in range(2)]
        qbuf = [sbn(f"qbuf{i}", [128, T_own], BF16) for i in range(2)]
        pbuf = [sbn(f"pbuf{i}", [128, 1024], BF16) for i in range(2)]
        ksq = sbn("ksq", [128, kv_hi], BF16)
        recipb = sbn("recipb", [128, 512], F32)
        recipb2 = sbn("recipb2", [128, 512], F32)
        Spair = [big0, big1]
        ObA = p4[:]
        ObB = p5[:]
        SbA = p6[:]
        SbB = p7[:]
        Baux = p6[:]

        for cc in range(8):
            GP.inc(G.dma_start(out=wp_bf[:, cc * C:(cc + 1) * C],
                               in_=d_Wp[cc * 128:(cc + 1) * 128, :]), 16)
        gfence()
        SY.wait_ge(GP.sem, GP.v)
        ldp = {}
        LD.inc(SY.dma_start(out=qbuf[0][:], in_=d_qd[:, 0:T_own]), 16)
        LD.inc(SY.dma_start(out=kbuf[0][:], in_=d_kd[:, 0:kv_hi]), 16)
        LD.inc(SY.dma_start(out=vbuf[0][:], in_=d_vd[:, 0:kv_hi]), 16)
        ldp[0] = syfence()

        te_pair_end = {}
        oblocks = _blocks(own_lo, own_hi)
        prev_tav = None
        prev_exp_last = None
        prev_evac = None

        qent = VE.v
        q1_ve, q1_sd, q1_st = {}, {}, {}
        q2_ve, q2_gp, q2_st = {}, {}, {}

        def emit_w1_quant(i):
            cc, hh = i // 2, i % 2
            stgw = stgA if i % 2 == 0 else stgB
            f8b = f8w1[i % 2]
            G.wait_ge(VE.sem, q1_ve[i - 2] if i >= 2 else qent)
            GP.inc(G.dma_start(
                out=stgw[:, 0:2048],
                in_=d_W1T[cc * 128:(cc + 1) * 128, hh * 2048:(hh + 1) * 2048]), 16)
            q1_sd[i] = gfence()
            V.wait_ge(GP.sem, q1_sd[i])
            if i >= 2:
                V.wait_ge(LD.sem, q1_st[i - 2])
            q1_ve[i] = vcustom(OPS["ANT_FQ8"], out=f8b[:, 0:2048], in0=stgw[:, 0:2048])
            SY.wait_ge(VE.sem, q1_ve[i])
            LD.inc(SY.dma_start(
                out=d_w1q8[cc * 128:(cc + 1) * 128, hh * 2048:(hh + 1) * 2048],
                in_=f8b[:, 0:2048]), 16)
            q1_st[i] = syfence()

        def emit_w2_quant(j):
            stgw = qstg[j % 2]
            f8b = f8w2[j % 2]
            G.wait_ge(VE.sem, q2_ve[j - 2] if j >= 2 else qent)
            for k in range(2):
                ncx = j * 2 + k
                GP.inc(G.dma_start(out=stgw[:, k * 1024:(k + 1) * 1024],
                                   in_=d_W2T[ncx * 128:(ncx + 1) * 128, :]), 16)
            q2_gp[j] = gfence()
            V.wait_ge(GP.sem, q2_gp[j])
            if j >= 2:
                V.wait_ge(LD.sem, q2_st[j - 2])
            q2_ve[j] = vcustom(OPS["ANT_FQ8"], out=f8b[:, 0:2048], in0=stgw[:, 0:2048])
            SY.wait_ge(VE.sem, q2_ve[j])
            for k in range(2):
                ncx = j * 2 + k
                LD.inc(SY.dma_start(out=d_w2q8[ncx * 128:(ncx + 1) * 128, :],
                                    in_=f8b[:, k * 1024:(k + 1) * 1024]), 16)
            q2_st[j] = syfence()

        for p in range(NPAIR):
            pk = p % 2
            if p + 1 < NPAIR:
                if p >= 1:
                    SY.wait_ge(TE.sem, te_pair_end[p - 1])
                    SY.wait_ge(SC.sem, sc_pair_end[p - 1])
                LD.inc(SY.dma_start(out=qbuf[1 - pk][:], in_=d_qd[:, (p + 1) * TOmax:(p + 1) * TOmax + T_own]), 16)
                LD.inc(SY.dma_start(out=kbuf[1 - pk][:], in_=d_kd[:, (p + 1) * T:(p + 1) * T + kv_hi]), 16)
                LD.inc(SY.dma_start(out=vbuf[1 - pk][:], in_=d_vd[:, (p + 1) * T:(p + 1) * T + kv_hi]), 16)
                ldp[p + 1] = syfence()
            S.wait_ge(LD.sem, ldp[p])
            S.wait_ge(TE.sem, TE.v)
            sksq = SC.inc(S.activation(out=ksq[:], in_=kbuf[pk][:], func=AF.Square))
            if prev_exp_last is not None:
                TN.wait_ge(SC.sem, prev_exp_last)
            if prev_evac is not None:
                TN.wait_ge(VE.sem, prev_evac)

            def warm(n):
                TN.ldweights(permsw[:])
                for _ in range(n):
                    TN.matmul(Spair[0][:, 0:512], permsw[:], qmask[:, 0:512],
                              start=True, stop=True)
            warm(6)
            TN.wait_ge(SC.sem, sksq)
            TN.wait_ge(VE.sem, VE.v)
            first = True
            vkm = None
            for (t0, w) in _blocks(0, kv_hi):
                TN.ldweights(ones1c[:])
                if vkm is not None:
                    TN.wait_ge(VE.sem, vkm)
                tkn = TE.inc(TN.matmul(Baux[0:1, 0:w], ones1c[:], ksq[:, t0:t0 + w], start=True, stop=True))
                V.wait_ge(TE.sem, tkn)
                if first:
                    vkm = VE.inc(V.reduce_max(out=kmax2[0:1, 0:1], in_=Baux[0:1, 0:w], axis=AX.X))
                    first = False
                else:
                    vk0 = VE.inc(V.reduce_max(out=kmax2[0:1, 1:2], in_=Baux[0:1, 0:w], axis=AX.X))
                    V.wait_ge(VE.sem, vk0)
                    vkm = VE.inc(V.tensor_max(out=kmax2[0:1, 0:1], in0=kmax2[0:1, 0:1], in1=kmax2[0:1, 1:2]))
            nob_p = len(oblocks)
            vqs = None
            tq2 = None
            knorm_te = TE.v
            for qi, (q0, qw) in enumerate(oblocks):
                qrel = q0 - own_lo
                S.wait_ge(TE.sem, tq2 if tq2 is not None else knorm_te)
                S.wait_ge(SC.sem, SC.v)
                sq2 = SC.inc(S.activation(out=ksq[:, 0:qw], in_=qbuf[pk][:, qrel:qrel + qw],
                                          func=AF.Square))
                TN.wait_ge(SC.sem, sq2)
                TN.wait_ge(VE.sem, vqs if vqs is not None else vkm)
                TN.ldweights(ones1c[:])
                tq2 = TE.inc(TN.matmul(Baux[0:1, 0:qw], ones1c[:], ksq[:, 0:qw], start=True, stop=True))
                V.wait_ge(TE.sem, tq2)
                if qi == 0:
                    V.wait_ge(VE.sem, vkm)
                vqs = VE.inc(V.reduce_max(out=kmax2[0:1, 2 + qi:3 + qi], in_=Baux[0:1, 0:qw], axis=AX.X))
            V.wait_ge(VE.sem, vqs)
            vq = VE.inc(V.tensor_scalar(out=kmax2[0:1, 8:8 + nob_p], in0=kmax2[0:1, 2:2 + nob_p],
                                        scalar1=kmax2[0:1, 0:1], scalar2=None, op0=ALU.mult))
            S.wait_ge(VE.sem, vq)
            sb0_ = SC.inc(S.activation(out=kmax2[0:1, 16:16 + nob_p], in_=kmax2[0:1, 8:8 + nob_p],
                                       func=AF.Ln))
            S.wait_ge(SC.sem, sb0_)
            ssb = SC.inc(S.activation(out=kmax2[0:1, 24:24 + nob_p], in_=kmax2[0:1, 16:16 + nob_p],
                                      func=AF.Exp, scale=0.5, bias=LNQ))
            V.wait_ge(SC.sem, ssb)
            vnb = VE.inc(V.tensor_scalar(out=scr_b[0:1, 0:nob_p], in0=kmax2[0:1, 24:24 + nob_p],
                                         scalar1=-1.0, scalar2=None, op0=ALU.mult))
            warm(5)
            TN.wait_ge(VE.sem, vnb)
            TN.ldweights(oner_lhs[:])
            tnb = TE.inc(TN.matmul(Baux[:, 0:nob_p], oner_lhs[:], scr_b[0:1, 0:nob_p],
                                   start=True, stop=True))
            S.wait_ge(TE.sem, tnb)
            snb = SC.inc(S.copy(out=negBall[:, 0:nob_p], in_=Baux[:, 0:nob_p]))
            TN.wait_ge(SC.sem, snb)
            emit_w1_quant(2 * p)
            emit_w1_quant(2 * p + 1)
            emit_w2_quant(2 * p)
            emit_w2_quant(2 * p + 1)
            for qi, (q0, qw) in enumerate(oblocks):
                qrel = q0 - own_lo
                KTQ = (q0 + qw + 127) // 128
                g0 = q0 // 128
                nbias = negBall[:, qi:qi + 1]
                if prev_evac is not None:
                    TN.wait_ge(VE.sem, prev_evac)
                if prev_exp_last is not None:
                    TN.wait_ge(SC.sem, prev_exp_last)
                exp_done = {}
                av_done = {}

                def emit_av(kt):
                    kk = kt % 2
                    TN.wait_ge(SC.sem, exp_done[kt])
                    TN.ldweights(vbuf[pk][:, kt * 128:kt * 128 + 64], tile_position=(0, 0))
                    TN.matmul(ObA[0:64, 0:qw], vbuf[pk][:, kt * 128:kt * 128 + 64], pbuf[kk][:, 0:qw],
                              start=(kt == 0), stop=(kt == KTQ - 1), tile_position=(0, 0))
                    TN.ldweights(vbuf[pk][:, kt * 128 + 64:(kt + 1) * 128], tile_position=(0, 64))
                    TN.matmul(ObB[64:128, 0:qw], vbuf[pk][:, kt * 128 + 64:(kt + 1) * 128],
                              pbuf[kk][:, 512:512 + qw], start=(kt == 0), stop=(kt == KTQ - 1),
                              tile_position=(0, 64))
                    TN.ldweights(ones128[:, 0:64], tile_position=(0, 0))
                    TN.matmul(SbA[0:64, 0:qw], ones128[:, 0:64], pbuf[kk][:, 0:qw],
                              start=(kt == 0), stop=(kt == KTQ - 1), tile_position=(0, 0))
                    TN.ldweights(ones128[:, 64:128], tile_position=(0, 64))
                    tav = TE.inc(TN.matmul(SbB[64:128, 0:qw], ones128[:, 64:128],
                                           pbuf[kk][:, 512:512 + qw],
                                           start=(kt == 0), stop=(kt == KTQ - 1),
                                           tile_position=(0, 64)))
                    av_done[kt] = tav
                    return tav

                for kt in range(KTQ):
                    kk = kt % 2
                    j = kt - g0
                    masked = j >= 0
                    if exp_done.get(kt - 2) is not None:
                        TN.wait_ge(SC.sem, exp_done[kt - 2])
                    TN.ldweights(kbuf[pk][0:64, kt * 128:(kt + 1) * 128], tile_position=(0, 0))
                    TN.matmul(Spair[kk][:, 0:qw], kbuf[pk][0:64, kt * 128:(kt + 1) * 128],
                              qbuf[pk][0:64, qrel:qrel + qw], start=True, stop=not masked,
                              tile_position=(0, 0))
                    TN.ldweights(kbuf[pk][64:128, kt * 128:(kt + 1) * 128], tile_position=(64, 0))
                    tqk0 = TN.matmul(Spair[kk][:, 512:512 + qw], kbuf[pk][64:128, kt * 128:(kt + 1) * 128],
                                     qbuf[pk][64:128, qrel:qrel + qw], start=True, stop=not masked,
                                     tile_position=(64, 0))
                    if masked:
                        jw = min(128, qw - j * 128)
                        last = (j == 0)
                        TN.ldweights(trib[:], tile_position=(0, 0))
                        TN.matmul(Spair[kk][:, j * 128: j * 128 + jw], trib[:], eyeb[:, 0:jw],
                                  start=False, stop=last)
                        tqk0 = TN.matmul(Spair[kk][:, 512 + j * 128: 512 + j * 128 + jw],
                                         trib[:], eyeb[:, 0:jw], start=False, stop=last)
                        if j > 0:
                            TN.ldweights(negrow[:])
                            TN.matmul(Spair[kk][:, 0:j * 128], negrow[:], onesr[0:1, 0:j * 128],
                                      start=False, stop=True)
                            tqk0 = TN.matmul(Spair[kk][:, 512:512 + j * 128], negrow[:],
                                             onesr[0:1, 0:j * 128], start=False, stop=True)
                    tqk = TE.inc(tqk0)
                    S.wait_ge(TE.sem, tqk)
                    if av_done.get(kt - 2) is not None:
                        S.wait_ge(TE.sem, av_done[kt - 2])
                    elif kt < 2 and prev_tav is not None:
                        S.wait_ge(TE.sem, prev_tav)
                    if qw == 512:
                        se = SC.inc(S.activation(out=pbuf[kk][:], in_=Spair[kk][:], func=AF.Exp,
                                                 scale=0.125, bias=nbias))
                    else:
                        S.activation(out=pbuf[kk][:, 0:qw], in_=Spair[kk][:, 0:qw], func=AF.Exp,
                                     scale=0.125, bias=nbias)
                        se = SC.inc(S.activation(out=pbuf[kk][:, 512:512 + qw], in_=Spair[kk][:, 512:512 + qw],
                                                 func=AF.Exp, scale=0.125, bias=nbias))
                    exp_done[kt] = se
                    if kt >= 1:
                        emit_av(kt - 1)
                tav = emit_av(KTQ - 1)
                prev_tav = tav
                prev_exp_last = exp_done[KTQ - 1]
                V.wait_ge(TE.sem, tav)
                VE.inc(V.reciprocal_approx_fast(out=recipb[0:64, 0:qw], in_=SbA[0:64, 0:qw]))
                vrc = VE.inc(V.reciprocal_approx_fast(out=recipb2[:, 0:qw], in_=SbB[:, 0:qw]))
                V.wait_ge(VE.sem, vrc)
                VE.inc(V.tensor_mul(out=obuf[0:64, p * T_own + qrel: p * T_own + qrel + qw],
                                    in0=ObA[0:64, 0:qw], in1=recipb[0:64, 0:qw]))
                prev_evac = VE.inc(V.tensor_mul(out=obuf[64:128, p * T_own + qrel: p * T_own + qrel + qw],
                                                in0=ObB[64:128, 0:qw], in1=recipb2[64:128, 0:qw]))
            te_pair_end[p] = TE.v
            if p == 0:
                sc_pair_end = {}
            sc_pair_end[p] = SC.v

        attn_stk.close()
        phase_barrier()
        if dbg:
            dbg_dump([(d_dxB[:, cc * T_own:(cc + 1) * T_own],
                       x_own[:, cc * T_own:(cc + 1) * T_own]) for cc in range(8)])
            dbg_dump([(d_dob[:, pp * T_own:(pp + 1) * T_own],
                       obuf[:, pp * T_own:(pp + 1) * T_own]) for pp in range(NPAIR)])

        TN.wait_ge(VE.sem, VE.v)
        TN.wait_ge(SC.sem, SC.v)
        nob = len(oblocks)
        cx_ve = {}
        for cc in range(8):
            for pidx in range(NPAIR):
                TN.ldweights(wp_bf[:, pidx * C + cc * 128: pidx * C + (cc + 1) * 128])
                for bi, (q0, qw) in enumerate(oblocks):
                    qrel = q0 - own_lo
                    if pidx == 0 and cc > 0:
                        TN.wait_ge(VE.sem, cx_ve[(cc - 1, bi)])
                    TN.matmul(pb[bi][:, 0:qw], wp_bf[:, pidx * C + cc * 128: pidx * C + (cc + 1) * 128],
                              obuf[:, pidx * T_own + qrel: pidx * T_own + qrel + qw],
                              start=(pidx == 0), stop=False)
            TN.wait_ge(VE.sem, ve_bp)
            TN.ldweights(bprow[0:1, cc * 128:(cc + 1) * 128])
            for bi, (q0, qw) in enumerate(oblocks):
                qrel = q0 - own_lo
                tpj = TE.inc(TN.matmul(pb[bi][:, 0:qw], bprow[0:1, cc * 128:(cc + 1) * 128],
                                       onesr[0:1, 0:qw], start=False, stop=True))
                V.wait_ge(TE.sem, tpj)
                cx_ve[(cc, bi)] = VE.inc(V.scalar_tensor_tensor(
                    out=x_own[:, cc * T_own + qrel: cc * T_own + qrel + qw],
                    in0=pb[bi][:, 0:qw], scalar=ga_c[:, cc:cc + 1],
                    in1=x_own[:, cc * T_own + qrel: cc * T_own + qrel + qw],
                    op0=ALU.mult, op1=ALU.add))

        bstk.close()
        phase_barrier()
        if dbg:
            dbg_dump([(d_dx2[:, cc * T_own:(cc + 1) * T_own],
                       x_own[:, cc * T_own:(cc + 1) * T_own]) for cc in range(8)])

        dstk = ExitStack()
        sbd = lambda n, shape, d: dstk.enter_context(nc.sbuf_tensor(f"{n}_{shalf}", shape, d))
        w1q = sbd("w1q", [128, 8 * 4096], FP8)
        w2q = sbd("w2q", [128, 32 * C], FP8)
        fqh2 = sbd("fqh2", [128, 8 * T_own], FP8)
        GW = 1216 if shalf == 0 else 1664
        fqh1 = sbd("fqh1", [128, 32 * GW], FP8)

        dv0 = syfence()
        S.wait_ge(LD.sem, dv0)
        for cc in range(8):
            if cc % 2 == 0:
                LD.inc(SY.dma_start(out=w1q[:, cc * 4096:(cc + 1) * 4096],
                                    in_=d_w1q8[cc * 128:(cc + 1) * 128, :]), 16)
            else:
                SD.inc(S.dma_start(out=w1q[:, cc * 4096:(cc + 1) * 4096],
                                   in_=d_w1q8[cc * 128:(cc + 1) * 128, :]), 16)
        ldw1 = syfence()
        sdw1 = SD.v
        G.wait_ge(LD.sem, dv0)
        for ncx in range(32):
            GP.inc(G.dma_start(out=w2q[:, ncx * 1024:(ncx + 1) * 1024],
                               in_=d_w2q8[ncx * 128:(ncx + 1) * 128, :]), 16)
            if ncx % 8 == 7:
                gfence()
        gw2 = gfence()
        TN.wait_ge(LD.sem, ldw1)
        TN.wait_ge(GP.sem, gw2)
        TN.wait_ge(SD.sem, sdw1)
        bguard = {}

        prev_d1_te = None
        prev_d1_ve = None
        fq_ve = {}
        for bi, (q0, qw) in enumerate(oblocks):
            qrel = q0 - own_lo
            for cc in range(8):
                xsq = xsq1 if cc % 2 == 0 else xsq2
                S.wait_ge(VE.sem, cx_ve[(cc, bi)])
                if cc >= 2:
                    S.wait_ge(TE.sem, te_var2[cc - 2])
                elif prev_d1_te is not None:
                    S.wait_ge(TE.sem, prev_d1_te)
                ssq = SC.inc(S.activation(out=xsq[:, 0:qw],
                                          in_=x_own[:, cc * T_own + qrel: cc * T_own + qrel + qw],
                                          func=AF.Square))
                TN.wait_ge(SC.sem, ssq)
                if cc == 0 and prev_d1_ve is not None:
                    TN.wait_ge(VE.sem, prev_d1_ve)
                TN.ldweights(ones1c[:])
                TN.matmul(pb[6][0:1, 0:qw], ones1c[:],
                          x_own[:, cc * T_own + qrel: cc * T_own + qrel + qw],
                          start=(cc == 0), stop=(cc == 7))
                tvq = TE.inc(TN.matmul(pb[7][0:1, 0:qw], ones1c[:], xsq[:, 0:qw],
                                       start=(cc == 0), stop=(cc == 7)))
                if cc == 0:
                    te_var2 = {}
                te_var2[cc] = tvq
            S.wait_ge(TE.sem, tvq)
            if prev_d1_te is not None:
                S.wait_ge(VE.sem, prev_d1_ve)
            S.activation(out=mrow[0:1, 0:qw], in_=pb[6][0:1, 0:qw], func=AF.Copy, scale=1.0 / 1024.0)
            scm2 = SC.inc(S.activation(out=srow[0:1, 0:qw], in_=pb[7][0:1, 0:qw],
                                       func=AF.Copy, scale=1.0 / 1024.0))
            V.wait_ge(SC.sem, scm2)
            vm2 = VE.inc(V.tensor_mul(out=stg1[0:1, 0:qw], in0=mrow[0:1, 0:qw], in1=mrow[0:1, 0:qw]))
            V.wait_ge(VE.sem, vm2)
            vvv = VE.inc(V.tensor_sub(out=stg1[0:1, 0:qw], in0=srow[0:1, 0:qw], in1=stg1[0:1, 0:qw]))
            S.wait_ge(VE.sem, vvv)
            sl2 = SC.inc(S.activation(out=lrow[0:1, 0:qw], in_=stg1[0:1, 0:qw], func=AF.Ln, bias=LN_EPS))
            S.wait_ge(SC.sem, sl2)
            scr2b = SC.inc(S.activation(out=rrow[0:1, 0:qw], in_=lrow[0:1, 0:qw], func=AF.Exp, scale=-0.5))
            V.wait_ge(SC.sem, scr2b)
            V.tensor_copy(out=rbf[0:1, 0:qw], in_=rrow[0:1, 0:qw])
            vms = VE.inc(V.tensor_mul(out=stg2[0:1, 0:qw], in0=mrow[0:1, 0:qw], in1=rrow[0:1, 0:qw]))
            V.wait_ge(VE.sem, vms)
            vmr = VE.inc(V.tensor_copy(out=mr2b[0:1, 0:qw], in_=stg2[0:1, 0:qw]))
            TN.wait_ge(VE.sem, vmr)
            TN.ldweights(oner_lhs[:])
            TN.matmul(pb[6][:, 0:qw], oner_lhs[:], rbf[0:1, 0:qw], start=True, stop=True)
            trp = TE.inc(TN.matmul(pb[7][:, 0:qw], oner_lhs[:], mr2b[0:1, 0:qw],
                                   start=True, stop=True))
            prev_d1_te = trp
            for cc in range(8):
                V.wait_ge(TE.sem, trp)
                vd2 = VE.inc(V.tensor_mul(out=stg1[:, 0:qw],
                             in0=x_own[:, cc * T_own + qrel: cc * T_own + qrel + qw], in1=pb[6][:, 0:qw]))
                V.wait_ge(VE.sem, vd2)
                fq_ve[(bi, cc)] = vcustom(
                    OPS["ANT_FQ8MOD2"], out=fqh2[:, cc * T_own + qrel: cc * T_own + qrel + qw],
                    in0=stg1[:, 0:qw], in1=pb[7][:, 0:qw],
                    s0=cbP1[:, cc:cc + 1], s1=sb_c[:, cc:cc + 1])
            prev_d1_ve = fq_ve[(bi, 7)]

        te_d1_end = TE.v
        for (g0t, gw) in _blocks(own_lo, own_hi, GW):
            grel = g0t - own_lo
            subs = _blocks(0, gw)
            ns = len(subs)
            TN.wait_ge(VE.sem, VE.v)
            TN.wait_ge(SC.sem, SC.v)
            DR = mybir.MatmulPerfMode.DoubleRow
            w1q3 = w1q.rearrange("p (c n) -> p c n", c=8)
            fqh23 = fqh2.rearrange("p (c t) -> p c t", c=8)
            h1_ve = {}
            for ncx in range(32):
                for j in range(4):
                    TN.ldweights(w1q3[:, 2 * j:2 * j + 2, ncx * 128:(ncx + 1) * 128], perf_mode=DR)
                    for sbi, (sb0, sw) in enumerate(subs):
                        if j == 0 and h1_ve.get((ncx - 2, sbi)) is not None:
                            TN.wait_ge(VE.sem, h1_ve[(ncx - 2, sbi)])
                        ti = TN.matmul(pb[(ncx % 2) * 4 + sbi][:, 0:sw],
                                       w1q3[:, 2 * j:2 * j + 2, ncx * 128:(ncx + 1) * 128],
                                       fqh23[:, 2 * j:2 * j + 2, grel + sb0: grel + sb0 + sw],
                                       start=(j == 0), stop=(j == 3), perf_mode=DR)
                tm1 = TE.inc(ti)
                for sbi, (sb0, sw) in enumerate(subs):
                    bst_ = bstg1 if sbi % 2 == 0 else bstg2
                    S.wait_ge(TE.sem, tm1)
                    if bguard.get(sbi % 2) is not None:
                        S.wait_ge(VE.sem, bguard[sbi % 2])
                    srl = SC.inc(S.activation(out=bst_[:, 0:sw], in_=pb[(ncx % 2) * 4 + sbi][:, 0:sw],
                                              func=AF.Relu, bias=fqb1c[:, ncx:ncx + 1]))
                    V.wait_ge(SC.sem, srl)
                    h1_ve[(ncx, sbi)] = vcustom(
                        OPS["ANT_FQ8"], out=fqh1[:, ncx * GW + sb0: ncx * GW + sb0 + sw],
                        in0=bst_[:, 0:sw])
                    bguard[sbi % 2] = h1_ve[(ncx, sbi)]
            y2_ve = {}
            st_guard = {}
            mk_guard = {}
            w2q3 = w2q.rearrange("p (x c) -> p x c", x=32)
            fqh13 = fqh1.rearrange("p (x g) -> p x g", x=32)
            if ns < 4:
                TN.ldweights(permsw[:])
                for _ in range(8):
                    TN.matmul(pb[7][:, 0:512], permsw[:], qmask[:, 0:512],
                              start=True, stop=True)
            for cc in range(8):
                for j in range(16):
                    TN.ldweights(w2q3[:, 2 * j:2 * j + 2, cc * 128:(cc + 1) * 128], perf_mode=DR)
                    for sbi, (sb0, sw) in enumerate(subs):
                        if j == 0:
                            if cc == 0:
                                TN.wait_ge(VE.sem, h1_ve[(31, sbi)])
                            elif y2_ve.get(cc - 1) is not None:
                                TN.wait_ge(VE.sem, y2_ve[cc - 1])
                        ti2 = TN.matmul(pb[sbi][:, 0:sw],
                                        w2q3[:, 2 * j:2 * j + 2, cc * 128:(cc + 1) * 128],
                                        fqh13[:, 2 * j:2 * j + 2, sb0: sb0 + sw],
                                        start=(j == 0), stop=(j == 15), perf_mode=DR)
                tm2 = TE.inc(ti2)
                for sbi, (sb0, sw) in enumerate(subs):
                    ostg = stg1 if sbi % 2 == 0 else stg2
                    msk = xsq1 if sbi % 2 == 0 else xsq2
                    SY.wait_ge(TE.sem, te_d1_end)
                    if mk_guard.get(sbi % 2) is not None:
                        SY.wait_ge(VE.sem, mk_guard[sbi % 2])
                    LD.inc(SY.dma_start(out=msk[:, 0:sw],
                                        in_=d_maskz[cc * 128:(cc + 1) * 128,
                                                    grel + sb0:grel + sb0 + sw]), 16)
                    dmk = syfence()
                    bst_ = bstg1 if sbi % 2 == 0 else bstg2
                    V.wait_ge(TE.sem, tm2)
                    V.wait_ge(LD.sem, dmk)
                    if ST.v > 0:
                        V.wait_ge(ST.sem, ST.v)
                    if bguard.get(sbi % 2) is not None:
                        V.wait_ge(VE.sem, bguard[sbi % 2])
                    vcustom(OPS["ANT_RM"], out=bst_[:, 0:sw], in0=pb[sbi][:, 0:sw],
                            s0=fqb2c[:, cc:cc + 1])
                    vmm = VE.inc(V.tensor_mul(out=msk[:, 0:sw], in0=bst_[:, 0:sw], in1=msk[:, 0:sw]))
                    bguard[sbi % 2] = vmm
                    V.wait_ge(VE.sem, vmm)
                    yv = VE.inc(V.scalar_tensor_tensor(
                        out=ostg[:, 0:sw], in0=msk[:, 0:sw], scalar=gb_c[:, cc:cc + 1],
                        in1=x_own[:, cc * T_own + grel + sb0: cc * T_own + grel + sb0 + sw],
                        op0=ALU.mult, op1=ALU.add))
                    mk_guard[sbi % 2] = yv
                    SY.wait_ge(VE.sem, yv)
                    st_guard[sbi % 2] = ST.inc(
                        SY.dma_start(out=d_out[cc * 128:(cc + 1) * 128, grel + sb0: grel + sb0 + sw],
                                     in_=ostg[:, 0:sw]), 16)
                y2_ve[cc] = yv

        dstk.close()
        bst.close()
        for eng in ALL_ENG:
            eng.wait_ge(ST.sem, ST.v)

    prev_cast_guard = {}
    idxs = [eng.compute_val(eng.partition_id() // 4) for eng in ALL_ENG]
    for i in nc.Switch(engines=ALL_ENG, index=idxs, n=2):
        for s_, v_ in prologue_vals:
            s_.v = v_
        prev_cast_guard = {}
        emit_body(i)

    top.close()
    mybir.codegen_inst_isa_subclasses(nc)
    return nc



def kernel(L0_attn_x, L0_attn_Wq, L0_attn_bq, L0_attn_Wk, L0_attn_bk,
           L0_attn_Wv, L0_attn_bv, L0_attn_Wp, L0_attn_bp,
           L0_ffn_W1, L0_ffn_b1, L0_ffn_W2, L0_ffn_b2, L0_c, L0_wmod):
    _register_ops()
    f32 = np.float32
    xT = np.ascontiguousarray(np.asarray(L0_attn_x, f32).transpose(0, 2, 1))
    W1T = np.ascontiguousarray(np.asarray(L0_ffn_W1, f32).T)
    W2T = np.ascontiguousarray(np.asarray(L0_ffn_W2, f32).T)
    cosT, sinT = _rope_tables()
    qm = np.ascontiguousarray(_boundary_masks().reshape(4 * 128, 512)).astype(ml_dtypes.bfloat16)
    perm = np.zeros((128, 128), np.float32)
    idx = np.arange(128)
    perm[idx, idx ^ 1] = 1.0
    perm = perm.astype(ml_dtypes.bfloat16)
    keep = _threefry_keep_mask()
    maskzT = np.ascontiguousarray(
        (keep.astype(np.float32) * np.float32(1.25)).astype(ml_dtypes.bfloat16).transpose(0, 2, 1))
    ccols = [np.ascontiguousarray(np.asarray(L0_c[b], f32).reshape(8, 128).T) for b in range(B)]
    b1col = np.ascontiguousarray(np.asarray(L0_ffn_b1, f32).reshape(32, 128).T)
    b2col = np.ascontiguousarray(np.asarray(L0_ffn_b2, f32).reshape(8, 128).T)
    row = lambda v: np.ascontiguousarray(np.asarray(v, f32).reshape(1, -1))

    in_maps = []
    for i in range(8):
        b, s = i % 4, i // 4
        lo, hi = (0, SPLIT) if s == 0 else (SPLIT, T)
        mz = np.zeros((C, SPLIT), ml_dtypes.bfloat16)
        mz[:, 0:hi - lo] = maskzT[b][:, lo:hi]
        in_maps.append({
            "xT": xT[b], "cosT": cosT, "sinT": sinT, "qmask4": qm, "maskz": mz,
            "permswap": perm,
            "eye128": np.eye(128, dtype=np.float32).astype(ml_dtypes.bfloat16),
            "tri128": np.where(np.arange(128)[:, None] < np.arange(128)[None, :],
                               np.float32(NEG), np.float32(0.0)).astype(ml_dtypes.bfloat16),
            "Wq": np.asarray(L0_attn_Wq, f32), "Wk": np.asarray(L0_attn_Wk, f32),
            "Wv": np.asarray(L0_attn_Wv, f32), "Wp": np.asarray(L0_attn_Wp, f32),
            "W1T": W1T, "W2T": W2T, "wmod": np.asarray(L0_wmod, f32),
            "ccol": ccols[b],
            "bq_row": row(L0_attn_bq), "bk_row": row(L0_attn_bk),
            "bv_row": row(L0_attn_bv), "bp_row": row(L0_attn_bp),
            "b1col": b1col, "b2col": b2col,
        })

    nc = build_program()
    res = run_bass_kernel_spmd(nc, in_maps, core_ids=list(range(8)))
    out = np.empty((B, T, C), np.float32)
    for i in range(8):
        b, s = i % 4, i // 4
        lo, hi = (0, SPLIT) if s == 0 else (SPLIT, T)
        out[b, lo:hi, :] = res.results[i]["outT"][:, 0:hi - lo].T
    return out
